# version 19
# baseline (speedup 1.0000x reference)
import math
from contextlib import ExitStack
from types import SimpleNamespace

import ml_dtypes
import numpy as np

import concourse.bass as bass
import concourse.mybir as mybir
from concourse.bass import ds
from concourse.expressions_rust import RuntimeValue
from concourse.bass_core_rust import IndirectOffsetOnAxis
from concourse.bass_utils import run_bass_kernel_spmd

F32 = mybir.dt.float32
BF16 = mybir.dt.bfloat16
AF = mybir.ActivationFunctionType
ALU = mybir.AluOpType
AX = mybir.AxisListType

CLIP = 8.0
LN_EPS = 1e-5
FFN_EPS = 1e-6
NEG_BIG = -1e30
ARS = 512


def make_cfg(S=4096, D=4096, NH=32, B=2, NCORES=8):
    c = SimpleNamespace()
    c.S, c.D, c.NH, c.B, c.NCORES = S, D, NH, B, NCORES
    c.HD = D // NH
    assert c.HD == 128
    c.GPC = NCORES // B
    c.CH = S // (2 * c.GPC)
    c.T = 2 * c.CH
    c.HPC = NH // c.GPC
    c.DC = D // 128
    c.QKVC = 3 * D // 128
    c.E, c.R = 8, 32
    c.TT = min(512, c.T)
    c.NTT = c.T // c.TT
    c.JT = c.TT // 128
    c.NTC = c.T // 128
    c.QB = S // 128
    c.NQG = c.QB // 4
    c.ANCH = c.T // 2
    assert c.DC % 4 == 0 and c.T % 128 == 0 and c.S % 512 == 0
    assert c.QB % 4 == 0
    c.BLK = 3 * c.HPC * c.HD
    c.ABLK = c.HPC * c.HD
    return c


def alibi_slopes_np(n_heads, bias_max=8.0):
    next_pow2 = 2 ** int(np.ceil(np.log2(n_heads)))
    m = np.arange(1, next_pow2 + 1, dtype=np.float32) * (bias_max / next_pow2)
    slopes = 1.0 / np.power(2.0, m)
    if next_pow2 != n_heads:
        slopes = np.concatenate([slopes[1::2], slopes[::2]])[:n_heads]
    return slopes.astype(np.float32)



class Prog:
    QUEUE = {
        'PE': 'tensor', 'DVE': 'vector', 'ACT': 'scalar', 'POOL': 'gpsimd',
        'SP': 'sync', 'SPD': 'sync', 'ACTD': 'scalar', 'PLD': 'gpsimd',
        'CC': 'gpsimd',
    }
    DMA_CH = frozenset(('SPD', 'ACTD', 'PLD'))
    NSEM = 16

    def __init__(self, nc):
        self.nc = nc
        self.sem = {}
        for ch in self.QUEUE:
            if ch in self.DMA_CH:
                self.sem[ch] = [nc.alloc_semaphore(f"sem_{ch}_{i}")
                                for i in range(self.NSEM)]
            else:
                self.sem[ch] = nc.alloc_semaphore(f"sem_{ch}")
        self.cnt = {ch: 0 for ch in self.QUEUE}
        self.lastw = {ch: {} for ch in self.QUEUE}
        self.ops = []
        self.st = {}

    def _wait_spec(self, pch, v):
        if pch in self.DMA_CH:
            idx = v - 1
            slot = idx % self.NSEM
            return (pch, slot), self.sem[pch][slot], 16 * (idx // self.NSEM + 1)
        return (pch, 0), self.sem[pch], v

    def emit(self, ch, fn, reads=(), writes=(), extra_waits=()):
        needs = {}

        def need(pch, v):
            if pch == ch and ch == 'PE':
                return
            key, semh, val = self._wait_spec(pch, v)
            cur = needs.get(key)
            if cur is None or val > cur[1]:
                needs[key] = (semh, val)

        for t in reads:
            s = self.st.get(t)
            if s and s[0]:
                need(*s[0])
        for t in writes:
            s = self.st.get(t)
            if s:
                if s[0]:
                    need(*s[0])
                for (rch, _slot), rv in s[1].items():
                    need(rch, rv)
        for pch, v in extra_waits:
            need(pch, v)
        waits = []
        for key in sorted(needs):
            semh, val = needs[key]
            if self.lastw[ch].get(key, 0) >= val:
                continue
            self.lastw[ch][key] = val
            waits.append((semh, val))
        self.cnt[ch] += 1
        mycnt = self.cnt[ch]
        if ch in self.DMA_CH:
            idx = mycnt - 1
            slot = idx % self.NSEM
            mysem = self.sem[ch][slot]
            amt = 16
            if idx >= self.NSEM:
                key = (ch, slot)
                val = 16 * (idx // self.NSEM)
                if self.lastw[ch].get(key, 0) < val:
                    self.lastw[ch][key] = val
                    waits.append((mysem, val))
        else:
            mysem = self.sem[ch]
            amt = 1
        self.ops.append((ch, waits, fn, mysem, amt))
        for t in writes:
            self.st[t] = [(ch, mycnt), {}]
        rkey = (ch, (mycnt - 1) % self.NSEM if ch in self.DMA_CH else 0)
        for t in reads:
            s = self.st.setdefault(t, [None, {}])
            if s[1].get(rkey, 0) < mycnt:
                s[1][rkey] = mycnt

    def snap(self):
        return [(pch, self.cnt[pch]) for pch in self.QUEUE if self.cnt[pch] > 0]

    def barrier(self, exclude=()):
        specs = []
        for pch in self.QUEUE:
            n = self.cnt[pch]
            if n == 0 or pch in exclude:
                continue
            if pch in self.DMA_CH:
                for s in range(min(n, self.NSEM)):
                    ns = (n + self.NSEM - 1 - s) // self.NSEM
                    specs.append((self.sem[pch][s], 16 * ns, (pch, s)))
            else:
                specs.append((self.sem[pch], n, (pch, 0)))
        for ch in ('PE', 'DVE', 'ACT', 'POOL', 'SP'):
            waits = []
            for semh, val, key in specs:
                if self.lastw[ch].get(key, 0) >= val:
                    continue
                self.lastw[ch][key] = val
                waits.append((semh, val))
            self.cnt[ch] += 1
            self.ops.append((ch, waits, lambda e: e.nop(),
                             self.sem[ch] if ch not in self.DMA_CH else None, 1))

    def build(self):
        by_queue = {}
        for op in self.ops:
            by_queue.setdefault(self.QUEUE[op[0]], []).append(op)

        def run_list(eng, lst):
            for ch, waits, fn, mysem, amt in lst:
                for semh, v in waits:
                    eng.wait_ge(semh, v)
                inst = fn(eng)
                if inst is not None:
                    inst.then_inc(mysem, amt)

        nc = self.nc
        with nc.Block() as block:
            for qname in ('tensor', 'vector', 'scalar', 'gpsimd', 'sync'):
                lst = by_queue.get(qname, [])
                if not lst:
                    continue
                getattr(block, qname)(
                    (lambda lst: (lambda eng: run_list(eng, lst)))(lst))



def build_program(cfg, dbg=()):
    c = cfg
    nc = bass.Bass("TRN2", target_bir_lowering=False)

    x_in = nc.dram_tensor("x", [c.T, c.D], F32, kind="ExternalInput")
    wqkv = nc.dram_tensor("wqkv", [c.QKVC * 128, c.D], F32, kind="ExternalInput")
    wout = nc.dram_tensor("wout", [c.D, c.D], F32, kind="ExternalInput")
    wffn = nc.dram_tensor("wffn", [c.D, c.D], F32, kind="ExternalInput")
    w8 = nc.dram_tensor("w8", [128, c.DC * c.E], F32, kind="ExternalInput")
    amat = nc.dram_tensor("amat", [128, c.DC * c.R], F32, kind="ExternalInput")
    bmat = nc.dram_tensor("bmat", [c.R, c.D], F32, kind="ExternalInput")
    limes = nc.dram_tensor("limes", [c.E, c.R], F32, kind="ExternalInput")
    limest = nc.dram_tensor("limest", [c.R, c.E], F32, kind="ExternalInput")
    qlnw = nc.dram_tensor("qlnw", [c.DC, 128], F32, kind="ExternalInput")
    qlnb = nc.dram_tensor("qlnb", [c.DC, 128], F32, kind="ExternalInput")
    klnw = nc.dram_tensor("klnw", [c.DC, 128], F32, kind="ExternalInput")
    klnb = nc.dram_tensor("klnb", [c.DC, 128], F32, kind="ExternalInput")
    fbias = nc.dram_tensor("fbias", [c.DC, 128], F32, kind="ExternalInput")
    sk3 = nc.dram_tensor("sk3", [3 * c.HPC, c.S], BF16, kind="ExternalInput")
    alcol = nc.dram_tensor("alcol", [128, (c.S // 128) * c.HPC], F32,
                           kind="ExternalInput")
    gidx = nc.dram_tensor("gidx", [128, 6 * c.GPC * c.HPC], mybir.dt.int32,
                          kind="ExternalInput")
    aidx = nc.dram_tensor("aidx", [128, c.NTT * 2 * c.DC], mybir.dt.int32,
                          kind="ExternalInput")
    out_t = nc.dram_tensor("out", [c.T, c.D], F32, kind="ExternalOutput")

    qkvc = nc.dram_tensor("qkvc", [2 * 3 * c.D, c.CH], BF16)
    qkva = nc.dram_tensor("qkva", [2 * c.GPC * 3 * c.D, c.CH], BF16)
    attnc = nc.dram_tensor("attnc", [c.ABLK, c.S], BF16)
    attnm = nc.dram_tensor("attnm", [c.GPC * c.ABLK, c.S], BF16)
    x2sp = nc.dram_tensor("x2sp", [c.T, c.D], F32)
    wbfd = nc.dram_tensor("wbfd", [2 * c.D, c.D], BF16)
    x2fm = nc.dram_tensor("x2fm", [c.D, c.T], BF16)
    scalc = nc.dram_tensor("scalc", [2, 1], F32)
    scala = nc.dram_tensor("scala", [2 * c.NCORES, 1], F32)

    p = Prog(nc)

    def S_(name, shape, dt):
        return nc.alloc_sbuf_tensor(name, shape, dt)

    ident_b = S_("ident_b", [128, 128], BF16)
    ident_f = S_("ident_f", [128, 128], F32)
    ones_f = S_("ones_f", [1, 128], F32)
    pvec = S_("pvec", [32, 128], F32)
    wq_s = S_("wq_s", [128, c.DC], F32)
    bq_s = S_("bq_s", [128, c.DC], F32)
    wk_s = S_("wk_s", [128, c.DC], F32)
    bk_s = S_("bk_s", [128, c.DC], F32)
    bcol = S_("bcol", [128, c.DC], F32)
    stat = S_("stat", [128, 16], F32)
    accs = S_("accs", [128, 8], F32)
    accq = S_("accq", [128, 8], F32)
    grec = S_("grec", [128, 16], F32)
    gsc = S_("gsc", [16, 20], F32)
    gbc = S_("gbc", [128, 2], F32)
    cst = S_("cst", [128, 4], F32)
    gix = S_("gix", [128, 6 * c.GPC * c.HPC], mybir.dt.int32)
    aix = S_("aix", [128, c.NTT * 2 * c.DC], mybir.dt.int32)

    es1 = ExitStack()

    def S1(name, shape, dt):
        return es1.enter_context(nc.sbuf_tensor(name, shape, dt))

    x_fm = S1("x_fm", [128, c.DC * c.T], BF16)
    qk_tm = S1("qk_tm", [128, c.NTC * c.D], BF16)
    sqd1 = S1("sqd", [128, c.D], F32)
    wsl = [S1(f"wsl{i}", [128, c.D], BF16) for i in range(3)]
    xst = [S1("xst0", [128, c.D], F32)]
    xbt = [S1("xbt0", [128, c.D], BF16)]
    est = [S1(f"est{i}", [128, 512], BF16) for i in range(2)]
    fmst = [S1(f"fmst{i}", [128, max(c.T, 512)], BF16) for i in range(2)]

    ps = {}
    for i in range(6):
        ps[f"f{i}"] = nc.alloc_psum_tensor(f"psf{i}", [128, 512], F32)
    for i in range(2):
        ps[f"b{i}"] = nc.alloc_psum_tensor(f"psb{i}", [128, 1024], BF16)

    def tk(*a):
        return a

    E = p.emit

    E('POOL', lambda e: e.memset(ident_b[:, :], 0.0), writes=[tk('ident_b')])
    E('POOL', lambda e: e.affine_select(
        out=ident_b[:, :], in_=ident_b[:, :], compare_op=ALU.not_equal,
        fill=1.0, base=0, pattern=[[-1, 128]], channel_multiplier=1),
      reads=[tk('ident_b')], writes=[tk('ident_b')])
    E('POOL', lambda e: e.memset(ident_f[:, :], 0.0), writes=[tk('ident_f')])
    E('POOL', lambda e: e.affine_select(
        out=ident_f[:, :], in_=ident_f[:, :], compare_op=ALU.not_equal,
        fill=1.0, base=0, pattern=[[-1, 128]], channel_multiplier=1),
      reads=[tk('ident_f')], writes=[tk('ident_f')])
    E('POOL', lambda e: e.memset(ones_f[:, :], 1.0), writes=[tk('ones_f')])
    E('POOL', lambda e: e.memset(cst[:, 0:1], 0.0), writes=[tk('cst')])
    E('POOL', lambda e: e.memset(cst[:, 1:2], LN_EPS), writes=[tk('cst')])
    E('POOL', lambda e: e.memset(cst[:, 2:3], 1e-9), writes=[tk('cst')])

    E('PLD', lambda e: e.dma_start(out=gix[:, :], in_=gidx[:, :]),
      writes=[tk('gix')])
    E('PLD', lambda e: e.dma_start(out=aix[:, :], in_=aidx[:, :]),
      writes=[tk('aix')])

    def load_param(name, dram, dest, scale=None):
        E('SPD', lambda e, d=dram: e.dma_start(out=pvec[0:c.DC, :], in_=d[:, :]),
          writes=[tk('pvec')])
        E('PE', lambda e: e.transpose(ps['f0'][:, 0:c.DC], pvec[0:c.DC, :],
                                      ident_f[0:c.DC, 0:c.DC]),
          reads=[tk('pvec'), tk('ident_f')], writes=[tk('pmm', 0)])
        if scale is None:
            E('DVE', lambda e, d2=dest: e.tensor_copy(d2[:, :],
                                                      ps['f0'][:, 0:c.DC]),
              reads=[tk('pmm', 0)], writes=[tk(name)])
        else:
            E('DVE', lambda e, d2=dest, sc=scale: e.tensor_scalar(
                out=d2[:, :], in0=ps['f0'][:, 0:c.DC], scalar1=sc,
                scalar2=None, op0=ALU.mult),
              reads=[tk('pmm', 0)], writes=[tk(name)])

    att_scale = 1.0 / math.sqrt(c.HD)
    load_param('wq_s', qlnw, wq_s, att_scale)
    load_param('bq_s', qlnb, bq_s, att_scale)
    load_param('wk_s', klnw, wk_s)
    load_param('bk_s', klnb, bk_s)
    load_param('bcol', fbias, bcol)


    xfm3 = x_fm.ap().rearrange("p (do t) -> p do t", do=c.DC)
    NXQ = max(1, c.DC // 8)
    XQW = c.D // NXQ
    for j in range(c.NTC):
        xs, xb = xst[0], xbt[0]
        for q in range(NXQ):
            E('SPD', lambda e, j=j, q=q, xs=xs: e.dma_start(
                out=xs[:, q * XQW:(q + 1) * XQW],
                in_=x_in[j * 128:(j + 1) * 128, q * XQW:(q + 1) * XQW]),
              writes=[tk('xst', q)])
            E('POOL', lambda e, q=q, xs=xs, xb=xb: e.tensor_copy(
                xb[:, q * XQW:(q + 1) * XQW], xs[:, q * XQW:(q + 1) * XQW]),
              reads=[tk('xst', q)], writes=[tk('xbt', q)])
        for dg in range(c.DC // 4):
            tb = ps[f"b{dg % 2}"]
            for dd in range(4):
                do = dg * 4 + dd
                E('PE', lambda e, xb=xb, do=do, dd=dd, tb=tb: e.transpose(
                    tb[:, dd * 128:(dd + 1) * 128],
                    xb[:, do * 128:(do + 1) * 128], ident_b[:, :]),
                  reads=[tk('xbt', do * 128 // XQW), tk('ident_b')],
                  writes=[tk('psb', dg % 2)])
            src3 = tb.ap()[:, 0:512].rearrange("p (g t) -> p g t", g=4)
            dst3 = xfm3[:, dg * 4:dg * 4 + 4, j * 128:(j + 1) * 128]
            E('DVE', lambda e, s=src3, d=dst3: e.tensor_copy(d, s),
              reads=[tk('psb', dg % 2)], writes=[tk('x_fm', dg, j)])

    def vrow0(e_in):
        return 2 * c.D + e_in * 128

    def qkrow0(part, e_in):
        return part * c.D + e_in * 128

    groups = [[b_ * c.GPC + r for r in range(c.GPC)] for b_ in range(c.B)]

    def emit_gather(part):
        for hf in (0, 1):
            toks = [tk('qkvc', part, e2, hf) for e2 in range(c.DC)]
            E('CC', lambda e, hf=hf, part=part: e.collective_compute(
                "AllGather", ALU.bypass, replica_groups=groups,
                ins=[qkvc[hf * 3 * c.D + part * c.D:
                          hf * 3 * c.D + (part + 1) * c.D, :].opt()],
                outs=[qkva[(hf * 3 + part) * c.GPC * c.D:
                           (hf * 3 + part + 1) * c.GPC * c.D, :].opt()]),
              reads=toks, writes=[tk('qkva', part, hf)])

    qktm3 = qk_tm.ap().rearrange("p (j d) -> p j d", j=c.NTC)
    pend1 = [None]
    for ec in range(c.QKVC):
        pidx, e_in = ec // c.DC, ec % c.DC
        part = (2, 0, 1)[pidx]
        wb = wsl[ec % 3]
        E('PLD', lambda e, ec=ec, wb=wb: e.dma_start(
            out=wb[:, :], in_=wqkv[ec * 128:(ec + 1) * 128, :]),
          writes=[tk('wsl', ec % 3)])
        for tt in range(c.NTT):
            pm = ps[f"f{tt % 2}"]
            for do in range(c.DC):
                E('PE', lambda e, wb=wb, do=do, tt=tt, pm=pm: e.matmul(
                    pm[:, 0:c.TT], wb[:, do * 128:(do + 1) * 128],
                    x_fm[:, do * c.T + tt * c.TT: do * c.T + (tt + 1) * c.TT],
                    start=(do == 0), stop=(do == c.DC - 1)),
                  reads=[tk('wsl', ec % 3)] + [tk('x_fm', do // 4, jj)
                         for jj in range(tt * c.JT, (tt + 1) * c.JT)],
                  writes=[tk('pmm', tt % 2)])
            est_b = est[tt % 2]
            if pend1[0] is not None:
                pend1[0]()
                pend1[0] = None
            E('DVE', lambda e, pm=pm, est_b=est_b: e.tensor_scalar(
                out=est_b[:, 0:c.TT], in0=pm[:, 0:c.TT], scalar1=CLIP,
                scalar2=-CLIP, op0=ALU.min, op1=ALU.max),
              reads=[tk('pmm', tt % 2)], writes=[tk('est', tt % 2)])
            if part < 2:
                def _p1t(est_b=est_b, tt=tt, e_in=e_in):
                    tb = ps[f"b{tt % 2}"]
                    for jj in range(c.JT):
                        E('PE', lambda e, est_b=est_b, jj=jj, tb=tb:
                          e.transpose(
                            tb[:, jj * 128:(jj + 1) * 128],
                            est_b[:, jj * 128:(jj + 1) * 128],
                            ident_b[:, :]),
                          reads=[tk('est', tt % 2), tk('ident_b')],
                          writes=[tk('psb', tt % 2)])
                    src3 = tb.ap()[:, 0:c.JT * 128].rearrange(
                        "p (g t) -> p g t", g=c.JT)
                    dst3 = qktm3[:, tt * c.JT:(tt + 1) * c.JT,
                                 e_in * 128:e_in * 128 + 128]
                    E('DVE', lambda e, s=src3, d=dst3: e.tensor_copy(d, s),
                      reads=[tk('psb', tt % 2)],
                      writes=[tk('qktm', e_in, tt)])
                pend1[0] = _p1t
            else:
                r0 = vrow0(e_in)
                lo_l, hi_l = tt * c.TT, (tt + 1) * c.TT
                for hf in (0, 1):
                    aa = max(lo_l, hf * c.CH)
                    bb = min(hi_l, (hf + 1) * c.CH)
                    if aa >= bb:
                        continue
                    E('SPD', lambda e, est_b=est_b, r0=r0, hf=hf, aa=aa,
                      bb=bb, lo_l=lo_l: e.dma_start(
                        out=qkvc[hf * 3 * c.D + r0:hf * 3 * c.D + r0 + 128,
                                 aa - hf * c.CH:bb - hf * c.CH],
                        in_=est_b[:, aa - lo_l:bb - lo_l]),
                      reads=[tk('est', tt % 2)],
                      writes=[tk('qkvc', 2, e_in, hf)])
        if part == 2 and e_in == c.DC - 1:
            emit_gather(2)
        if part < 2 and e_in == c.DC - 1:
            if pend1[0] is not None:
                pend1[0]()
                pend1[0] = None
            for j in range(c.NTC):
                tile = qk_tm[:, j * c.D:(j + 1) * c.D]
                rtoks = [tk('qktm', e2, j // c.JT) for e2 in range(c.DC)]
                E('DVE', lambda e, tile=tile: e.reduce_sum(
                    stat[:, 0:1], tile, axis=AX.X, negate=True),
                  reads=rtoks, writes=[tk('stat', 0)])
                E('ACT', lambda e: e.activation(
                    out=stat[:, 1:2], in_=stat[:, 0:1], func=AF.Copy,
                    scale=1.0 / c.D),
                  reads=[tk('stat', 0)], writes=[tk('stat', 1)])
                E('ACT', lambda e, tile=tile: e.activation(
                    out=sqd1[:, 0:c.D], in_=tile, func=AF.Square,
                    bias=stat[:, 1:2], accum_out=stat[:, 2:3]),
                  reads=rtoks + [tk('stat', 1)],
                  writes=[tk('sqd'), tk('stat', 2)])
                E('ACT', lambda e: e.activation(
                    out=stat[:, 3:4], in_=stat[:, 2:3], func=AF.Ln,
                    scale=1.0 / c.D, bias=cst[:, 1:2]),
                  reads=[tk('stat', 2), tk('cst')], writes=[tk('stat', 3)])
                E('ACT', lambda e: e.activation(
                    out=stat[:, 4:5], in_=stat[:, 3:4], func=AF.Exp,
                    scale=-0.5, bias=cst[:, 0:1]),
                  reads=[tk('stat', 3), tk('cst')], writes=[tk('stat', 4)])
                E('DVE', lambda e, tile=tile: e.tensor_scalar(
                    out=tile, in0=tile, scalar1=stat[:, 1:2],
                    scalar2=stat[:, 4:5], op0=ALU.add, op1=ALU.mult),
                  reads=rtoks + [tk('stat', 1), tk('stat', 4)], writes=rtoks)
            wsv, bsv = (wq_s, bq_s) if part == 0 else (wk_s, bk_s)
            G4 = min(4, c.NTC)
            for ein2 in range(c.DC):
                fst = fmst[ein2 % 2]
                for jg in range(c.NTC // G4):
                    tb = ps[f"b{jg % 2}"]
                    for jj in range(G4):
                        j = jg * G4 + jj
                        E('PE', lambda e, j=j, jj=jj, ein2=ein2, tb=tb:
                          e.transpose(
                              tb[:, jj * 128:(jj + 1) * 128],
                              qk_tm[:, j * c.D + ein2 * 128:
                                    j * c.D + ein2 * 128 + 128],
                              ident_b[:, :]),
                          reads=[tk('qktm', ein2, j // c.JT), tk('ident_b')],
                          writes=[tk('psb', jg % 2)])
                    E('DVE', lambda e, tb=tb, jg=jg, G4=G4, fst=fst, wsv=wsv,
                      bsv=bsv, ein2=ein2: e.tensor_scalar(
                          out=fst[:, jg * G4 * 128:(jg + 1) * G4 * 128],
                          in0=tb[:, 0:G4 * 128],
                          scalar1=wsv[:, ein2:ein2 + 1],
                          scalar2=bsv[:, ein2:ein2 + 1],
                          op0=ALU.mult, op1=ALU.add),
                      reads=[tk('psb', jg % 2), tk(wsv.name), tk(bsv.name)],
                      writes=[tk('fmst', ein2 % 2, jg)])
                r0 = qkrow0(part, ein2)
                for hf in (0, 1):
                    E('SPD', lambda e, fst=fst, r0=r0, hf=hf: e.dma_start(
                        out=qkvc[hf * 3 * c.D + r0:hf * 3 * c.D + r0 + 128,
                                 0:c.CH],
                        in_=fst[:, hf * c.CH:(hf + 1) * c.CH]),
                      reads=[tk('fmst', ein2 % 2, jg2)
                             for jg2 in range(c.NTC // G4)],
                      writes=[tk('qkvc', part, ein2, hf)])
            if part == 0:
                emit_gather(0)

    es1.close()
    p.barrier()
    emit_gather(1)

    es3 = ExitStack()

    def S3(name, shape, dt):
        return es3.enter_context(nc.sbuf_tensor(name, shape, dt))

    qsbs = [S3(f"qsb{k}", [128, c.S], BF16) for k in range(2)]
    ksbs = [S3(f"ksb{k}", [128, c.S], BF16) for k in range(2)]
    vfsbs = [S3(f"vfsb{k}", [128, c.S], BF16) for k in range(2)]
    vtsbs = [S3(f"vtsb{k}", [128, c.S], BF16) for k in range(2)]
    atsb = [S3(f"atsb{i}", [128, c.S], BF16) for i in range(2)]
    skh = [S3(f"skh{i}", [128, c.S], BF16) for i in range(2)]
    prbT = [S3(f"prbt{t}", [128, 512], BF16) for t in range(3)]
    zrec = [S3(f"zrec{i}", [128, 512], F32) for i in range(2)]
    alc = S3("alc", [128, (c.S // 128) * c.HPC], F32)
    ones3 = S3("ones3", [128, 128], BF16)
    msk = S3("msk", [128, 1024], BF16)
    NKB = c.S // 128
    E('POOL', lambda e: e.memset(ones3[:, :], 1.0), writes=[tk('ones3')])
    for _i in range(2):
        E('POOL', lambda e, _i=_i: e.memset(skh[_i][:, :], 0.0),
          writes=[tk('skh', _i)])
    E('POOL', lambda e: e.memset(msk[:, :], 0.0), writes=[tk('msk')])
    E('POOL', lambda e: e.affine_select(
        out=msk[:, :], in_=msk[:, :], compare_op=ALU.is_ge,
        fill=NEG_BIG, base=-384, pattern=[[1, 1024]], channel_multiplier=-1),
      reads=[tk('msk')], writes=[tk('msk')])
    E('PLD', lambda e: e.dma_start(out=alc[:, :], in_=alcol[:, :]),
      writes=[tk('alc')])

    def chunk_owner(g):
        r = g if g < c.GPC else 2 * c.GPC - 1 - g
        lo = 0 if g < c.GPC else c.CH
        slot = 0 if g < c.GPC else 1
        return r, lo, slot

    KVS = 2 if c.NQG % 2 == 0 else 1
    GH = 2 * c.GPC // KVS

    def emit_loads(i, hf):
        hp = i % 2
        qsb, ksb, vfsb = qsbs[hp], ksbs[hp], vfsbs[hp]
        for g in range(0 if hf else 0, (hf + 1) * GH):
            parts = ((0, qsb, ('qsb', hp)), (2, vfsb, ('vfsb', hp)),
                     (1, ksb, ('ksb', hp)))
            if hf == 1 and g < GH:
                parts = ((2, vfsb, ('vfsb', hp)), (1, ksb, ('ksb', hp)))
            for part, dest, dtok in parts:
                kcol = (i * 3 + part) * 2 * c.GPC + g
                E('PLD', lambda e, dest=dest, g=g, kcol=kcol:
                  e.indirect_dma_start(
                      out=dest[:, g * c.CH:(g + 1) * c.CH], out_offset=None,
                      in_=qkva.ap(),
                      in_offset=IndirectOffsetOnAxis(gix[:, kcol:kcol + 1], 0),
                      element_offset=0),
                  reads=[tk('qkva', part, g // c.GPC), tk('gix')],
                  writes=[dtok + (g,)])

    def emit_vt(i, hf):
        hp = i % 2
        vfsb, vtsb = vfsbs[hp], vtsbs[hp]
        vread = [tk('vfsb', hp, g) for g in range(0, (hf + 1) * GH)]
        w0 = 0
        for cg in range((hf + 1) * (c.S // KVS) // 512):
            tb = ps[f"b{cg % 2}"]
            for cc in range(4):
                ch_i = (w0 // 128) + cg * 4 + cc
                E('PE', lambda e, ch_i=ch_i, cc=cc, tb=tb, vfsb=vfsb:
                  e.transpose(
                    tb[:, cc * 128:(cc + 1) * 128],
                    vfsb[:, ch_i * 128:(ch_i + 1) * 128], ident_b[:, :]),
                  reads=vread + [tk('ident_b')], writes=[tk('psb', cg % 2)])
            E('DVE', lambda e, cg=cg, tb=tb, vtsb=vtsb, w0=w0: e.tensor_copy(
                vtsb[:, w0 + cg * 512:w0 + (cg + 1) * 512], tb[:, 0:512]),
              reads=[tk('psb', cg % 2)],
              writes=[tk('vtsb', hp, (w0 // 512) + cg)])

    def emit_dh(i):
        hq = i % 2
        E('PLD', lambda e, i=i, hq=hq: e.dma_start(
            out=skh[hq][0:3, :], in_=sk3[3 * i:3 * i + 3, :]),
          writes=[tk('skh', hq)])

    def emit_attn(i, gi):
        hp = i % 2
        hq = i % 2
        qsb, ksb, vtsb = qsbs[hp], ksbs[hp], vtsbs[hp]
        B0 = gi * 4
        ntl = B0 + 4
        q0 = gi * 512
        qread = [tk('qsb', hp, g)
                 for g in range(q0 // c.CH,
                                min(2 * c.GPC, (q0 + 512 + c.CH - 1) // c.CH))]
        scb = [ps['f0'], ps['f1'], ps['f3']]
        pav = ps['f2'] if gi % 2 == 0 else ps['f5']
        zr = zrec[gi % 2]
        pend = [None]
        for t in range(ntl):
            pm = scb[t % 3]
            d = t - B0
            kread = [tk('ksb', hp, g)
                     for g in range(t * 128 // c.CH,
                                    min(2 * c.GPC,
                                        ((t + 1) * 128 + c.CH - 1) // c.CH))]
            ptok = tk('psc', t % 3)
            if d >= 0:
                off = 384 - 128 * d
                E('PE', lambda e, pm=pm, off=off: e.matmul(
                    pm[:, 0:512], ident_b[:, :], msk[:, off:off + 512],
                    start=True, stop=False),
                  reads=[tk('ident_b'), tk('msk')], writes=[ptok])
            E('PE', lambda e, pm=pm, hq=hq, q0=q0, d=d: e.matmul(
                pm[:, 0:512], ones3[:, :], skh[hq][:, q0:q0 + 512],
                start=(d < 0), stop=False),
              reads=[tk('ones3'), tk('skh', hq)], writes=[ptok])
            E('PE', lambda e, pm=pm, t=t, q0=q0, qsb=qsb, ksb=ksb: e.matmul(
                pm[:, 0:512], ksb[:, t * 128:(t + 1) * 128],
                qsb[:, q0:q0 + 512], start=False, stop=True),
              reads=kread + qread, writes=[ptok])
            if pend[0] is not None:
                pend[0]()
                pend[0] = None
            pb = prbT[t % 3]
            E('ACT', lambda e, pm=pm, pb=pb, i=i, t=t: e.activation(
                out=pb[:, :], in_=pm[:, 0:512], func=AF.Exp,
                bias=alc[:, i * NKB + t:i * NKB + t + 1]),
              reads=[ptok, tk('alc')], writes=[tk('prbt', t % 3)])

            def _zav(t=t, pb=pb):
                fs = (t == 0)
                fe = (t == ntl - 1)
                E('PE', lambda e, pb=pb, fs=fs, fe=fe: e.matmul(
                    ps['f4'][:, 0:512], ones3[:, :], pb[:, :],
                    start=fs, stop=fe),
                  reads=[tk('ones3'), tk('prbt', t % 3)], writes=[tk('pz')])
                E('PE', lambda e, pb=pb, t=t, fs=fs, fe=fe, pav=pav,
                  vtsb=vtsb: e.matmul(
                    pav[:, 0:512], vtsb[:, t * 128:(t + 1) * 128], pb[:, :],
                    start=fs, stop=fe),
                  reads=[tk('vtsb', hp, t // 4), tk('prbt', t % 3)],
                  writes=[tk('pav', gi % 2)])
            pend[0] = _zav
        pend[0]()
        E('DVE', lambda e, zr=zr: e.reciprocal(zr[:, :], ps['f4'][:, 0:512]),
          reads=[tk('pz')], writes=[tk('zrec', gi % 2)])
        E('DVE', lambda e, i=i, q0=q0, pav=pav, zr=zr: e.tensor_tensor(
            atsb[i % 2][:, q0:q0 + 512], pav[:, 0:512], zr[:, :],
            op=ALU.mult),
          reads=[tk('pav', gi % 2), tk('zrec', gi % 2)],
          writes=[tk('atsb', i % 2, gi)])

    def emit_attn_dma(i, hf):
        w0 = hf * (c.S // KVS)
        wid = c.S // KVS
        E('SPD', lambda e, i=i, w0=w0, wid=wid: e.dma_start(
            out=attnc[i * c.HD:(i + 1) * c.HD, w0:w0 + wid],
            in_=atsb[i % 2][:, w0:w0 + wid]),
          reads=[tk('atsb', i % 2, gi2)
                 for gi2 in range(hf * c.NQG // KVS,
                                  (hf + 1) * c.NQG // KVS)],
          writes=[tk('attnc', i, hf)])

    wcast_jobs = ([(wout, 0, ec) for ec in range(c.DC)]
                  + [(wffn, 1, ec) for ec in range(c.DC)])

    def emit_wcast(batch_idx, nbatches):
        n = len(wcast_jobs)
        lo = batch_idx * n // nbatches
        hi = (batch_idx + 1) * n // nbatches
        for (wd, half, ec) in wcast_jobs[lo:hi]:
            E('PLD', lambda e, wd=wd, half=half, ec=ec: e.dma_start(
                out=wbfd[half * c.D + ec * 128:half * c.D + (ec + 1) * 128, :],
                in_=wd[ec * 128:(ec + 1) * 128, :]),
              writes=[tk('wbfd', half, ec)])

    NSPg = 4 if c.HPC >= 4 else (2 if c.HPC >= 2 else 1)
    HHg = c.HPC // NSPg

    def emit_attn_gather(hg):
        E('CC', lambda e, hg=hg: e.collective_compute(
            "AllGather", ALU.bypass, replica_groups=groups,
            ins=[attnc[hg * HHg * c.HD:(hg + 1) * HHg * c.HD, :].opt()],
            outs=[attnm[hg * c.GPC * HHg * c.HD:
                        (hg + 1) * c.GPC * HHg * c.HD, :].opt()]),
          reads=[tk('attnc', i, hf2) for i in range(hg * HHg, (hg + 1) * HHg)
                 for hf2 in range(KVS)],
          writes=[tk('attnm', hg)])

    NQH = c.NQG // KVS
    seq = [(hf, i, hf * NQH + gl) for hf in range(KVS)
           for i in range(c.HPC) for gl in range(NQH)]
    emit_loads(0, 0)
    emit_vt(0, 0)
    emit_dh(0)
    NWB = KVS * c.HPC
    wb_done = [0]
    for k in range(len(seq)):
        hf, i, gi = seq[k]
        first_g = (gi == hf * NQH)
        last_g = (gi == (hf + 1) * NQH - 1)
        if first_g:
            nhf, ni = (hf, i + 1) if i + 1 < c.HPC else (hf + 1, 0)
            if nhf < KVS:
                emit_loads(ni, nhf)
            emit_wcast(wb_done[0], NWB)
            wb_done[0] += 1
        emit_attn(i, gi)
        if last_g:
            nhf, ni = (hf, i + 1) if i + 1 < c.HPC else (hf + 1, 0)
            if nhf < KVS:
                emit_vt(ni, nhf)
                emit_dh(ni)
            emit_attn_dma(i, hf)
            if hf == KVS - 1 and (i + 1) % HHg == 0:
                hg2 = (i + 1) // HHg - 1
                emit_attn_gather(hg2)

    es3.close()
    p.barrier(exclude=('CC',))

    es5 = ExitStack()

    def S5(name, shape, dt):
        return es5.enter_context(nc.sbuf_tensor(name, shape, dt))

    arhs = S5("arhs", [128, c.DC * ARS], BF16)
    abuf = S5("abuf", [128, c.DC * c.R], BF16)
    bbuf = S5("bbuf", [c.R, c.D], BF16)
    w8buf = S5("w8buf", [128, c.DC * c.E], BF16)
    limb = S5("limb", [c.E, c.R], BF16)
    limtb = S5("limtb", [c.R, c.E], BF16)
    wsl = [S5(f"wsl5{i}", [128, c.D], BF16) for i in range(3)]
    fmst5 = [S5(f"fmst5{i}", [128, 512], BF16) for i in range(2)]
    wgrps = [S5(f"wgrp{i}", [128, 512], F32) for i in range(2)]
    E('PLD', lambda e: e.dma_start(out=abuf[:, :], in_=amat[:, :]),
      writes=[tk('abuf')])
    E('PLD', lambda e: e.dma_start(out=bbuf[:, :], in_=bmat[:, :]),
      writes=[tk('bbuf')])
    E('PLD', lambda e: e.dma_start(out=w8buf[:, :], in_=w8[:, :]),
      writes=[tk('w8buf')])
    E('PLD', lambda e: e.dma_start(out=limb[:, :], in_=limes[:, :]),
      writes=[tk('limb')])
    E('PLD', lambda e: e.dma_start(out=limtb[:, :], in_=limest[:, :]),
      writes=[tk('limtb')])
    xres = [S5(f"xres{i}", [128, 512], F32) for i in range(4)]
    s1sb = [S5(f"s1sb{i}", [128, c.D], F32) for i in range(c.JT)]
    x2bt = [S5(f"x2bt{i}", [128, 512], BF16) for i in range(2)]
    rdt = [S5(f"rdt{i}", [128, 512], F32) for i in range(2)]
    zfm = S5("zfm", [c.R, c.T], F32)
    zab = S5("zab", [c.R, c.ANCH], BF16)
    h8fm = S5("h8fm", [c.E, c.T], F32)
    h8ab = S5("h8ab", [c.E, c.ANCH], F32)
    dsfm = S5("dsfm", [c.E, c.ANCH], F32)
    gtm = S5("gtm", [128, 32], F32)
    wffull = S5("wffull", [c.E, c.T], BF16)
    pmfm = S5("pmfm", [c.R, c.T], F32)
    zpbf = S5("zpbf", [c.R, c.T], BF16)

    attnm3 = attnm.ap().rearrange("(do p) t -> p do t", p=128)

    attnm_flat = attnm.ap().rearrange("r (t one) -> (r t) one", one=1)

    def hg_of(do):
        return (do % c.HPC) // HHg if NSPg > 1 else 0

    def arhs_load(tt, src3, dyncols, rtoks):
        lo_l, hi_l = tt * c.TT, (tt + 1) * c.TT
        for h in (0, 1):
            a = max(lo_l, h * c.CH)
            b = min(hi_l, (h + 1) * c.CH)
            if a >= b:
                continue
            if dyncols:
                for do in sorted(range(c.DC), key=lambda d2: (hg_of(d2), d2)):
                    col = (tt * 2 + h) * c.DC + do
                    E('PLD', lambda e, do=do, col=col, a=a, b=b, lo_l=lo_l:
                      e.indirect_dma_start(
                          out=arhs[:, do * ARS + (a - lo_l):
                                   do * ARS + (b - lo_l)],
                          out_offset=None, in_=attnm_flat,
                          in_offset=IndirectOffsetOnAxis(
                              aix[:, col:col + 1], 0),
                          element_offset=0),
                      reads=[tk('attnm', hg_of(do)), tk('aix')],
                      writes=[tk('arhs', do)])
            else:
                sp = src3[:, :, a:b]
                dp = arhs3[:, :, a - lo_l:b - lo_l]
                E('ACTD', lambda e, sp=sp, dp=dp: e.dma_start(out=dp, in_=sp),
                  reads=rtoks,
                  writes=[tk('arhs', do) for do in range(c.DC)])
    arhs3 = arhs.ap().rearrange("p (do t) -> p do t", do=c.DC)
    x2fm3 = x2fm.ap().rearrange("(do p) t -> p do t", p=128)

    def stream_proj(tt, whalf, src_resid, resid_tok, phase,
                    extra_mm=None, bias_col=None):
        JT = c.JT
        E('DVE', lambda e: e.memset(accs[:, 0:JT], 0.0), writes=[tk('accs')])
        E('DVE', lambda e: e.memset(accq[:, 0:JT], 0.0), writes=[tk('accq')])
        pend = [None]
        for ec in range(c.DC):
            wb = wsl[ec % 3]
            E('PLD', lambda e, ec=ec, wb=wb, whalf=whalf: e.dma_start(
                out=wb[:, :],
                in_=wbfd[whalf * c.D + ec * 128:
                         whalf * c.D + (ec + 1) * 128, :]),
              reads=[tk('wbfd', whalf, ec)],
              writes=[tk('wsl', ec % 3)])
            if ec % 4 == 0:
                for jj in range(JT):
                    rt = ([resid_tok(tt, jj, ec // 4)]
                          if resid_tok is not None else [])
                    E('SPD', lambda e, jj=jj, ec=ec, tt=tt,
                      src_resid=src_resid: e.dma_start(
                          out=xres[jj][:, :],
                          in_=src_resid[tt * c.TT + jj * 128:
                                        tt * c.TT + (jj + 1) * 128,
                                        (ec // 4) * 512:(ec // 4 + 1) * 512]),
                      reads=rt, writes=[tk('xres', jj)])
            pm = ps[f"f{ec % 2}"]
            dorder = (sorted(range(c.DC), key=lambda do: (hg_of(do), do))
                      if phase == 5 else list(range(c.DC)))
            for di, do in enumerate(dorder):
                E('PE', lambda e, wb=wb, do=do, pm=pm, di=di: e.matmul(
                    pm[:, 0:c.TT], wb[:, do * 128:(do + 1) * 128],
                    arhs[:, do * ARS:do * ARS + c.TT],
                    start=(di == 0),
                    stop=(di == c.DC - 1 and extra_mm is None)),
                  reads=[tk('wsl', ec % 3), tk('arhs', do)],
                  writes=[tk('pmm', ec % 2)])
            if extra_mm is not None:
                extra_mm(tt, pm, ec)
            wgrp = wgrps[ec % 2]
            if pend[0] is not None:
                pend[0]()
                pend[0] = None
            if bias_col is None:
                E('DVE', lambda e, pm=pm, wgrp=wgrp: e.tensor_copy(
                    wgrp[:, 0:c.TT], pm[:, 0:c.TT]),
                  reads=[tk('pmm', ec % 2)], writes=[tk('wgrp', ec % 2)])
            else:
                E('DVE', lambda e, pm=pm, ec=ec, wgrp=wgrp: e.tensor_scalar(
                    out=wgrp[:, 0:c.TT], in0=pm[:, 0:c.TT],
                    scalar1=bias_col[:, ec:ec + 1], scalar2=None,
                    op0=ALU.add),
                  reads=[tk('pmm', ec % 2), tk('bcol')],
                  writes=[tk('wgrp', ec % 2)])

            def _transposes(ec=ec, wgrp=wgrp):
                for jj in range(JT):
                    pt = ps[f"f{2 + jj}"]
                    E('PE', lambda e, jj=jj, ec=ec, pt=pt, wgrp=wgrp:
                      e.transpose(
                        pt[:, (ec % 4) * 128:(ec % 4 + 1) * 128],
                        wgrp[:, jj * 128:(jj + 1) * 128], ident_f[:, :]),
                      reads=[tk('wgrp', ec % 2), tk('ident_f')],
                      writes=[tk('ptr', jj)])
            pend[0] = _transposes
            if ec % 4 == 3:
                pend[0]()
                pend[0] = None
            if ec % 4 == 3:
                for jj in range(JT):
                    pt = ps[f"f{2 + jj}"]
                    eg0 = ec // 4
                    E('DVE', lambda e, pt=pt, jj=jj, eg0=eg0:
                      e.scalar_tensor_tensor(
                          out=s1sb[jj][:, eg0 * 512:(eg0 + 1) * 512],
                          in0=pt[:, 0:512], scalar=1.0,
                          in1=xres[jj][:, :], op0=ALU.mult, op1=ALU.add,
                          accum_out=stat[:, 0:1]),
                      reads=[tk('ptr', jj), tk('xres', jj)],
                      writes=[tk('s1sb', jj, eg0), tk('stat', 0)])
                    E('DVE', lambda e, jj=jj: e.tensor_tensor(
                        accs[:, jj:jj + 1], accs[:, jj:jj + 1],
                        stat[:, 0:1], ALU.add),
                      reads=[tk('accs'), tk('stat', 0)], writes=[tk('accs')])
                    E('ACT', lambda e, jj=jj, eg0=eg0: e.activation(
                        out=rdt[0][:, 0:512],
                        in_=s1sb[jj][:, eg0 * 512:(eg0 + 1) * 512],
                        func=AF.Square, bias=cst[:, 0:1],
                        accum_out=stat[:, 1:2]),
                      reads=[tk('s1sb', jj, eg0), tk('cst')],
                      writes=[tk('rdt', 0), tk('stat', 1)])
                    E('DVE', lambda e, jj=jj: e.tensor_tensor(
                        accq[:, jj:jj + 1], accq[:, jj:jj + 1],
                        stat[:, 1:2], ALU.add),
                      reads=[tk('accq'), tk('stat', 1)], writes=[tk('accq')])
        for jj in range(JT):
            E('ACT', lambda e, jj=jj: e.activation(
                out=stat[:, 2:3], in_=accs[:, jj:jj + 1], func=AF.Copy,
                scale=-1.0 / c.D),
              reads=[tk('accs')], writes=[tk('stat', 2)])
            E('ACT', lambda e: e.activation(
                out=stat[:, 3:4], in_=stat[:, 2:3], func=AF.Square,
                bias=cst[:, 0:1]),
              reads=[tk('stat', 2), tk('cst')], writes=[tk('stat', 3)])
            E('ACT', lambda e, jj=jj: e.activation(
                out=stat[:, 4:5], in_=accq[:, jj:jj + 1], func=AF.Copy,
                scale=1.0 / c.D),
              reads=[tk('accq')], writes=[tk('stat', 4)])
            E('DVE', lambda e: e.tensor_sub(
                stat[:, 5:6], stat[:, 4:5], stat[:, 3:4]),
              reads=[tk('stat', 3), tk('stat', 4)], writes=[tk('stat', 5)])
            E('ACT', lambda e: e.activation(
                out=stat[:, 6:7], in_=stat[:, 5:6], func=AF.Ln,
                bias=cst[:, 1:2]),
              reads=[tk('stat', 5), tk('cst')], writes=[tk('stat', 6)])
            E('ACT', lambda e: e.activation(
                out=stat[:, 7:8], in_=stat[:, 6:7], func=AF.Exp, scale=-0.5,
                bias=cst[:, 0:1]),
              reads=[tk('stat', 6), tk('cst')], writes=[tk('stat', 7)])
            for eg in range(c.DC // 4):
                rb = rdt[eg % 2]
                E('DVE', lambda e, rb=rb, jj=jj, eg=eg: e.tensor_scalar(
                    out=rb[:, :], in0=s1sb[jj][:, eg * 512:(eg + 1) * 512],
                    scalar1=stat[:, 2:3], scalar2=stat[:, 7:8],
                    op0=ALU.add, op1=ALU.mult),
                  reads=[tk('s1sb', jj, eg), tk('stat', 2), tk('stat', 7)],
                  writes=[tk('rdt', eg % 2)])
                if phase == 5:
                    E('SPD', lambda e, rb=rb, jj=jj, eg=eg, tt=tt:
                      e.dma_start(
                          out=x2sp[tt * c.TT + jj * 128:
                                   tt * c.TT + (jj + 1) * 128,
                                   eg * 512:(eg + 1) * 512],
                          in_=rb[:, :]),
                      reads=[tk('rdt', eg % 2)],
                      writes=[tk('x2sp', tt, jj, eg)])
                    xb2 = x2bt[eg % 2]
                    E('POOL', lambda e, rb=rb, xb2=xb2: e.tensor_copy(
                        xb2[:, :], rb[:, :]),
                      reads=[tk('rdt', eg % 2)], writes=[tk('x2bt', eg % 2)])
                    tb = ps[f"b{eg % 2}"]
                    for dd in range(4):
                        E('PE', lambda e, xb2=xb2, dd=dd, tb=tb: e.transpose(
                            tb[:, dd * 128:(dd + 1) * 128],
                            xb2[:, dd * 128:(dd + 1) * 128], ident_b[:, :]),
                          reads=[tk('x2bt', eg % 2), tk('ident_b')],
                          writes=[tk('psb', eg % 2)])
                    fst = fmst5[eg % 2]
                    E('DVE', lambda e, tb=tb, fst=fst: e.tensor_copy(
                        fst[:, 0:512], tb[:, 0:512]),
                      reads=[tk('psb', eg % 2)], writes=[tk('fmst2', eg % 2)])
                    src3 = fst.ap()[:, 0:512].rearrange(
                        "p (g t) -> p g t", g=4)
                    dst3 = x2fm3[:, eg * 4:eg * 4 + 4,
                                 tt * c.TT + jj * 128:
                                 tt * c.TT + (jj + 1) * 128]
                    E('SPD', lambda e, s=src3, d=dst3: e.dma_start(
                        out=d, in_=s),
                      reads=[tk('fmst2', eg % 2)],
                      writes=[tk('x2fm', tt, jj, eg)])
                else:
                    E('SPD', lambda e, rb=rb, jj=jj, eg=eg, tt=tt:
                      e.dma_start(
                          out=out_t[tt * c.TT + jj * 128:
                                    tt * c.TT + (jj + 1) * 128,
                                    eg * 512:(eg + 1) * 512],
                          in_=rb[:, :]),
                      reads=[tk('rdt', eg % 2)],
                      writes=[tk('out', tt, jj, eg)])

    def emit_gating_proj(tt):
        src3 = x2fm3[:, :, tt * c.TT:(tt + 1) * c.TT]
        dst3 = arhs3[:, :, 0:c.TT]
        E('ACTD', lambda e, s=src3, d=dst3: e.dma_start(out=d, in_=s),
          reads=[tk('x2fm', tt, jj, eg) for jj in range(c.JT)
                 for eg in range(c.DC // 4)],
          writes=[tk('arhs', do) for do in range(c.DC)])
        for do in range(c.DC):
            E('PE', lambda e, do=do: e.matmul(
                ps['f0'][0:c.R, 0:c.TT], abuf[:, do * c.R:(do + 1) * c.R],
                arhs[:, do * ARS:do * ARS + c.TT],
                start=(do == 0), stop=(do == c.DC - 1)),
              reads=[tk('abuf'), tk('arhs', do)], writes=[tk('pmm', 0)])
        E('DVE', lambda e, tt=tt: e.tensor_copy(
            zfm[:, tt * c.TT:(tt + 1) * c.TT], ps['f0'][0:c.R, 0:c.TT]),
          reads=[tk('pmm', 0)], writes=[tk('zfm', tt)])
        for do in range(c.DC):
            E('PE', lambda e, do=do: e.matmul(
                ps['f1'][0:c.E, 0:c.TT], w8buf[:, do * c.E:(do + 1) * c.E],
                arhs[:, do * ARS:do * ARS + c.TT],
                start=(do == 0), stop=(do == c.DC - 1)),
              reads=[tk('w8buf'), tk('arhs', do)], writes=[tk('pmm', 1)])
        E('DVE', lambda e, tt=tt: e.tensor_scalar(
            out=h8fm[:, tt * c.TT:(tt + 1) * c.TT],
            in0=ps['f1'][0:c.E, 0:c.TT],
            scalar1=bcol[0:c.E, 0:1], scalar2=None, op0=ALU.add),
          reads=[tk('pmm', 1), tk('bcol')], writes=[tk('h8fm', tt)])

    for tt in range(c.NTT):
        arhs_load(tt, attnm3, True, None)
        stream_proj(tt, 0, x_in, None, 5)
        emit_gating_proj(tt)


    zread = [tk('zfm', tt) for tt in range(c.NTT)]
    hread = [tk('h8fm', tt) for tt in range(c.NTT)]
    z3 = zfm.ap().rearrange("p (t two) -> p t two", two=2)
    h3 = h8fm.ap().rearrange("p (t two) -> p t two", two=2)
    E('DVE', lambda e: e.tensor_copy(zab[:, :], z3[:, :, 1:2]),
      reads=zread, writes=[tk('zab')])
    E('DVE', lambda e: e.tensor_copy(h8ab[:, :], h3[0:c.E, :, 1:2]),
      reads=hread, writes=[tk('h8ab')])
    npt = (c.ANCH + 511) // 512
    for q in range(npt):
        w = min(512, c.ANCH - q * 512)
        E('PE', lambda e, q=q, w=w: e.matmul(
            ps['f0'][0:c.E, 0:w], limtb[:, :],
            zab[:, q * 512:q * 512 + w], start=True, stop=True),
          reads=[tk('zab'), tk('limtb')], writes=[tk('pmm', 0)])
        E('DVE', lambda e, q=q, w=w: e.tensor_copy(
            dsfm[:, q * 512:q * 512 + w], ps['f0'][0:c.E, 0:w]),
          reads=[tk('pmm', 0)], writes=[tk('dsfm', q)])
    dread = [tk('dsfm', q) for q in range(npt)]
    E('DVE', lambda e: e.reduce_max(
        gsc[0:c.E, 0:1], h8ab[:, :], axis=AX.X, apply_absolute_value=True),
      reads=[tk('h8ab')], writes=[tk('gsc', 0)])
    E('DVE', lambda e: e.reduce_max(
        gsc[0:c.E, 1:2], dsfm[:, :], axis=AX.X, apply_absolute_value=True),
      reads=dread, writes=[tk('gsc', 1)])
    E('PE', lambda e: e.transpose(ps['f0'][0:2, 0:c.E], gsc[0:c.E, 0:2],
                                  ident_f[0:c.E, 0:c.E]),
      reads=[tk('gsc', 0), tk('gsc', 1), tk('ident_f')],
      writes=[tk('pmm', 0)])
    E('DVE', lambda e: e.tensor_copy(gsc[0:2, 2:2 + c.E],
                                     ps['f0'][0:2, 0:c.E]),
      reads=[tk('pmm', 0)], writes=[tk('gsc', 2)])
    E('DVE', lambda e: e.reduce_max(gsc[0:2, 10:11], gsc[0:2, 2:2 + c.E],
                                    axis=AX.X),
      reads=[tk('gsc', 2)], writes=[tk('gsc', 10)])
    E('SPD', lambda e: e.dma_start(out=scalc[:, :], in_=gsc[0:2, 10:11]),
      reads=[tk('gsc', 10)], writes=[tk('scalc')])
    E('CC', lambda e: e.collective_compute(
        "AllGather", ALU.bypass, replica_groups=[list(range(c.NCORES))],
        ins=[scalc.ap().opt()], outs=[scala.ap().opt()]),
      reads=[tk('scalc')], writes=[tk('scala')])
    E('SPD', lambda e: e.dma_start(out=gsc[0:2 * c.NCORES, 3:4],
                                   in_=scala[:, :]),
      reads=[tk('scala')], writes=[tk('gsc', 3)])
    E('PE', lambda e: e.transpose(
        ps['f0'][0:1, 0:2 * c.NCORES], gsc[0:2 * c.NCORES, 3:4],
        ident_f[0:2 * c.NCORES, 0:2 * c.NCORES]),
      reads=[tk('gsc', 3), tk('ident_f')], writes=[tk('pmm', 0)])
    E('DVE', lambda e: e.tensor_copy(gsc[0:1, 4:4 + 2 * c.NCORES],
                                     ps['f0'][0:1, 0:2 * c.NCORES]),
      reads=[tk('pmm', 0)], writes=[tk('gsc', 4)])
    g4 = gsc.ap()[0:1, 4:4 + 2 * c.NCORES].rearrange(
        "p (r two) -> p r two", two=2)
    E('DVE', lambda e: e.reduce_max(gsc[0:1, 12:13], g4[:, :, 0:1],
                                    axis=AX.XY),
      reads=[tk('gsc', 4)], writes=[tk('gsc', 12)])
    E('DVE', lambda e: e.reduce_max(gsc[0:1, 13:14], g4[:, :, 1:2],
                                    axis=AX.XY),
      reads=[tk('gsc', 4)], writes=[tk('gsc', 13)])
    for k in (12, 13):
        E('DVE', lambda e, k=k: e.tensor_scalar(
            out=gsc[0:1, k:k + 1], in0=gsc[0:1, k:k + 1], scalar1=FFN_EPS,
            scalar2=None, op0=ALU.max),
          reads=[tk('gsc', k)], writes=[tk('gsc', k)])
        E('DVE', lambda e, k=k: e.reciprocal(gsc[0:1, k + 2:k + 3],
                                             gsc[0:1, k:k + 1]),
          reads=[tk('gsc', k)], writes=[tk('gsc', k + 2)])
        E('ACT', lambda e, k=k: e.activation(
            out=gsc[0:1, k + 4:k + 5], in_=gsc[0:1, k + 2:k + 3],
            func=AF.Copy, scale=0.5),
          reads=[tk('gsc', k + 2)], writes=[tk('gsc', k + 4)])
    for k, col in ((16, 0), (17, 1)):
        E('PE', lambda e, k=k, col=col: e.matmul(
            ps['f0'][:, col:col + 1], ones_f[0:1, :], gsc[0:1, k:k + 1],
            start=True, stop=True),
          reads=[tk('gsc', k), tk('ones_f')], writes=[tk('pmm', 0)])
        E('DVE', lambda e, col=col: e.tensor_copy(gbc[:, col:col + 1],
                                                  ps['f0'][:, col:col + 1]),
          reads=[tk('pmm', 0)], writes=[tk('gbc', col)])

    ACB = min(128, c.ANCH)
    wf3 = wffull.ap().rearrange("p (t two) -> p t two", two=2)
    for ac in range(c.ANCH // ACB):
        a0 = ac * ACB
        E('PE', lambda e, a0=a0: e.transpose(
            ps['f0'][0:ACB, 0:c.E], h8ab[:, a0:a0 + ACB],
            ident_f[0:c.E, 0:c.E]),
          reads=[tk('h8ab'), tk('ident_f')], writes=[tk('pmm', 0)])
        E('DVE', lambda e: e.tensor_copy(gtm[0:ACB, 0:c.E],
                                         ps['f0'][0:ACB, 0:c.E]),
          reads=[tk('pmm', 0)], writes=[tk('gtm', 0)])
        E('PE', lambda e, a0=a0: e.transpose(
            ps['f1'][0:ACB, 0:c.E], dsfm[0:c.E, a0:a0 + ACB],
            ident_f[0:c.E, 0:c.E]),
          reads=dread + [tk('ident_f')], writes=[tk('pmm', 1)])
        E('DVE', lambda e: e.tensor_copy(gtm[0:ACB, 8:8 + c.E],
                                         ps['f1'][0:ACB, 0:c.E]),
          reads=[tk('pmm', 1)], writes=[tk('gtm', 1)])
        E('DVE', lambda e: e.tensor_scalar(
            out=gtm[0:ACB, 0:c.E], in0=gtm[0:ACB, 0:c.E],
            scalar1=gbc[0:ACB, 0:1], scalar2=None, op0=ALU.mult),
          reads=[tk('gtm', 0), tk('gbc', 0)], writes=[tk('gtm', 0)])
        E('DVE', lambda e: e.scalar_tensor_tensor(
            out=gtm[0:ACB, 0:c.E], in0=gtm[0:ACB, 8:8 + c.E],
            scalar=gbc[0:ACB, 1:2], in1=gtm[0:ACB, 0:c.E],
            op0=ALU.mult, op1=ALU.add),
          reads=[tk('gtm', 0), tk('gtm', 1), tk('gbc', 1)],
          writes=[tk('gtm', 0)])
        E('DVE', lambda e: e.reduce_max(gtm[0:ACB, 16:17], gtm[0:ACB, 0:c.E],
                                        axis=AX.X, negate=True),
          reads=[tk('gtm', 0)], writes=[tk('gtm', 16)])
        E('ACT', lambda e: e.activation(
            out=gtm[0:ACB, 0:c.E], in_=gtm[0:ACB, 0:c.E], func=AF.Exp,
            bias=gtm[0:ACB, 16:17], accum_out=gtm[0:ACB, 17:18]),
          reads=[tk('gtm', 0), tk('gtm', 16)],
          writes=[tk('gtm', 0), tk('gtm', 17)])
        E('DVE', lambda e: e.reciprocal(gtm[0:ACB, 18:19],
                                        gtm[0:ACB, 17:18]),
          reads=[tk('gtm', 17)], writes=[tk('gtm', 18)])
        E('DVE', lambda e: e.tensor_scalar(
            out=gtm[0:ACB, 0:c.E], in0=gtm[0:ACB, 0:c.E],
            scalar1=gtm[0:ACB, 18:19], scalar2=None, op0=ALU.mult),
          reads=[tk('gtm', 0), tk('gtm', 18)], writes=[tk('gtm', 0)])
        E('DVE', lambda e: e.reduce_max(gtm[0:ACB, 19:20], gtm[0:ACB, 0:c.E],
                                        axis=AX.X),
          reads=[tk('gtm', 0)], writes=[tk('gtm', 19)])
        E('DVE', lambda e: e.tensor_scalar(
            out=gtm[0:ACB, 8:8 + c.E], in0=gtm[0:ACB, 0:c.E],
            scalar1=gtm[0:ACB, 19:20], scalar2=None, op0=ALU.subtract),
          reads=[tk('gtm', 0), tk('gtm', 19)], writes=[tk('gtm', 1)])
        E('ACT', lambda e: e.activation(
            out=gtm[0:ACB, 8:8 + c.E], in_=gtm[0:ACB, 8:8 + c.E],
            func=AF.Sign, bias=cst[0:ACB, 0:1]),
          reads=[tk('gtm', 1), tk('cst')], writes=[tk('gtm', 1)])
        E('DVE', lambda e: e.tensor_scalar(
            out=gtm[0:ACB, 8:8 + c.E], in0=gtm[0:ACB, 8:8 + c.E],
            scalar1=1.0, scalar2=None, op0=ALU.add),
          reads=[tk('gtm', 1)], writes=[tk('gtm', 1)])
        E('DVE', lambda e: e.scalar_tensor_tensor(
            out=gtm[0:ACB, 8:8 + c.E], in0=gtm[0:ACB, 8:8 + c.E],
            scalar=NEG_BIG, in1=gtm[0:ACB, 0:c.E], op0=ALU.mult,
            op1=ALU.add),
          reads=[tk('gtm', 0), tk('gtm', 1)], writes=[tk('gtm', 1)])
        E('DVE', lambda e: e.reduce_max(gtm[0:ACB, 20:21],
                                        gtm[0:ACB, 8:8 + c.E], axis=AX.X),
          reads=[tk('gtm', 1)], writes=[tk('gtm', 20)])
        E('ACT', lambda e: e.activation(
            out=gtm[0:ACB, 21:22], in_=gtm[0:ACB, 20:21], func=AF.Copy,
            scale=-2.0),
          reads=[tk('gtm', 20)], writes=[tk('gtm', 21)])
        E('ACT', lambda e: e.activation(
            out=gtm[0:ACB, 8:8 + c.E], in_=gtm[0:ACB, 0:c.E],
            func=AF.Sigmoid, scale=2.0, bias=gtm[0:ACB, 21:22]),
          reads=[tk('gtm', 0), tk('gtm', 21)], writes=[tk('gtm', 1)])
        E('DVE', lambda e: e.tensor_mul(gtm[0:ACB, 0:c.E],
                                        gtm[0:ACB, 0:c.E],
                                        gtm[0:ACB, 8:8 + c.E]),
          reads=[tk('gtm', 0), tk('gtm', 1)], writes=[tk('gtm', 0)])
        E('DVE', lambda e: e.reduce_sum(gtm[0:ACB, 22:23],
                                        gtm[0:ACB, 0:c.E], axis=AX.X),
          reads=[tk('gtm', 0)], writes=[tk('gtm', 22)])
        E('ACT', lambda e: e.activation(
            out=gtm[0:ACB, 23:24], in_=gtm[0:ACB, 22:23], func=AF.Copy,
            bias=1e-9),
          reads=[tk('gtm', 22)], writes=[tk('gtm', 23)])
        E('DVE', lambda e: e.reciprocal(gtm[0:ACB, 24:25],
                                        gtm[0:ACB, 23:24]),
          reads=[tk('gtm', 23)], writes=[tk('gtm', 24)])
        E('DVE', lambda e: e.tensor_scalar(
            out=gtm[0:ACB, 0:c.E], in0=gtm[0:ACB, 0:c.E],
            scalar1=gtm[0:ACB, 24:25], scalar2=None, op0=ALU.mult),
          reads=[tk('gtm', 0), tk('gtm', 24)], writes=[tk('gtm', 0)])
        E('PE', lambda e: e.transpose(
            ps['f0'][0:c.E, 0:ACB], gtm[0:ACB, 0:c.E],
            ident_f[0:ACB, 0:ACB]),
          reads=[tk('gtm', 0), tk('ident_f')], writes=[tk('pmm', 0)])
        E('DVE', lambda e, a0=a0: e.tensor_copy(
            wf3[0:c.E, a0:a0 + ACB, 0:1], ps['f0'][0:c.E, 0:ACB]),
          reads=[tk('pmm', 0)], writes=[tk('wffull', ac, 0)])
        E('DVE', lambda e, a0=a0: e.tensor_copy(
            wf3[0:c.E, a0:a0 + ACB, 1:2], ps['f0'][0:c.E, 0:ACB]),
          reads=[tk('pmm', 0)], writes=[tk('wffull', ac, 1)])
    wfread = [tk('wffull', ac, s) for ac in range(c.ANCH // ACB)
              for s in range(2)]
    for tt in range(c.NTT):
        E('PE', lambda e, tt=tt: e.matmul(
            ps['f0'][0:c.R, 0:c.TT], limb[:, :],
            wffull[:, tt * c.TT:(tt + 1) * c.TT], start=True, stop=True),
          reads=wfread + [tk('limb')], writes=[tk('pmm', 0)])
        E('DVE', lambda e, tt=tt: e.tensor_copy(
            pmfm[:, tt * c.TT:(tt + 1) * c.TT], ps['f0'][0:c.R, 0:c.TT]),
          reads=[tk('pmm', 0)], writes=[tk('pmfm', tt)])
    E('DVE', lambda e: e.tensor_mul(pmfm[:, :], pmfm[:, :], zfm[:, :]),
      reads=zread + [tk('pmfm', tt2) for tt2 in range(c.NTT)],
      writes=[tk('pmfm2')])
    E('DVE', lambda e: e.tensor_scalar(
        out=zpbf[:, :], in0=pmfm[:, :], scalar1=0.5, scalar2=None,
        op0=ALU.mult),
      reads=[tk('pmfm2')], writes=[tk('zpbf')])

    def ffn_extra(tt, pm, ec):
        E('PE', lambda e, pm=pm, ec=ec, tt=tt: e.matmul(
            pm[:, 0:c.TT], bbuf[:, ec * 128:(ec + 1) * 128],
            zpbf[:, tt * c.TT:(tt + 1) * c.TT], start=False, stop=True),
          reads=[tk('bbuf'), tk('zpbf')], writes=[tk('pmm', ec % 2)])

    for tt in range(c.NTT):
        src3 = x2fm3[:, :, tt * c.TT:(tt + 1) * c.TT]
        dst3 = arhs3[:, :, 0:c.TT]
        E('ACTD', lambda e, s=src3, d=dst3: e.dma_start(out=d, in_=s),
          reads=[tk('x2fm', tt, jj, eg) for jj in range(c.JT)
                 for eg in range(c.DC // 4)],
          writes=[tk('arhs', do) for do in range(c.DC)])
        stream_proj(tt, 1, x2sp,
                    (lambda tt2, jj2, eg2: tk('x2sp', tt2, jj2, eg2)),
                    7, extra_mm=ffn_extra, bias_col=bcol)

    dbg_map = {'qkvc': qkvc, 'qkva': qkva, 'attnc': attnc, 'attnm': attnm,
               'x2sp': x2sp, 'x2fm': x2fm}
    if dbg:
        snap = p.snap()
        for name in dbg:
            if name not in dbg_map:
                continue
            src = dbg_map[name]
            dt = nc.dram_tensor(f"dbg_{name}", list(src.shape), src.dtype,
                                kind="ExternalOutput")
            E('SPD', lambda e, dt=dt, src=src: e.dma_start(
                out=dt[:, :], in_=src[:, :]), extra_waits=snap)

    p.barrier()
    es5.close()
    p.build()
    return nc



def host_layouts(cfg, Wqkv, Wout, Wffn, A, Bmat, LiMEs, qw, qb, kw, kb, fb):
    c = cfg

    def wt(W, nec):
        return np.ascontiguousarray(
            W.reshape(nec, 128, c.DC, 128).transpose(0, 3, 2, 1)
            .reshape(nec * 128, c.DC * 128))

    d = {}
    Wqkv_r = np.concatenate([Wqkv[2 * c.D:3 * c.D], Wqkv[0:c.D],
                             Wqkv[c.D:2 * c.D]], axis=0)
    d['wqkv'] = wt(Wqkv_r, c.QKVC)
    d['wout'] = wt(Wout, c.DC)
    d['wffn'] = wt(Wffn, c.DC)
    d['w8'] = np.ascontiguousarray(
        Wffn[0:c.E].reshape(c.E, c.DC, 128).transpose(2, 1, 0)
        .reshape(128, c.DC * c.E))
    d['amat'] = np.ascontiguousarray(
        A.reshape(c.DC, 128, c.R).transpose(1, 0, 2).reshape(128, c.DC * c.R))
    d['bmat'] = np.ascontiguousarray(Bmat)
    d['limes'] = np.ascontiguousarray(LiMEs)
    d['limest'] = np.ascontiguousarray(LiMEs.T)
    d['qlnw'] = np.ascontiguousarray(qw.reshape(c.DC, 128))
    d['qlnb'] = np.ascontiguousarray(qb.reshape(c.DC, 128))
    d['klnw'] = np.ascontiguousarray(kw.reshape(c.DC, 128))
    d['klnb'] = np.ascontiguousarray(kb.reshape(c.DC, 128))
    d['fbias'] = np.ascontiguousarray(fb.reshape(c.DC, 128))
    return d


def shard_x(cfg, x):
    c = cfg
    shards = []
    for core in range(c.NCORES):
        b, j = core // c.GPC, core % c.GPC
        g2 = 2 * c.GPC - 1 - j
        shards.append(np.ascontiguousarray(np.concatenate([
            x[b, j * c.CH:(j + 1) * c.CH],
            x[b, g2 * c.CH:(g2 + 1) * c.CH]], axis=0)))
    return shards


def unshard_out(cfg, outs):
    c = cfg
    D = outs[0].shape[1]
    full = np.empty((c.B, c.S, D), dtype=outs[0].dtype)
    for core in range(c.NCORES):
        b, j = core // c.GPC, core % c.GPC
        g2 = 2 * c.GPC - 1 - j
        full[b, j * c.CH:(j + 1) * c.CH] = outs[core][0:c.CH]
        full[b, g2 * c.CH:(g2 + 1) * c.CH] = outs[core][c.CH:2 * c.CH]
    return full


def core_sk3(cfg):
    c = cfg
    sl = alibi_slopes_np(c.NH).astype(np.float64)
    gk = np.arange(c.S, dtype=np.float64)
    out = []
    for core in range(c.NCORES):
        j = core % c.GPC
        rows = np.zeros((3 * c.HPC, c.S), dtype=ml_dtypes.bfloat16)
        for i in range(c.HPC):
            v = -sl[j * c.HPC + i] * gk
            hi = v.astype(ml_dtypes.bfloat16)
            r1 = v - hi.astype(np.float64)
            lo = r1.astype(ml_dtypes.bfloat16)
            r2 = r1 - lo.astype(np.float64)
            lo2 = r2.astype(ml_dtypes.bfloat16)
            rows[3 * i + 0] = hi
            rows[3 * i + 1] = lo
            rows[3 * i + 2] = lo2
        out.append(rows)
    return out


def core_alcol(cfg):
    c = cfg
    sl = alibi_slopes_np(c.NH).astype(np.float64)
    nkb = c.S // 128
    out = []
    for core in range(c.NCORES):
        j = core % c.GPC
        a_ = np.zeros((128, c.HPC * nkb), dtype=np.float32)
        p = np.arange(128, dtype=np.float64)
        for i in range(c.HPC):
            for t in range(nkb):
                a_[:, i * nkb + t] = sl[j * c.HPC + i] * (128 * t + p)
        out.append(a_)
    return out


def core_gidx(cfg):
    c = cfg
    out = []
    for core in range(c.NCORES):
        j = core % c.GPC
        g_ = np.zeros((128, 6 * c.GPC * c.HPC), dtype=np.int32)
        for i in range(c.HPC):
            for part in range(3):
                for g2 in range(2 * c.GPC):
                    r = g2 if g2 < c.GPC else 2 * c.GPC - 1 - g2
                    hf = 0 if g2 < c.GPC else 1
                    kcol = (i * 3 + part) * 2 * c.GPC + g2
                    base = ((hf * 3 + part) * c.GPC * c.D + r * c.D
                            + (j * c.HPC + i) * c.HD)
                    g_[:, kcol] = base + np.arange(128)
        out.append(g_)
    return out


def core_aidx(cfg):
    c = cfg
    out = []
    for core in range(c.NCORES):
        j = core % c.GPC
        cbase = [j * c.CH, (2 * c.GPC - 1 - j) * c.CH]
        a_ = np.zeros((128, c.NTT * 2 * c.DC), dtype=np.int32)
        for tt in range(c.NTT):
            lo_l, hi_l = tt * c.TT, (tt + 1) * c.TT
            for h in (0, 1):
                aa = max(lo_l, h * c.CH)
                bb = min(hi_l, (h + 1) * c.CH)
                if aa >= bb:
                    continue
                gcol = cbase[h] + (aa - h * c.CH)
                NSP = 4 if c.HPC >= 4 else (2 if c.HPC >= 2 else 1)
                HH = c.HPC // NSP
                for do in range(c.DC):
                    gh = do
                    r, i = gh // c.HPC, gh % c.HPC
                    hg, i2 = i // HH, i % HH
                    row0 = (hg * c.GPC * HH + r * HH + i2) * 128
                    col = (tt * 2 + h) * c.DC + do
                    a_[:, col] = (row0 + np.arange(128)) * c.S + gcol
        out.append(a_)
    return out


def make_in_maps(cfg, x_full, weights):
    shards = shard_x(cfg, x_full)
    gidxs = core_gidx(cfg)
    aidxs = core_aidx(cfg)
    sk3s = core_sk3(cfg)
    alcols = core_alcol(cfg)
    return [dict(weights, x=shards[i], gidx=gidxs[i],
                 aidx=aidxs[i], sk3=sk3s[i], alcol=alcols[i])
            for i in range(cfg.NCORES)]



def kernel(L0_attn_hidden_states, L0_attn_Wqkv, L0_attn_q_ln_w, L0_attn_q_ln_b,
           L0_attn_k_ln_w, L0_attn_k_ln_b, L0_attn_Wout, L0_ffn_W, L0_ffn_b,
           L0_ffn_A, L0_ffn_Bmat, L0_ffn_LiMEs):
    cfg = make_cfg()
    x = np.asarray(L0_attn_hidden_states, dtype=np.float32)
    weights = host_layouts(
        cfg, np.asarray(L0_attn_Wqkv, np.float32),
        np.asarray(L0_attn_Wout, np.float32),
        np.asarray(L0_ffn_W, np.float32),
        np.asarray(L0_ffn_A, np.float32),
        np.asarray(L0_ffn_Bmat, np.float32),
        np.asarray(L0_ffn_LiMEs, np.float32),
        np.asarray(L0_attn_q_ln_w, np.float32),
        np.asarray(L0_attn_q_ln_b, np.float32),
        np.asarray(L0_attn_k_ln_w, np.float32),
        np.asarray(L0_attn_k_ln_b, np.float32),
        np.asarray(L0_ffn_b, np.float32))
    in_maps = make_in_maps(cfg, x, weights)
    nc = build_program(cfg)
    res = run_bass_kernel_spmd(nc, in_maps, core_ids=list(range(cfg.NCORES)))
    outs = [np.asarray(res.results[i]["out"]) for i in range(cfg.NCORES)]
    return unshard_out(cfg, outs)


# version 20
# speedup vs baseline: 1.0502x; 1.0502x over previous
import math
from contextlib import ExitStack
from types import SimpleNamespace

import ml_dtypes
import numpy as np

import concourse.bass as bass
import concourse.mybir as mybir
from concourse.bass import ds
from concourse.expressions_rust import RuntimeValue
from concourse.bass_core_rust import IndirectOffsetOnAxis
from concourse.bass_utils import run_bass_kernel_spmd

F32 = mybir.dt.float32
BF16 = mybir.dt.bfloat16
AF = mybir.ActivationFunctionType
ALU = mybir.AluOpType
AX = mybir.AxisListType

CLIP = 8.0
LN_EPS = 1e-5
FFN_EPS = 1e-6
NEG_BIG = -1e30
ARS = 512


def make_cfg(S=4096, D=4096, NH=32, B=2, NCORES=8):
    c = SimpleNamespace()
    c.S, c.D, c.NH, c.B, c.NCORES = S, D, NH, B, NCORES
    c.HD = D // NH
    assert c.HD == 128
    c.GPC = NCORES // B
    c.CH = S // (2 * c.GPC)
    c.T = 2 * c.CH
    c.HPC = NH // c.GPC
    c.DC = D // 128
    c.QKVC = 3 * D // 128
    c.E, c.R = 8, 32
    c.TT = min(512, c.T)
    c.NTT = c.T // c.TT
    c.JT = c.TT // 128
    c.NTC = c.T // 128
    c.QB = S // 128
    c.NQG = c.QB // 4
    c.ANCH = c.T // 2
    assert c.DC % 4 == 0 and c.T % 128 == 0 and c.S % 512 == 0
    assert c.QB % 4 == 0
    c.BLK = 3 * c.HPC * c.HD
    c.ABLK = c.HPC * c.HD
    return c


def alibi_slopes_np(n_heads, bias_max=8.0):
    next_pow2 = 2 ** int(np.ceil(np.log2(n_heads)))
    m = np.arange(1, next_pow2 + 1, dtype=np.float32) * (bias_max / next_pow2)
    slopes = 1.0 / np.power(2.0, m)
    if next_pow2 != n_heads:
        slopes = np.concatenate([slopes[1::2], slopes[::2]])[:n_heads]
    return slopes.astype(np.float32)



class Prog:
    QUEUE = {
        'PE': 'tensor', 'DVE': 'vector', 'ACT': 'scalar', 'POOL': 'gpsimd',
        'SP': 'sync', 'SPD': 'sync', 'ACTD': 'scalar', 'PLD': 'gpsimd',
        'CC': 'gpsimd',
    }
    DMA_CH = frozenset(('SPD', 'ACTD', 'PLD'))
    NSEM = 16

    def __init__(self, nc):
        self.nc = nc
        self.sem = {}
        for ch in self.QUEUE:
            if ch in self.DMA_CH:
                self.sem[ch] = [nc.alloc_semaphore(f"sem_{ch}_{i}")
                                for i in range(self.NSEM)]
            else:
                self.sem[ch] = nc.alloc_semaphore(f"sem_{ch}")
        self.cnt = {ch: 0 for ch in self.QUEUE}
        self.lastw = {ch: {} for ch in self.QUEUE}
        self.ops = []
        self.st = {}

    def _wait_spec(self, pch, v):
        if pch in self.DMA_CH:
            idx = v - 1
            slot = idx % self.NSEM
            return (pch, slot), self.sem[pch][slot], 16 * (idx // self.NSEM + 1)
        return (pch, 0), self.sem[pch], v

    def emit(self, ch, fn, reads=(), writes=(), extra_waits=()):
        needs = {}

        def need(pch, v):
            if pch == ch and ch == 'PE':
                return
            key, semh, val = self._wait_spec(pch, v)
            cur = needs.get(key)
            if cur is None or val > cur[1]:
                needs[key] = (semh, val)

        for t in reads:
            s = self.st.get(t)
            if s and s[0]:
                need(*s[0])
        for t in writes:
            s = self.st.get(t)
            if s:
                if s[0]:
                    need(*s[0])
                for (rch, _slot), rv in s[1].items():
                    need(rch, rv)
        for pch, v in extra_waits:
            need(pch, v)
        waits = []
        for key in sorted(needs):
            semh, val = needs[key]
            if self.lastw[ch].get(key, 0) >= val:
                continue
            self.lastw[ch][key] = val
            waits.append((semh, val))
        self.cnt[ch] += 1
        mycnt = self.cnt[ch]
        if ch in self.DMA_CH:
            idx = mycnt - 1
            slot = idx % self.NSEM
            mysem = self.sem[ch][slot]
            amt = 16
            if idx >= self.NSEM:
                key = (ch, slot)
                val = 16 * (idx // self.NSEM)
                if self.lastw[ch].get(key, 0) < val:
                    self.lastw[ch][key] = val
                    waits.append((mysem, val))
        else:
            mysem = self.sem[ch]
            amt = 1
        self.ops.append((ch, waits, fn, mysem, amt))
        for t in writes:
            self.st[t] = [(ch, mycnt), {}]
        rkey = (ch, (mycnt - 1) % self.NSEM if ch in self.DMA_CH else 0)
        for t in reads:
            s = self.st.setdefault(t, [None, {}])
            if s[1].get(rkey, 0) < mycnt:
                s[1][rkey] = mycnt

    def snap(self):
        return [(pch, self.cnt[pch]) for pch in self.QUEUE if self.cnt[pch] > 0]

    def barrier(self, exclude=()):
        specs = []
        for pch in self.QUEUE:
            n = self.cnt[pch]
            if n == 0 or pch in exclude:
                continue
            if pch in self.DMA_CH:
                for s in range(min(n, self.NSEM)):
                    ns = (n + self.NSEM - 1 - s) // self.NSEM
                    specs.append((self.sem[pch][s], 16 * ns, (pch, s)))
            else:
                specs.append((self.sem[pch], n, (pch, 0)))
        for ch in ('PE', 'DVE', 'ACT', 'POOL', 'SP'):
            waits = []
            for semh, val, key in specs:
                if self.lastw[ch].get(key, 0) >= val:
                    continue
                self.lastw[ch][key] = val
                waits.append((semh, val))
            self.cnt[ch] += 1
            self.ops.append((ch, waits, lambda e: e.nop(),
                             self.sem[ch] if ch not in self.DMA_CH else None, 1))

    def build(self):
        by_queue = {}
        for op in self.ops:
            by_queue.setdefault(self.QUEUE[op[0]], []).append(op)

        def run_list(eng, lst):
            for ch, waits, fn, mysem, amt in lst:
                for semh, v in waits:
                    eng.wait_ge(semh, v)
                inst = fn(eng)
                if inst is not None:
                    inst.then_inc(mysem, amt)

        nc = self.nc
        with nc.Block() as block:
            for qname in ('tensor', 'vector', 'scalar', 'gpsimd', 'sync'):
                lst = by_queue.get(qname, [])
                if not lst:
                    continue
                getattr(block, qname)(
                    (lambda lst: (lambda eng: run_list(eng, lst)))(lst))



def build_program(cfg, dbg=()):
    c = cfg
    nc = bass.Bass("TRN2", target_bir_lowering=False)

    x_in = nc.dram_tensor("x", [c.T, c.D], F32, kind="ExternalInput")
    wqkv = nc.dram_tensor("wqkv", [c.QKVC * 128, c.D], F32, kind="ExternalInput")
    wout = nc.dram_tensor("wout", [c.D, c.D], F32, kind="ExternalInput")
    wffn = nc.dram_tensor("wffn", [c.D, c.D], F32, kind="ExternalInput")
    w8 = nc.dram_tensor("w8", [128, c.DC * c.E], F32, kind="ExternalInput")
    amat = nc.dram_tensor("amat", [128, c.DC * c.R], F32, kind="ExternalInput")
    bmat = nc.dram_tensor("bmat", [c.R, c.D], F32, kind="ExternalInput")
    limes = nc.dram_tensor("limes", [c.E, c.R], F32, kind="ExternalInput")
    limest = nc.dram_tensor("limest", [c.R, c.E], F32, kind="ExternalInput")
    qlnw = nc.dram_tensor("qlnw", [c.DC, 128], F32, kind="ExternalInput")
    qlnb = nc.dram_tensor("qlnb", [c.DC, 128], F32, kind="ExternalInput")
    klnw = nc.dram_tensor("klnw", [c.DC, 128], F32, kind="ExternalInput")
    klnb = nc.dram_tensor("klnb", [c.DC, 128], F32, kind="ExternalInput")
    fbias = nc.dram_tensor("fbias", [c.DC, 128], F32, kind="ExternalInput")
    sk3 = nc.dram_tensor("sk3", [3 * c.HPC, c.S], BF16, kind="ExternalInput")
    alcol = nc.dram_tensor("alcol", [128, (c.S // 128) * c.HPC], F32,
                           kind="ExternalInput")
    gidx = nc.dram_tensor("gidx", [128, 6 * c.GPC * c.HPC], mybir.dt.int32,
                          kind="ExternalInput")
    aidx = nc.dram_tensor("aidx", [128, c.NTT * 2 * c.DC], mybir.dt.int32,
                          kind="ExternalInput")
    out_t = nc.dram_tensor("out", [c.T, c.D], F32, kind="ExternalOutput")

    qkvc = nc.dram_tensor("qkvc", [2 * 3 * c.D, c.CH], BF16)
    qkva = nc.dram_tensor("qkva", [2 * c.GPC * 3 * c.D, c.CH], BF16)
    attnc = nc.dram_tensor("attnc", [c.ABLK, c.S], BF16)
    attnm = nc.dram_tensor("attnm", [c.GPC * c.ABLK, c.S], BF16)
    x2sp = nc.dram_tensor("x2sp", [c.T, c.D], F32)
    wbfd = nc.dram_tensor("wbfd", [2 * c.D, c.D], BF16)
    x2fm = nc.dram_tensor("x2fm", [c.D, c.T], BF16)
    scalc = nc.dram_tensor("scalc", [2, 1], F32)
    scala = nc.dram_tensor("scala", [2 * c.NCORES, 1], F32)

    p = Prog(nc)

    def S_(name, shape, dt):
        return nc.alloc_sbuf_tensor(name, shape, dt)

    ident_b = S_("ident_b", [128, 128], BF16)
    ident_f = S_("ident_f", [128, 128], F32)
    ones_f = S_("ones_f", [1, 128], F32)
    pvec = S_("pvec", [32, 128], F32)
    wq_s = S_("wq_s", [128, c.DC], F32)
    bq_s = S_("bq_s", [128, c.DC], F32)
    wk_s = S_("wk_s", [128, c.DC], F32)
    bk_s = S_("bk_s", [128, c.DC], F32)
    bcol = S_("bcol", [128, c.DC], F32)
    stat = S_("stat", [128, 16], F32)
    accs = S_("accs", [128, 8], F32)
    accq = S_("accq", [128, 8], F32)
    grec = S_("grec", [128, 16], F32)
    gsc = S_("gsc", [16, 20], F32)
    gbc = S_("gbc", [128, 2], F32)
    cst = S_("cst", [128, 4], F32)
    gix = S_("gix", [128, 6 * c.GPC * c.HPC], mybir.dt.int32)
    aix = S_("aix", [128, c.NTT * 2 * c.DC], mybir.dt.int32)

    es1 = ExitStack()

    def S1(name, shape, dt):
        return es1.enter_context(nc.sbuf_tensor(name, shape, dt))

    x_fm = S1("x_fm", [128, c.DC * c.T], BF16)
    qk_tm = S1("qk_tm", [128, c.NTC * c.D], BF16)
    sqd1 = S1("sqd", [128, c.D], F32)
    wsl = [S1(f"wsl{i}", [128, c.D], BF16) for i in range(3)]
    xst = [S1("xst0", [128, c.D], F32)]
    xbt = [S1("xbt0", [128, c.D], BF16)]
    est = [S1(f"est{i}", [128, 512], BF16) for i in range(2)]
    fmst = [S1(f"fmst{i}", [128, max(c.T, 512)], BF16) for i in range(2)]

    ps = {}
    for i in range(6):
        ps[f"f{i}"] = nc.alloc_psum_tensor(f"psf{i}", [128, 512], F32)
    for i in range(2):
        ps[f"b{i}"] = nc.alloc_psum_tensor(f"psb{i}", [128, 1024], BF16)

    def tk(*a):
        return a

    E = p.emit

    E('POOL', lambda e: e.memset(ident_b[:, :], 0.0), writes=[tk('ident_b')])
    E('POOL', lambda e: e.affine_select(
        out=ident_b[:, :], in_=ident_b[:, :], compare_op=ALU.not_equal,
        fill=1.0, base=0, pattern=[[-1, 128]], channel_multiplier=1),
      reads=[tk('ident_b')], writes=[tk('ident_b')])
    E('POOL', lambda e: e.memset(ident_f[:, :], 0.0), writes=[tk('ident_f')])
    E('POOL', lambda e: e.affine_select(
        out=ident_f[:, :], in_=ident_f[:, :], compare_op=ALU.not_equal,
        fill=1.0, base=0, pattern=[[-1, 128]], channel_multiplier=1),
      reads=[tk('ident_f')], writes=[tk('ident_f')])
    E('POOL', lambda e: e.memset(ones_f[:, :], 1.0), writes=[tk('ones_f')])
    E('POOL', lambda e: e.memset(cst[:, 0:1], 0.0), writes=[tk('cst')])
    E('POOL', lambda e: e.memset(cst[:, 1:2], LN_EPS), writes=[tk('cst')])
    E('POOL', lambda e: e.memset(cst[:, 2:3], 1e-9), writes=[tk('cst')])

    E('PLD', lambda e: e.dma_start(out=gix[:, :], in_=gidx[:, :]),
      writes=[tk('gix')])
    E('PLD', lambda e: e.dma_start(out=aix[:, :], in_=aidx[:, :]),
      writes=[tk('aix')])

    def load_param(name, dram, dest, scale=None):
        E('SPD', lambda e, d=dram: e.dma_start(out=pvec[0:c.DC, :], in_=d[:, :]),
          writes=[tk('pvec')])
        E('PE', lambda e: e.transpose(ps['f0'][:, 0:c.DC], pvec[0:c.DC, :],
                                      ident_f[0:c.DC, 0:c.DC]),
          reads=[tk('pvec'), tk('ident_f')], writes=[tk('pmm', 0)])
        if scale is None:
            E('DVE', lambda e, d2=dest: e.tensor_copy(d2[:, :],
                                                      ps['f0'][:, 0:c.DC]),
              reads=[tk('pmm', 0)], writes=[tk(name)])
        else:
            E('DVE', lambda e, d2=dest, sc=scale: e.tensor_scalar(
                out=d2[:, :], in0=ps['f0'][:, 0:c.DC], scalar1=sc,
                scalar2=None, op0=ALU.mult),
              reads=[tk('pmm', 0)], writes=[tk(name)])

    att_scale = 1.0 / math.sqrt(c.HD)
    load_param('wq_s', qlnw, wq_s, att_scale)
    load_param('bq_s', qlnb, bq_s, att_scale)
    load_param('wk_s', klnw, wk_s)
    load_param('bk_s', klnb, bk_s)
    load_param('bcol', fbias, bcol)


    xfm3 = x_fm.ap().rearrange("p (do t) -> p do t", do=c.DC)
    NXQ = max(1, c.DC // 8)
    XQW = c.D // NXQ
    for j in range(c.NTC):
        xs, xb = xst[0], xbt[0]
        for q in range(NXQ):
            E('SPD', lambda e, j=j, q=q, xs=xs: e.dma_start(
                out=xs[:, q * XQW:(q + 1) * XQW],
                in_=x_in[j * 128:(j + 1) * 128, q * XQW:(q + 1) * XQW]),
              writes=[tk('xst', q)])
            E('POOL', lambda e, q=q, xs=xs, xb=xb: e.tensor_copy(
                xb[:, q * XQW:(q + 1) * XQW], xs[:, q * XQW:(q + 1) * XQW]),
              reads=[tk('xst', q)], writes=[tk('xbt', q)])
        for dg in range(c.DC // 4):
            tb = ps[f"b{dg % 2}"]
            for dd in range(4):
                do = dg * 4 + dd
                E('PE', lambda e, xb=xb, do=do, dd=dd, tb=tb: e.transpose(
                    tb[:, dd * 128:(dd + 1) * 128],
                    xb[:, do * 128:(do + 1) * 128], ident_b[:, :]),
                  reads=[tk('xbt', do * 128 // XQW), tk('ident_b')],
                  writes=[tk('psb', dg % 2)])
            src3 = tb.ap()[:, 0:512].rearrange("p (g t) -> p g t", g=4)
            dst3 = xfm3[:, dg * 4:dg * 4 + 4, j * 128:(j + 1) * 128]
            E('DVE', lambda e, s=src3, d=dst3: e.tensor_copy(d, s),
              reads=[tk('psb', dg % 2)], writes=[tk('x_fm', dg, j)])

    def vrow0(e_in):
        return 2 * c.D + e_in * 128

    def qkrow0(part, e_in):
        return part * c.D + e_in * 128

    groups = [[b_ * c.GPC + r for r in range(c.GPC)] for b_ in range(c.B)]

    def emit_gather_half(part, hf):
        toks = [tk('qkvc', part, e2, hf) for e2 in range(c.DC)]
        E('CC', lambda e, hf=hf, part=part: e.collective_compute(
            "AllGather", ALU.bypass, replica_groups=groups,
            ins=[qkvc[hf * 3 * c.D + part * c.D:
                      hf * 3 * c.D + (part + 1) * c.D, :].opt()],
            outs=[qkva[(hf * 3 + part) * c.GPC * c.D:
                       (hf * 3 + part + 1) * c.GPC * c.D, :].opt()]),
          reads=toks, writes=[tk('qkva', part, hf)])

    def emit_gather(part):
        for hf in (0, 1):
            emit_gather_half(part, hf)

    qktm3 = qk_tm.ap().rearrange("p (j d) -> p j d", j=c.NTC)
    pend1 = [None]
    for ec in range(c.QKVC):
        pidx, e_in = ec // c.DC, ec % c.DC
        part = (2, 0, 1)[pidx]
        wb = wsl[ec % 3]
        E('PLD', lambda e, ec=ec, wb=wb: e.dma_start(
            out=wb[:, :], in_=wqkv[ec * 128:(ec + 1) * 128, :]),
          writes=[tk('wsl', ec % 3)])
        for tt in range(c.NTT):
            pm = ps[f"f{tt % 2}"]
            for do in range(c.DC):
                E('PE', lambda e, wb=wb, do=do, tt=tt, pm=pm: e.matmul(
                    pm[:, 0:c.TT], wb[:, do * 128:(do + 1) * 128],
                    x_fm[:, do * c.T + tt * c.TT: do * c.T + (tt + 1) * c.TT],
                    start=(do == 0), stop=(do == c.DC - 1)),
                  reads=[tk('wsl', ec % 3)] + [tk('x_fm', do // 4, jj)
                         for jj in range(tt * c.JT, (tt + 1) * c.JT)],
                  writes=[tk('pmm', tt % 2)])
            est_b = est[tt % 2]
            if pend1[0] is not None:
                pend1[0]()
                pend1[0] = None
            E('DVE', lambda e, pm=pm, est_b=est_b: e.tensor_scalar(
                out=est_b[:, 0:c.TT], in0=pm[:, 0:c.TT], scalar1=CLIP,
                scalar2=-CLIP, op0=ALU.min, op1=ALU.max),
              reads=[tk('pmm', tt % 2)], writes=[tk('est', tt % 2)])
            if part < 2:
                def _p1t(est_b=est_b, tt=tt, e_in=e_in):
                    tb = ps[f"b{tt % 2}"]
                    for jj in range(c.JT):
                        E('PE', lambda e, est_b=est_b, jj=jj, tb=tb:
                          e.transpose(
                            tb[:, jj * 128:(jj + 1) * 128],
                            est_b[:, jj * 128:(jj + 1) * 128],
                            ident_b[:, :]),
                          reads=[tk('est', tt % 2), tk('ident_b')],
                          writes=[tk('psb', tt % 2)])
                    src3 = tb.ap()[:, 0:c.JT * 128].rearrange(
                        "p (g t) -> p g t", g=c.JT)
                    dst3 = qktm3[:, tt * c.JT:(tt + 1) * c.JT,
                                 e_in * 128:e_in * 128 + 128]
                    E('DVE', lambda e, s=src3, d=dst3: e.tensor_copy(d, s),
                      reads=[tk('psb', tt % 2)],
                      writes=[tk('qktm', e_in, tt)])
                pend1[0] = _p1t
            else:
                r0 = vrow0(e_in)
                lo_l, hi_l = tt * c.TT, (tt + 1) * c.TT
                for hf in (0, 1):
                    aa = max(lo_l, hf * c.CH)
                    bb = min(hi_l, (hf + 1) * c.CH)
                    if aa >= bb:
                        continue
                    E('SPD', lambda e, est_b=est_b, r0=r0, hf=hf, aa=aa,
                      bb=bb, lo_l=lo_l: e.dma_start(
                        out=qkvc[hf * 3 * c.D + r0:hf * 3 * c.D + r0 + 128,
                                 aa - hf * c.CH:bb - hf * c.CH],
                        in_=est_b[:, aa - lo_l:bb - lo_l]),
                      reads=[tk('est', tt % 2)],
                      writes=[tk('qkvc', 2, e_in, hf)])
        if part == 2 and e_in == c.DC - 1:
            emit_gather(2)
        if part < 2 and e_in == c.DC - 1:
            if pend1[0] is not None:
                pend1[0]()
                pend1[0] = None
            wsv, bsv = (wq_s, bq_s) if part == 0 else (wk_s, bk_s)
            NHV = 2 if c.NTC >= 2 else 1
            NJ = c.NTC // NHV
            for jgh in range(NHV):
                for j in range(jgh * NJ, (jgh + 1) * NJ):
                    tile = qk_tm[:, j * c.D:(j + 1) * c.D]
                    rtoks = [tk('qktm', e2, j // c.JT) for e2 in range(c.DC)]
                    E('DVE', lambda e, tile=tile: e.reduce_sum(
                        stat[:, 0:1], tile, axis=AX.X, negate=True),
                      reads=rtoks, writes=[tk('stat', 0)])
                    E('ACT', lambda e: e.activation(
                        out=stat[:, 1:2], in_=stat[:, 0:1], func=AF.Copy,
                        scale=1.0 / c.D),
                      reads=[tk('stat', 0)], writes=[tk('stat', 1)])
                    E('ACT', lambda e, tile=tile: e.activation(
                        out=sqd1[:, 0:c.D], in_=tile, func=AF.Square,
                        bias=stat[:, 1:2], accum_out=stat[:, 2:3]),
                      reads=rtoks + [tk('stat', 1)],
                      writes=[tk('sqd'), tk('stat', 2)])
                    E('ACT', lambda e: e.activation(
                        out=stat[:, 3:4], in_=stat[:, 2:3], func=AF.Ln,
                        scale=1.0 / c.D, bias=cst[:, 1:2]),
                      reads=[tk('stat', 2), tk('cst')],
                      writes=[tk('stat', 3)])
                    E('ACT', lambda e: e.activation(
                        out=stat[:, 4:5], in_=stat[:, 3:4], func=AF.Exp,
                        scale=-0.5, bias=cst[:, 0:1]),
                      reads=[tk('stat', 3), tk('cst')],
                      writes=[tk('stat', 4)])
                    E('DVE', lambda e, tile=tile: e.tensor_scalar(
                        out=tile, in0=tile, scalar1=stat[:, 1:2],
                        scalar2=stat[:, 4:5], op0=ALU.add, op1=ALU.mult),
                      reads=rtoks + [tk('stat', 1), tk('stat', 4)],
                      writes=rtoks)
                for ein2 in range(c.DC):
                    fst = fmst[ein2 % 2]
                    for jg4 in range(max(1, NJ // 4)):
                        G4 = min(4, NJ)
                        tb = ps[f"b{jg4 % 2}"]
                        for jj in range(G4):
                            j = jgh * NJ + jg4 * 4 + jj
                            E('PE', lambda e, j=j, jj=jj, ein2=ein2, tb=tb:
                              e.transpose(
                                  tb[:, jj * 128:(jj + 1) * 128],
                                  qk_tm[:, j * c.D + ein2 * 128:
                                        j * c.D + ein2 * 128 + 128],
                                  ident_b[:, :]),
                              reads=[tk('qktm', ein2, j // c.JT),
                                     tk('ident_b')],
                              writes=[tk('psb', jg4 % 2)])
                        c0 = (jgh * NJ + jg4 * 4) * 128
                        E('DVE', lambda e, tb=tb, c0=c0, G4=G4, fst=fst,
                          wsv=wsv, bsv=bsv, ein2=ein2: e.tensor_scalar(
                              out=fst[:, c0:c0 + G4 * 128],
                              in0=tb[:, 0:G4 * 128],
                              scalar1=wsv[:, ein2:ein2 + 1],
                              scalar2=bsv[:, ein2:ein2 + 1],
                              op0=ALU.mult, op1=ALU.add),
                          reads=[tk('psb', jg4 % 2), tk(wsv.name),
                                 tk(bsv.name)],
                          writes=[tk('fmst', ein2 % 2, jgh, jg4)])
                    r0 = qkrow0(part, ein2)
                    hfs = (jgh,) if NHV == 2 else (0, 1)
                    for hf in hfs:
                        E('SPD', lambda e, fst=fst, r0=r0, hf=hf:
                          e.dma_start(
                            out=qkvc[hf * 3 * c.D + r0:
                                     hf * 3 * c.D + r0 + 128, 0:c.CH],
                            in_=fst[:, hf * c.CH:(hf + 1) * c.CH]),
                          reads=[tk('fmst', ein2 % 2, jgh, jg4b)
                                 for jg4b in range(max(1, NJ // 4))],
                          writes=[tk('qkvc', part, ein2, hf)])
                if NHV == 2:
                    emit_gather_half(part, jgh)
            if NHV == 1:
                emit_gather(part)

    es1.close()
    p.barrier(exclude=('CC',))

    es3 = ExitStack()

    def S3(name, shape, dt):
        return es3.enter_context(nc.sbuf_tensor(name, shape, dt))

    qsbs = [S3(f"qsb{k}", [128, c.S], BF16) for k in range(2)]
    ksbs = [S3(f"ksb{k}", [128, c.S], BF16) for k in range(2)]
    vfsbs = [S3(f"vfsb{k}", [128, c.S], BF16) for k in range(2)]
    vtsbs = [S3(f"vtsb{k}", [128, c.S], BF16) for k in range(2)]
    atsb = [S3(f"atsb{i}", [128, c.S], BF16) for i in range(2)]
    skh = [S3(f"skh{i}", [128, c.S], BF16) for i in range(2)]
    prbT = [S3(f"prbt{t}", [128, 512], BF16) for t in range(3)]
    zrec = [S3(f"zrec{i}", [128, 512], F32) for i in range(2)]
    alc = S3("alc", [128, (c.S // 128) * c.HPC], F32)
    ones3 = S3("ones3", [128, 128], BF16)
    msk = S3("msk", [128, 1024], BF16)
    NKB = c.S // 128
    E('POOL', lambda e: e.memset(ones3[:, :], 1.0), writes=[tk('ones3')])
    for _i in range(2):
        E('POOL', lambda e, _i=_i: e.memset(skh[_i][:, :], 0.0),
          writes=[tk('skh', _i)])
    E('POOL', lambda e: e.memset(msk[:, :], 0.0), writes=[tk('msk')])
    E('POOL', lambda e: e.affine_select(
        out=msk[:, :], in_=msk[:, :], compare_op=ALU.is_ge,
        fill=NEG_BIG, base=-384, pattern=[[1, 1024]], channel_multiplier=-1),
      reads=[tk('msk')], writes=[tk('msk')])
    E('PLD', lambda e: e.dma_start(out=alc[:, :], in_=alcol[:, :]),
      writes=[tk('alc')])

    def chunk_owner(g):
        r = g if g < c.GPC else 2 * c.GPC - 1 - g
        lo = 0 if g < c.GPC else c.CH
        slot = 0 if g < c.GPC else 1
        return r, lo, slot

    KVS = 2 if c.NQG % 2 == 0 else 1
    GH = 2 * c.GPC // KVS

    def emit_loads(i, hf):
        hp = i % 2
        qsb, ksb, vfsb = qsbs[hp], ksbs[hp], vfsbs[hp]
        for g in range(0 if hf else 0, (hf + 1) * GH):
            parts = ((0, qsb, ('qsb', hp)), (2, vfsb, ('vfsb', hp)),
                     (1, ksb, ('ksb', hp)))
            if hf == 1 and g < GH:
                parts = ((2, vfsb, ('vfsb', hp)), (1, ksb, ('ksb', hp)))
            for part, dest, dtok in parts:
                kcol = (i * 3 + part) * 2 * c.GPC + g
                E('PLD', lambda e, dest=dest, g=g, kcol=kcol:
                  e.indirect_dma_start(
                      out=dest[:, g * c.CH:(g + 1) * c.CH], out_offset=None,
                      in_=qkva.ap(),
                      in_offset=IndirectOffsetOnAxis(gix[:, kcol:kcol + 1], 0),
                      element_offset=0),
                  reads=[tk('qkva', part, g // c.GPC), tk('gix')],
                  writes=[dtok + (g,)])

    def emit_vt(i, hf):
        hp = i % 2
        vfsb, vtsb = vfsbs[hp], vtsbs[hp]
        vread = [tk('vfsb', hp, g) for g in range(0, (hf + 1) * GH)]
        w0 = 0
        for cg in range((hf + 1) * (c.S // KVS) // 512):
            tb = ps[f"b{cg % 2}"]
            for cc in range(4):
                ch_i = (w0 // 128) + cg * 4 + cc
                E('PE', lambda e, ch_i=ch_i, cc=cc, tb=tb, vfsb=vfsb:
                  e.transpose(
                    tb[:, cc * 128:(cc + 1) * 128],
                    vfsb[:, ch_i * 128:(ch_i + 1) * 128], ident_b[:, :]),
                  reads=vread + [tk('ident_b')], writes=[tk('psb', cg % 2)])
            E('DVE', lambda e, cg=cg, tb=tb, vtsb=vtsb, w0=w0: e.tensor_copy(
                vtsb[:, w0 + cg * 512:w0 + (cg + 1) * 512], tb[:, 0:512]),
              reads=[tk('psb', cg % 2)],
              writes=[tk('vtsb', hp, (w0 // 512) + cg)])

    def emit_dh(i):
        hq = i % 2
        E('PLD', lambda e, i=i, hq=hq: e.dma_start(
            out=skh[hq][0:3, :], in_=sk3[3 * i:3 * i + 3, :]),
          writes=[tk('skh', hq)])

    def emit_attn(i, gi):
        hp = i % 2
        hq = i % 2
        qsb, ksb, vtsb = qsbs[hp], ksbs[hp], vtsbs[hp]
        B0 = gi * 4
        ntl = B0 + 4
        q0 = gi * 512
        qread = [tk('qsb', hp, g)
                 for g in range(q0 // c.CH,
                                min(2 * c.GPC, (q0 + 512 + c.CH - 1) // c.CH))]
        scb = [ps['f0'], ps['f1'], ps['f3']]
        pav = ps['f2'] if gi % 2 == 0 else ps['f5']
        zr = zrec[gi % 2]
        pend = [None]
        for t in range(ntl):
            pm = scb[t % 3]
            d = t - B0
            kread = [tk('ksb', hp, g)
                     for g in range(t * 128 // c.CH,
                                    min(2 * c.GPC,
                                        ((t + 1) * 128 + c.CH - 1) // c.CH))]
            ptok = tk('psc', t % 3)
            if d >= 0:
                off = 384 - 128 * d
                E('PE', lambda e, pm=pm, off=off: e.matmul(
                    pm[:, 0:512], ident_b[:, :], msk[:, off:off + 512],
                    start=True, stop=False),
                  reads=[tk('ident_b'), tk('msk')], writes=[ptok])
            E('PE', lambda e, pm=pm, hq=hq, q0=q0, d=d: e.matmul(
                pm[:, 0:512], ones3[:, :], skh[hq][:, q0:q0 + 512],
                start=(d < 0), stop=False),
              reads=[tk('ones3'), tk('skh', hq)], writes=[ptok])
            E('PE', lambda e, pm=pm, t=t, q0=q0, qsb=qsb, ksb=ksb: e.matmul(
                pm[:, 0:512], ksb[:, t * 128:(t + 1) * 128],
                qsb[:, q0:q0 + 512], start=False, stop=True),
              reads=kread + qread, writes=[ptok])
            if pend[0] is not None:
                pend[0]()
                pend[0] = None
            pb = prbT[t % 3]
            E('ACT', lambda e, pm=pm, pb=pb, i=i, t=t: e.activation(
                out=pb[:, :], in_=pm[:, 0:512], func=AF.Exp,
                bias=alc[:, i * NKB + t:i * NKB + t + 1]),
              reads=[ptok, tk('alc')], writes=[tk('prbt', t % 3)])

            def _zav(t=t, pb=pb):
                fs = (t == 0)
                fe = (t == ntl - 1)
                E('PE', lambda e, pb=pb, fs=fs, fe=fe: e.matmul(
                    ps['f4'][:, 0:512], ones3[:, :], pb[:, :],
                    start=fs, stop=fe),
                  reads=[tk('ones3'), tk('prbt', t % 3)], writes=[tk('pz')])
                E('PE', lambda e, pb=pb, t=t, fs=fs, fe=fe, pav=pav,
                  vtsb=vtsb: e.matmul(
                    pav[:, 0:512], vtsb[:, t * 128:(t + 1) * 128], pb[:, :],
                    start=fs, stop=fe),
                  reads=[tk('vtsb', hp, t // 4), tk('prbt', t % 3)],
                  writes=[tk('pav', gi % 2)])
            pend[0] = _zav
        pend[0]()
        E('DVE', lambda e, zr=zr: e.reciprocal(zr[:, :], ps['f4'][:, 0:512]),
          reads=[tk('pz')], writes=[tk('zrec', gi % 2)])
        E('DVE', lambda e, i=i, q0=q0, pav=pav, zr=zr: e.tensor_tensor(
            atsb[i % 2][:, q0:q0 + 512], pav[:, 0:512], zr[:, :],
            op=ALU.mult),
          reads=[tk('pav', gi % 2), tk('zrec', gi % 2)],
          writes=[tk('atsb', i % 2, gi)])

    def emit_attn_dma(i, hf):
        w0 = hf * (c.S // KVS)
        wid = c.S // KVS
        E('SPD', lambda e, i=i, w0=w0, wid=wid: e.dma_start(
            out=attnc[i * c.HD:(i + 1) * c.HD, w0:w0 + wid],
            in_=atsb[i % 2][:, w0:w0 + wid]),
          reads=[tk('atsb', i % 2, gi2)
                 for gi2 in range(hf * c.NQG // KVS,
                                  (hf + 1) * c.NQG // KVS)],
          writes=[tk('attnc', i, hf)])

    wcast_jobs = ([(wout, 0, ec) for ec in range(c.DC)]
                  + [(wffn, 1, ec) for ec in range(c.DC)])

    def emit_wcast(batch_idx, nbatches):
        n = len(wcast_jobs)
        lo = batch_idx * n // nbatches
        hi = (batch_idx + 1) * n // nbatches
        for (wd, half, ec) in wcast_jobs[lo:hi]:
            E('PLD', lambda e, wd=wd, half=half, ec=ec: e.dma_start(
                out=wbfd[half * c.D + ec * 128:half * c.D + (ec + 1) * 128, :],
                in_=wd[ec * 128:(ec + 1) * 128, :]),
              writes=[tk('wbfd', half, ec)])

    NSPg = 4 if c.HPC >= 4 else (2 if c.HPC >= 2 else 1)
    HHg = c.HPC // NSPg

    def emit_attn_gather(hg):
        E('CC', lambda e, hg=hg: e.collective_compute(
            "AllGather", ALU.bypass, replica_groups=groups,
            ins=[attnc[hg * HHg * c.HD:(hg + 1) * HHg * c.HD, :].opt()],
            outs=[attnm[hg * c.GPC * HHg * c.HD:
                        (hg + 1) * c.GPC * HHg * c.HD, :].opt()]),
          reads=[tk('attnc', i, hf2) for i in range(hg * HHg, (hg + 1) * HHg)
                 for hf2 in range(KVS)],
          writes=[tk('attnm', hg)])

    NQH = c.NQG // KVS
    seq = [(hf, i, hf * NQH + gl) for hf in range(KVS)
           for i in range(c.HPC) for gl in range(NQH)]
    emit_loads(0, 0)
    emit_vt(0, 0)
    emit_dh(0)
    NWB = KVS * c.HPC
    wb_done = [0]
    for k in range(len(seq)):
        hf, i, gi = seq[k]
        first_g = (gi == hf * NQH)
        last_g = (gi == (hf + 1) * NQH - 1)
        if first_g:
            nhf, ni = (hf, i + 1) if i + 1 < c.HPC else (hf + 1, 0)
            if nhf < KVS:
                emit_loads(ni, nhf)
            emit_wcast(wb_done[0], NWB)
            wb_done[0] += 1
        emit_attn(i, gi)
        if last_g:
            nhf, ni = (hf, i + 1) if i + 1 < c.HPC else (hf + 1, 0)
            if nhf < KVS:
                emit_vt(ni, nhf)
                emit_dh(ni)
            emit_attn_dma(i, hf)
            if hf == KVS - 1 and (i + 1) % HHg == 0:
                hg2 = (i + 1) // HHg - 1
                emit_attn_gather(hg2)

    es3.close()
    p.barrier(exclude=('CC',))

    es5 = ExitStack()

    def S5(name, shape, dt):
        return es5.enter_context(nc.sbuf_tensor(name, shape, dt))

    arhs = S5("arhs", [128, c.DC * ARS], BF16)
    abuf = S5("abuf", [128, c.DC * c.R], BF16)
    bbuf = S5("bbuf", [c.R, c.D], BF16)
    w8buf = S5("w8buf", [128, c.DC * c.E], BF16)
    limb = S5("limb", [c.E, c.R], BF16)
    limtb = S5("limtb", [c.R, c.E], BF16)
    wsl = [S5(f"wsl5{i}", [128, c.D], BF16) for i in range(3)]
    fmst5 = [S5(f"fmst5{i}", [128, 512], BF16) for i in range(2)]
    wgrps = [S5(f"wgrp{i}", [128, 512], F32) for i in range(2)]
    E('PLD', lambda e: e.dma_start(out=abuf[:, :], in_=amat[:, :]),
      writes=[tk('abuf')])
    E('PLD', lambda e: e.dma_start(out=bbuf[:, :], in_=bmat[:, :]),
      writes=[tk('bbuf')])
    E('PLD', lambda e: e.dma_start(out=w8buf[:, :], in_=w8[:, :]),
      writes=[tk('w8buf')])
    E('PLD', lambda e: e.dma_start(out=limb[:, :], in_=limes[:, :]),
      writes=[tk('limb')])
    E('PLD', lambda e: e.dma_start(out=limtb[:, :], in_=limest[:, :]),
      writes=[tk('limtb')])
    xres = [S5(f"xres{i}", [128, 512], F32) for i in range(4)]
    s1sb = [S5(f"s1sb{i}", [128, c.D], F32) for i in range(c.JT)]
    x2bt = [S5(f"x2bt{i}", [128, 512], BF16) for i in range(2)]
    rdt = [S5(f"rdt{i}", [128, 512], F32) for i in range(2)]
    zfm = S5("zfm", [c.R, c.T], F32)
    zab = S5("zab", [c.R, c.ANCH], BF16)
    h8fm = S5("h8fm", [c.E, c.T], F32)
    h8ab = S5("h8ab", [c.E, c.ANCH], F32)
    dsfm = S5("dsfm", [c.E, c.ANCH], F32)
    gtm = S5("gtm", [128, 32], F32)
    wffull = S5("wffull", [c.E, c.T], BF16)
    pmfm = S5("pmfm", [c.R, c.T], F32)
    zpbf = S5("zpbf", [c.R, c.T], BF16)

    attnm3 = attnm.ap().rearrange("(do p) t -> p do t", p=128)

    attnm_flat = attnm.ap().rearrange("r (t one) -> (r t) one", one=1)

    def hg_of(do):
        return (do % c.HPC) // HHg if NSPg > 1 else 0

    def arhs_load(tt, src3, dyncols, rtoks):
        lo_l, hi_l = tt * c.TT, (tt + 1) * c.TT
        for h in (0, 1):
            a = max(lo_l, h * c.CH)
            b = min(hi_l, (h + 1) * c.CH)
            if a >= b:
                continue
            if dyncols:
                for do in sorted(range(c.DC), key=lambda d2: (hg_of(d2), d2)):
                    col = (tt * 2 + h) * c.DC + do
                    E('PLD', lambda e, do=do, col=col, a=a, b=b, lo_l=lo_l:
                      e.indirect_dma_start(
                          out=arhs[:, do * ARS + (a - lo_l):
                                   do * ARS + (b - lo_l)],
                          out_offset=None, in_=attnm_flat,
                          in_offset=IndirectOffsetOnAxis(
                              aix[:, col:col + 1], 0),
                          element_offset=0),
                      reads=[tk('attnm', hg_of(do)), tk('aix')],
                      writes=[tk('arhs', do)])
            else:
                sp = src3[:, :, a:b]
                dp = arhs3[:, :, a - lo_l:b - lo_l]
                E('ACTD', lambda e, sp=sp, dp=dp: e.dma_start(out=dp, in_=sp),
                  reads=rtoks,
                  writes=[tk('arhs', do) for do in range(c.DC)])
    arhs3 = arhs.ap().rearrange("p (do t) -> p do t", do=c.DC)
    x2fm3 = x2fm.ap().rearrange("(do p) t -> p do t", p=128)

    def stream_proj(tt, whalf, src_resid, resid_tok, phase,
                    extra_mm=None, bias_col=None):
        JT = c.JT
        E('DVE', lambda e: e.memset(accs[:, 0:JT], 0.0), writes=[tk('accs')])
        E('DVE', lambda e: e.memset(accq[:, 0:JT], 0.0), writes=[tk('accq')])
        pend = [None]
        for ec in range(c.DC):
            wb = wsl[ec % 3]
            E('PLD', lambda e, ec=ec, wb=wb, whalf=whalf: e.dma_start(
                out=wb[:, :],
                in_=wbfd[whalf * c.D + ec * 128:
                         whalf * c.D + (ec + 1) * 128, :]),
              reads=[tk('wbfd', whalf, ec)],
              writes=[tk('wsl', ec % 3)])
            if ec % 4 == 0:
                for jj in range(JT):
                    rt = ([resid_tok(tt, jj, ec // 4)]
                          if resid_tok is not None else [])
                    E('SPD', lambda e, jj=jj, ec=ec, tt=tt,
                      src_resid=src_resid: e.dma_start(
                          out=xres[jj][:, :],
                          in_=src_resid[tt * c.TT + jj * 128:
                                        tt * c.TT + (jj + 1) * 128,
                                        (ec // 4) * 512:(ec // 4 + 1) * 512]),
                      reads=rt, writes=[tk('xres', jj)])
            pm = ps[f"f{ec % 2}"]
            dorder = (sorted(range(c.DC), key=lambda do: (hg_of(do), do))
                      if phase == 5 else list(range(c.DC)))
            for di, do in enumerate(dorder):
                E('PE', lambda e, wb=wb, do=do, pm=pm, di=di: e.matmul(
                    pm[:, 0:c.TT], wb[:, do * 128:(do + 1) * 128],
                    arhs[:, do * ARS:do * ARS + c.TT],
                    start=(di == 0),
                    stop=(di == c.DC - 1 and extra_mm is None)),
                  reads=[tk('wsl', ec % 3), tk('arhs', do)],
                  writes=[tk('pmm', ec % 2)])
            if extra_mm is not None:
                extra_mm(tt, pm, ec)
            wgrp = wgrps[ec % 2]
            if pend[0] is not None:
                pend[0]()
                pend[0] = None
            if bias_col is None:
                E('DVE', lambda e, pm=pm, wgrp=wgrp: e.tensor_copy(
                    wgrp[:, 0:c.TT], pm[:, 0:c.TT]),
                  reads=[tk('pmm', ec % 2)], writes=[tk('wgrp', ec % 2)])
            else:
                E('DVE', lambda e, pm=pm, ec=ec, wgrp=wgrp: e.tensor_scalar(
                    out=wgrp[:, 0:c.TT], in0=pm[:, 0:c.TT],
                    scalar1=bias_col[:, ec:ec + 1], scalar2=None,
                    op0=ALU.add),
                  reads=[tk('pmm', ec % 2), tk('bcol')],
                  writes=[tk('wgrp', ec % 2)])

            def _transposes(ec=ec, wgrp=wgrp):
                for jj in range(JT):
                    pt = ps[f"f{2 + jj}"]
                    E('PE', lambda e, jj=jj, ec=ec, pt=pt, wgrp=wgrp:
                      e.transpose(
                        pt[:, (ec % 4) * 128:(ec % 4 + 1) * 128],
                        wgrp[:, jj * 128:(jj + 1) * 128], ident_f[:, :]),
                      reads=[tk('wgrp', ec % 2), tk('ident_f')],
                      writes=[tk('ptr', jj)])
            pend[0] = _transposes
            if ec % 4 == 3:
                pend[0]()
                pend[0] = None
            if ec % 4 == 3:
                for jj in range(JT):
                    pt = ps[f"f{2 + jj}"]
                    eg0 = ec // 4
                    E('DVE', lambda e, pt=pt, jj=jj, eg0=eg0:
                      e.scalar_tensor_tensor(
                          out=s1sb[jj][:, eg0 * 512:(eg0 + 1) * 512],
                          in0=pt[:, 0:512], scalar=1.0,
                          in1=xres[jj][:, :], op0=ALU.mult, op1=ALU.add,
                          accum_out=stat[:, 0:1]),
                      reads=[tk('ptr', jj), tk('xres', jj)],
                      writes=[tk('s1sb', jj, eg0), tk('stat', 0)])
                    E('DVE', lambda e, jj=jj: e.tensor_tensor(
                        accs[:, jj:jj + 1], accs[:, jj:jj + 1],
                        stat[:, 0:1], ALU.add),
                      reads=[tk('accs'), tk('stat', 0)], writes=[tk('accs')])
                    E('ACT', lambda e, jj=jj, eg0=eg0: e.activation(
                        out=rdt[0][:, 0:512],
                        in_=s1sb[jj][:, eg0 * 512:(eg0 + 1) * 512],
                        func=AF.Square, bias=cst[:, 0:1],
                        accum_out=stat[:, 1:2]),
                      reads=[tk('s1sb', jj, eg0), tk('cst')],
                      writes=[tk('rdt', 0), tk('stat', 1)])
                    E('DVE', lambda e, jj=jj: e.tensor_tensor(
                        accq[:, jj:jj + 1], accq[:, jj:jj + 1],
                        stat[:, 1:2], ALU.add),
                      reads=[tk('accq'), tk('stat', 1)], writes=[tk('accq')])
        for jj in range(JT):
            E('ACT', lambda e, jj=jj: e.activation(
                out=stat[:, 2:3], in_=accs[:, jj:jj + 1], func=AF.Copy,
                scale=-1.0 / c.D),
              reads=[tk('accs')], writes=[tk('stat', 2)])
            E('ACT', lambda e: e.activation(
                out=stat[:, 3:4], in_=stat[:, 2:3], func=AF.Square,
                bias=cst[:, 0:1]),
              reads=[tk('stat', 2), tk('cst')], writes=[tk('stat', 3)])
            E('ACT', lambda e, jj=jj: e.activation(
                out=stat[:, 4:5], in_=accq[:, jj:jj + 1], func=AF.Copy,
                scale=1.0 / c.D),
              reads=[tk('accq')], writes=[tk('stat', 4)])
            E('DVE', lambda e: e.tensor_sub(
                stat[:, 5:6], stat[:, 4:5], stat[:, 3:4]),
              reads=[tk('stat', 3), tk('stat', 4)], writes=[tk('stat', 5)])
            E('ACT', lambda e: e.activation(
                out=stat[:, 6:7], in_=stat[:, 5:6], func=AF.Ln,
                bias=cst[:, 1:2]),
              reads=[tk('stat', 5), tk('cst')], writes=[tk('stat', 6)])
            E('ACT', lambda e: e.activation(
                out=stat[:, 7:8], in_=stat[:, 6:7], func=AF.Exp, scale=-0.5,
                bias=cst[:, 0:1]),
              reads=[tk('stat', 6), tk('cst')], writes=[tk('stat', 7)])
            for eg in range(c.DC // 4):
                rb = rdt[eg % 2]
                E('DVE', lambda e, rb=rb, jj=jj, eg=eg: e.tensor_scalar(
                    out=rb[:, :], in0=s1sb[jj][:, eg * 512:(eg + 1) * 512],
                    scalar1=stat[:, 2:3], scalar2=stat[:, 7:8],
                    op0=ALU.add, op1=ALU.mult),
                  reads=[tk('s1sb', jj, eg), tk('stat', 2), tk('stat', 7)],
                  writes=[tk('rdt', eg % 2)])
                if phase == 5:
                    E('SPD', lambda e, rb=rb, jj=jj, eg=eg, tt=tt:
                      e.dma_start(
                          out=x2sp[tt * c.TT + jj * 128:
                                   tt * c.TT + (jj + 1) * 128,
                                   eg * 512:(eg + 1) * 512],
                          in_=rb[:, :]),
                      reads=[tk('rdt', eg % 2)],
                      writes=[tk('x2sp', tt, jj, eg)])
                    xb2 = x2bt[eg % 2]
                    E('POOL', lambda e, rb=rb, xb2=xb2: e.tensor_copy(
                        xb2[:, :], rb[:, :]),
                      reads=[tk('rdt', eg % 2)], writes=[tk('x2bt', eg % 2)])
                    tb = ps[f"b{eg % 2}"]
                    for dd in range(4):
                        E('PE', lambda e, xb2=xb2, dd=dd, tb=tb: e.transpose(
                            tb[:, dd * 128:(dd + 1) * 128],
                            xb2[:, dd * 128:(dd + 1) * 128], ident_b[:, :]),
                          reads=[tk('x2bt', eg % 2), tk('ident_b')],
                          writes=[tk('psb', eg % 2)])
                    fst = fmst5[eg % 2]
                    E('DVE', lambda e, tb=tb, fst=fst: e.tensor_copy(
                        fst[:, 0:512], tb[:, 0:512]),
                      reads=[tk('psb', eg % 2)], writes=[tk('fmst2', eg % 2)])
                    src3 = fst.ap()[:, 0:512].rearrange(
                        "p (g t) -> p g t", g=4)
                    dst3 = x2fm3[:, eg * 4:eg * 4 + 4,
                                 tt * c.TT + jj * 128:
                                 tt * c.TT + (jj + 1) * 128]
                    E('SPD', lambda e, s=src3, d=dst3: e.dma_start(
                        out=d, in_=s),
                      reads=[tk('fmst2', eg % 2)],
                      writes=[tk('x2fm', tt, jj, eg)])
                else:
                    E('SPD', lambda e, rb=rb, jj=jj, eg=eg, tt=tt:
                      e.dma_start(
                          out=out_t[tt * c.TT + jj * 128:
                                    tt * c.TT + (jj + 1) * 128,
                                    eg * 512:(eg + 1) * 512],
                          in_=rb[:, :]),
                      reads=[tk('rdt', eg % 2)],
                      writes=[tk('out', tt, jj, eg)])

    def emit_gating_proj(tt):
        src3 = x2fm3[:, :, tt * c.TT:(tt + 1) * c.TT]
        dst3 = arhs3[:, :, 0:c.TT]
        E('ACTD', lambda e, s=src3, d=dst3: e.dma_start(out=d, in_=s),
          reads=[tk('x2fm', tt, jj, eg) for jj in range(c.JT)
                 for eg in range(c.DC // 4)],
          writes=[tk('arhs', do) for do in range(c.DC)])
        for do in range(c.DC):
            E('PE', lambda e, do=do: e.matmul(
                ps['f0'][0:c.R, 0:c.TT], abuf[:, do * c.R:(do + 1) * c.R],
                arhs[:, do * ARS:do * ARS + c.TT],
                start=(do == 0), stop=(do == c.DC - 1)),
              reads=[tk('abuf'), tk('arhs', do)], writes=[tk('pmm', 0)])
        E('DVE', lambda e, tt=tt: e.tensor_copy(
            zfm[:, tt * c.TT:(tt + 1) * c.TT], ps['f0'][0:c.R, 0:c.TT]),
          reads=[tk('pmm', 0)], writes=[tk('zfm', tt)])
        for do in range(c.DC):
            E('PE', lambda e, do=do: e.matmul(
                ps['f1'][0:c.E, 0:c.TT], w8buf[:, do * c.E:(do + 1) * c.E],
                arhs[:, do * ARS:do * ARS + c.TT],
                start=(do == 0), stop=(do == c.DC - 1)),
              reads=[tk('w8buf'), tk('arhs', do)], writes=[tk('pmm', 1)])
        E('DVE', lambda e, tt=tt: e.tensor_scalar(
            out=h8fm[:, tt * c.TT:(tt + 1) * c.TT],
            in0=ps['f1'][0:c.E, 0:c.TT],
            scalar1=bcol[0:c.E, 0:1], scalar2=None, op0=ALU.add),
          reads=[tk('pmm', 1), tk('bcol')], writes=[tk('h8fm', tt)])

    for tt in range(c.NTT):
        arhs_load(tt, attnm3, True, None)
        stream_proj(tt, 0, x_in, None, 5)
        emit_gating_proj(tt)


    zread = [tk('zfm', tt) for tt in range(c.NTT)]
    hread = [tk('h8fm', tt) for tt in range(c.NTT)]
    z3 = zfm.ap().rearrange("p (t two) -> p t two", two=2)
    h3 = h8fm.ap().rearrange("p (t two) -> p t two", two=2)
    E('DVE', lambda e: e.tensor_copy(zab[:, :], z3[:, :, 1:2]),
      reads=zread, writes=[tk('zab')])
    E('DVE', lambda e: e.tensor_copy(h8ab[:, :], h3[0:c.E, :, 1:2]),
      reads=hread, writes=[tk('h8ab')])
    npt = (c.ANCH + 511) // 512
    for q in range(npt):
        w = min(512, c.ANCH - q * 512)
        E('PE', lambda e, q=q, w=w: e.matmul(
            ps['f0'][0:c.E, 0:w], limtb[:, :],
            zab[:, q * 512:q * 512 + w], start=True, stop=True),
          reads=[tk('zab'), tk('limtb')], writes=[tk('pmm', 0)])
        E('DVE', lambda e, q=q, w=w: e.tensor_copy(
            dsfm[:, q * 512:q * 512 + w], ps['f0'][0:c.E, 0:w]),
          reads=[tk('pmm', 0)], writes=[tk('dsfm', q)])
    dread = [tk('dsfm', q) for q in range(npt)]
    E('DVE', lambda e: e.reduce_max(
        gsc[0:c.E, 0:1], h8ab[:, :], axis=AX.X, apply_absolute_value=True),
      reads=[tk('h8ab')], writes=[tk('gsc', 0)])
    E('DVE', lambda e: e.reduce_max(
        gsc[0:c.E, 1:2], dsfm[:, :], axis=AX.X, apply_absolute_value=True),
      reads=dread, writes=[tk('gsc', 1)])
    E('PE', lambda e: e.transpose(ps['f0'][0:2, 0:c.E], gsc[0:c.E, 0:2],
                                  ident_f[0:c.E, 0:c.E]),
      reads=[tk('gsc', 0), tk('gsc', 1), tk('ident_f')],
      writes=[tk('pmm', 0)])
    E('DVE', lambda e: e.tensor_copy(gsc[0:2, 2:2 + c.E],
                                     ps['f0'][0:2, 0:c.E]),
      reads=[tk('pmm', 0)], writes=[tk('gsc', 2)])
    E('DVE', lambda e: e.reduce_max(gsc[0:2, 10:11], gsc[0:2, 2:2 + c.E],
                                    axis=AX.X),
      reads=[tk('gsc', 2)], writes=[tk('gsc', 10)])
    E('SPD', lambda e: e.dma_start(out=scalc[:, :], in_=gsc[0:2, 10:11]),
      reads=[tk('gsc', 10)], writes=[tk('scalc')])
    E('CC', lambda e: e.collective_compute(
        "AllGather", ALU.bypass, replica_groups=[list(range(c.NCORES))],
        ins=[scalc.ap().opt()], outs=[scala.ap().opt()]),
      reads=[tk('scalc')], writes=[tk('scala')])
    E('SPD', lambda e: e.dma_start(out=gsc[0:2 * c.NCORES, 3:4],
                                   in_=scala[:, :]),
      reads=[tk('scala')], writes=[tk('gsc', 3)])
    E('PE', lambda e: e.transpose(
        ps['f0'][0:1, 0:2 * c.NCORES], gsc[0:2 * c.NCORES, 3:4],
        ident_f[0:2 * c.NCORES, 0:2 * c.NCORES]),
      reads=[tk('gsc', 3), tk('ident_f')], writes=[tk('pmm', 0)])
    E('DVE', lambda e: e.tensor_copy(gsc[0:1, 4:4 + 2 * c.NCORES],
                                     ps['f0'][0:1, 0:2 * c.NCORES]),
      reads=[tk('pmm', 0)], writes=[tk('gsc', 4)])
    g4 = gsc.ap()[0:1, 4:4 + 2 * c.NCORES].rearrange(
        "p (r two) -> p r two", two=2)
    E('DVE', lambda e: e.reduce_max(gsc[0:1, 12:13], g4[:, :, 0:1],
                                    axis=AX.XY),
      reads=[tk('gsc', 4)], writes=[tk('gsc', 12)])
    E('DVE', lambda e: e.reduce_max(gsc[0:1, 13:14], g4[:, :, 1:2],
                                    axis=AX.XY),
      reads=[tk('gsc', 4)], writes=[tk('gsc', 13)])
    for k in (12, 13):
        E('DVE', lambda e, k=k: e.tensor_scalar(
            out=gsc[0:1, k:k + 1], in0=gsc[0:1, k:k + 1], scalar1=FFN_EPS,
            scalar2=None, op0=ALU.max),
          reads=[tk('gsc', k)], writes=[tk('gsc', k)])
        E('DVE', lambda e, k=k: e.reciprocal(gsc[0:1, k + 2:k + 3],
                                             gsc[0:1, k:k + 1]),
          reads=[tk('gsc', k)], writes=[tk('gsc', k + 2)])
        E('ACT', lambda e, k=k: e.activation(
            out=gsc[0:1, k + 4:k + 5], in_=gsc[0:1, k + 2:k + 3],
            func=AF.Copy, scale=0.5),
          reads=[tk('gsc', k + 2)], writes=[tk('gsc', k + 4)])
    for k, col in ((16, 0), (17, 1)):
        E('PE', lambda e, k=k, col=col: e.matmul(
            ps['f0'][:, col:col + 1], ones_f[0:1, :], gsc[0:1, k:k + 1],
            start=True, stop=True),
          reads=[tk('gsc', k), tk('ones_f')], writes=[tk('pmm', 0)])
        E('DVE', lambda e, col=col: e.tensor_copy(gbc[:, col:col + 1],
                                                  ps['f0'][:, col:col + 1]),
          reads=[tk('pmm', 0)], writes=[tk('gbc', col)])

    ACB = min(128, c.ANCH)
    wf3 = wffull.ap().rearrange("p (t two) -> p t two", two=2)
    for ac in range(c.ANCH // ACB):
        a0 = ac * ACB
        E('PE', lambda e, a0=a0: e.transpose(
            ps['f0'][0:ACB, 0:c.E], h8ab[:, a0:a0 + ACB],
            ident_f[0:c.E, 0:c.E]),
          reads=[tk('h8ab'), tk('ident_f')], writes=[tk('pmm', 0)])
        E('DVE', lambda e: e.tensor_copy(gtm[0:ACB, 0:c.E],
                                         ps['f0'][0:ACB, 0:c.E]),
          reads=[tk('pmm', 0)], writes=[tk('gtm', 0)])
        E('PE', lambda e, a0=a0: e.transpose(
            ps['f1'][0:ACB, 0:c.E], dsfm[0:c.E, a0:a0 + ACB],
            ident_f[0:c.E, 0:c.E]),
          reads=dread + [tk('ident_f')], writes=[tk('pmm', 1)])
        E('DVE', lambda e: e.tensor_copy(gtm[0:ACB, 8:8 + c.E],
                                         ps['f1'][0:ACB, 0:c.E]),
          reads=[tk('pmm', 1)], writes=[tk('gtm', 1)])
        E('DVE', lambda e: e.tensor_scalar(
            out=gtm[0:ACB, 0:c.E], in0=gtm[0:ACB, 0:c.E],
            scalar1=gbc[0:ACB, 0:1], scalar2=None, op0=ALU.mult),
          reads=[tk('gtm', 0), tk('gbc', 0)], writes=[tk('gtm', 0)])
        E('DVE', lambda e: e.scalar_tensor_tensor(
            out=gtm[0:ACB, 0:c.E], in0=gtm[0:ACB, 8:8 + c.E],
            scalar=gbc[0:ACB, 1:2], in1=gtm[0:ACB, 0:c.E],
            op0=ALU.mult, op1=ALU.add),
          reads=[tk('gtm', 0), tk('gtm', 1), tk('gbc', 1)],
          writes=[tk('gtm', 0)])
        E('DVE', lambda e: e.reduce_max(gtm[0:ACB, 16:17], gtm[0:ACB, 0:c.E],
                                        axis=AX.X, negate=True),
          reads=[tk('gtm', 0)], writes=[tk('gtm', 16)])
        E('ACT', lambda e: e.activation(
            out=gtm[0:ACB, 0:c.E], in_=gtm[0:ACB, 0:c.E], func=AF.Exp,
            bias=gtm[0:ACB, 16:17], accum_out=gtm[0:ACB, 17:18]),
          reads=[tk('gtm', 0), tk('gtm', 16)],
          writes=[tk('gtm', 0), tk('gtm', 17)])
        E('DVE', lambda e: e.reciprocal(gtm[0:ACB, 18:19],
                                        gtm[0:ACB, 17:18]),
          reads=[tk('gtm', 17)], writes=[tk('gtm', 18)])
        E('DVE', lambda e: e.tensor_scalar(
            out=gtm[0:ACB, 0:c.E], in0=gtm[0:ACB, 0:c.E],
            scalar1=gtm[0:ACB, 18:19], scalar2=None, op0=ALU.mult),
          reads=[tk('gtm', 0), tk('gtm', 18)], writes=[tk('gtm', 0)])
        E('DVE', lambda e: e.reduce_max(gtm[0:ACB, 19:20], gtm[0:ACB, 0:c.E],
                                        axis=AX.X),
          reads=[tk('gtm', 0)], writes=[tk('gtm', 19)])
        E('DVE', lambda e: e.tensor_scalar(
            out=gtm[0:ACB, 8:8 + c.E], in0=gtm[0:ACB, 0:c.E],
            scalar1=gtm[0:ACB, 19:20], scalar2=None, op0=ALU.subtract),
          reads=[tk('gtm', 0), tk('gtm', 19)], writes=[tk('gtm', 1)])
        E('ACT', lambda e: e.activation(
            out=gtm[0:ACB, 8:8 + c.E], in_=gtm[0:ACB, 8:8 + c.E],
            func=AF.Sign, bias=cst[0:ACB, 0:1]),
          reads=[tk('gtm', 1), tk('cst')], writes=[tk('gtm', 1)])
        E('DVE', lambda e: e.tensor_scalar(
            out=gtm[0:ACB, 8:8 + c.E], in0=gtm[0:ACB, 8:8 + c.E],
            scalar1=1.0, scalar2=None, op0=ALU.add),
          reads=[tk('gtm', 1)], writes=[tk('gtm', 1)])
        E('DVE', lambda e: e.scalar_tensor_tensor(
            out=gtm[0:ACB, 8:8 + c.E], in0=gtm[0:ACB, 8:8 + c.E],
            scalar=NEG_BIG, in1=gtm[0:ACB, 0:c.E], op0=ALU.mult,
            op1=ALU.add),
          reads=[tk('gtm', 0), tk('gtm', 1)], writes=[tk('gtm', 1)])
        E('DVE', lambda e: e.reduce_max(gtm[0:ACB, 20:21],
                                        gtm[0:ACB, 8:8 + c.E], axis=AX.X),
          reads=[tk('gtm', 1)], writes=[tk('gtm', 20)])
        E('ACT', lambda e: e.activation(
            out=gtm[0:ACB, 21:22], in_=gtm[0:ACB, 20:21], func=AF.Copy,
            scale=-2.0),
          reads=[tk('gtm', 20)], writes=[tk('gtm', 21)])
        E('ACT', lambda e: e.activation(
            out=gtm[0:ACB, 8:8 + c.E], in_=gtm[0:ACB, 0:c.E],
            func=AF.Sigmoid, scale=2.0, bias=gtm[0:ACB, 21:22]),
          reads=[tk('gtm', 0), tk('gtm', 21)], writes=[tk('gtm', 1)])
        E('DVE', lambda e: e.tensor_mul(gtm[0:ACB, 0:c.E],
                                        gtm[0:ACB, 0:c.E],
                                        gtm[0:ACB, 8:8 + c.E]),
          reads=[tk('gtm', 0), tk('gtm', 1)], writes=[tk('gtm', 0)])
        E('DVE', lambda e: e.reduce_sum(gtm[0:ACB, 22:23],
                                        gtm[0:ACB, 0:c.E], axis=AX.X),
          reads=[tk('gtm', 0)], writes=[tk('gtm', 22)])
        E('ACT', lambda e: e.activation(
            out=gtm[0:ACB, 23:24], in_=gtm[0:ACB, 22:23], func=AF.Copy,
            bias=1e-9),
          reads=[tk('gtm', 22)], writes=[tk('gtm', 23)])
        E('DVE', lambda e: e.reciprocal(gtm[0:ACB, 24:25],
                                        gtm[0:ACB, 23:24]),
          reads=[tk('gtm', 23)], writes=[tk('gtm', 24)])
        E('DVE', lambda e: e.tensor_scalar(
            out=gtm[0:ACB, 0:c.E], in0=gtm[0:ACB, 0:c.E],
            scalar1=gtm[0:ACB, 24:25], scalar2=None, op0=ALU.mult),
          reads=[tk('gtm', 0), tk('gtm', 24)], writes=[tk('gtm', 0)])
        E('PE', lambda e: e.transpose(
            ps['f0'][0:c.E, 0:ACB], gtm[0:ACB, 0:c.E],
            ident_f[0:ACB, 0:ACB]),
          reads=[tk('gtm', 0), tk('ident_f')], writes=[tk('pmm', 0)])
        E('DVE', lambda e, a0=a0: e.tensor_copy(
            wf3[0:c.E, a0:a0 + ACB, 0:1], ps['f0'][0:c.E, 0:ACB]),
          reads=[tk('pmm', 0)], writes=[tk('wffull', ac, 0)])
        E('DVE', lambda e, a0=a0: e.tensor_copy(
            wf3[0:c.E, a0:a0 + ACB, 1:2], ps['f0'][0:c.E, 0:ACB]),
          reads=[tk('pmm', 0)], writes=[tk('wffull', ac, 1)])
    wfread = [tk('wffull', ac, s) for ac in range(c.ANCH // ACB)
              for s in range(2)]
    for tt in range(c.NTT):
        E('PE', lambda e, tt=tt: e.matmul(
            ps['f0'][0:c.R, 0:c.TT], limb[:, :],
            wffull[:, tt * c.TT:(tt + 1) * c.TT], start=True, stop=True),
          reads=wfread + [tk('limb')], writes=[tk('pmm', 0)])
        E('DVE', lambda e, tt=tt: e.tensor_copy(
            pmfm[:, tt * c.TT:(tt + 1) * c.TT], ps['f0'][0:c.R, 0:c.TT]),
          reads=[tk('pmm', 0)], writes=[tk('pmfm', tt)])
    E('DVE', lambda e: e.tensor_mul(pmfm[:, :], pmfm[:, :], zfm[:, :]),
      reads=zread + [tk('pmfm', tt2) for tt2 in range(c.NTT)],
      writes=[tk('pmfm2')])
    E('DVE', lambda e: e.tensor_scalar(
        out=zpbf[:, :], in0=pmfm[:, :], scalar1=0.5, scalar2=None,
        op0=ALU.mult),
      reads=[tk('pmfm2')], writes=[tk('zpbf')])

    def ffn_extra(tt, pm, ec):
        E('PE', lambda e, pm=pm, ec=ec, tt=tt: e.matmul(
            pm[:, 0:c.TT], bbuf[:, ec * 128:(ec + 1) * 128],
            zpbf[:, tt * c.TT:(tt + 1) * c.TT], start=False, stop=True),
          reads=[tk('bbuf'), tk('zpbf')], writes=[tk('pmm', ec % 2)])

    for tt in range(c.NTT):
        src3 = x2fm3[:, :, tt * c.TT:(tt + 1) * c.TT]
        dst3 = arhs3[:, :, 0:c.TT]
        E('ACTD', lambda e, s=src3, d=dst3: e.dma_start(out=d, in_=s),
          reads=[tk('x2fm', tt, jj, eg) for jj in range(c.JT)
                 for eg in range(c.DC // 4)],
          writes=[tk('arhs', do) for do in range(c.DC)])
        stream_proj(tt, 1, x2sp,
                    (lambda tt2, jj2, eg2: tk('x2sp', tt2, jj2, eg2)),
                    7, extra_mm=ffn_extra, bias_col=bcol)

    dbg_map = {'qkvc': qkvc, 'qkva': qkva, 'attnc': attnc, 'attnm': attnm,
               'x2sp': x2sp, 'x2fm': x2fm}
    if dbg:
        snap = p.snap()
        for name in dbg:
            if name not in dbg_map:
                continue
            src = dbg_map[name]
            dt = nc.dram_tensor(f"dbg_{name}", list(src.shape), src.dtype,
                                kind="ExternalOutput")
            E('SPD', lambda e, dt=dt, src=src: e.dma_start(
                out=dt[:, :], in_=src[:, :]), extra_waits=snap)

    p.barrier()
    es5.close()
    p.build()
    return nc



def host_layouts(cfg, Wqkv, Wout, Wffn, A, Bmat, LiMEs, qw, qb, kw, kb, fb):
    c = cfg

    def wt(W, nec):
        return np.ascontiguousarray(
            W.reshape(nec, 128, c.DC, 128).transpose(0, 3, 2, 1)
            .reshape(nec * 128, c.DC * 128))

    d = {}
    Wqkv_r = np.concatenate([Wqkv[2 * c.D:3 * c.D], Wqkv[0:c.D],
                             Wqkv[c.D:2 * c.D]], axis=0)
    d['wqkv'] = wt(Wqkv_r, c.QKVC)
    d['wout'] = wt(Wout, c.DC)
    d['wffn'] = wt(Wffn, c.DC)
    d['w8'] = np.ascontiguousarray(
        Wffn[0:c.E].reshape(c.E, c.DC, 128).transpose(2, 1, 0)
        .reshape(128, c.DC * c.E))
    d['amat'] = np.ascontiguousarray(
        A.reshape(c.DC, 128, c.R).transpose(1, 0, 2).reshape(128, c.DC * c.R))
    d['bmat'] = np.ascontiguousarray(Bmat)
    d['limes'] = np.ascontiguousarray(LiMEs)
    d['limest'] = np.ascontiguousarray(LiMEs.T)
    d['qlnw'] = np.ascontiguousarray(qw.reshape(c.DC, 128))
    d['qlnb'] = np.ascontiguousarray(qb.reshape(c.DC, 128))
    d['klnw'] = np.ascontiguousarray(kw.reshape(c.DC, 128))
    d['klnb'] = np.ascontiguousarray(kb.reshape(c.DC, 128))
    d['fbias'] = np.ascontiguousarray(fb.reshape(c.DC, 128))
    return d


def shard_x(cfg, x):
    c = cfg
    shards = []
    for core in range(c.NCORES):
        b, j = core // c.GPC, core % c.GPC
        g2 = 2 * c.GPC - 1 - j
        shards.append(np.ascontiguousarray(np.concatenate([
            x[b, j * c.CH:(j + 1) * c.CH],
            x[b, g2 * c.CH:(g2 + 1) * c.CH]], axis=0)))
    return shards


def unshard_out(cfg, outs):
    c = cfg
    D = outs[0].shape[1]
    full = np.empty((c.B, c.S, D), dtype=outs[0].dtype)
    for core in range(c.NCORES):
        b, j = core // c.GPC, core % c.GPC
        g2 = 2 * c.GPC - 1 - j
        full[b, j * c.CH:(j + 1) * c.CH] = outs[core][0:c.CH]
        full[b, g2 * c.CH:(g2 + 1) * c.CH] = outs[core][c.CH:2 * c.CH]
    return full


def core_sk3(cfg):
    c = cfg
    sl = alibi_slopes_np(c.NH).astype(np.float64)
    gk = np.arange(c.S, dtype=np.float64)
    out = []
    for core in range(c.NCORES):
        j = core % c.GPC
        rows = np.zeros((3 * c.HPC, c.S), dtype=ml_dtypes.bfloat16)
        for i in range(c.HPC):
            v = -sl[j * c.HPC + i] * gk
            hi = v.astype(ml_dtypes.bfloat16)
            r1 = v - hi.astype(np.float64)
            lo = r1.astype(ml_dtypes.bfloat16)
            r2 = r1 - lo.astype(np.float64)
            lo2 = r2.astype(ml_dtypes.bfloat16)
            rows[3 * i + 0] = hi
            rows[3 * i + 1] = lo
            rows[3 * i + 2] = lo2
        out.append(rows)
    return out


def core_alcol(cfg):
    c = cfg
    sl = alibi_slopes_np(c.NH).astype(np.float64)
    nkb = c.S // 128
    out = []
    for core in range(c.NCORES):
        j = core % c.GPC
        a_ = np.zeros((128, c.HPC * nkb), dtype=np.float32)
        p = np.arange(128, dtype=np.float64)
        for i in range(c.HPC):
            for t in range(nkb):
                a_[:, i * nkb + t] = sl[j * c.HPC + i] * (128 * t + p)
        out.append(a_)
    return out


def core_gidx(cfg):
    c = cfg
    out = []
    for core in range(c.NCORES):
        j = core % c.GPC
        g_ = np.zeros((128, 6 * c.GPC * c.HPC), dtype=np.int32)
        for i in range(c.HPC):
            for part in range(3):
                for g2 in range(2 * c.GPC):
                    r = g2 if g2 < c.GPC else 2 * c.GPC - 1 - g2
                    hf = 0 if g2 < c.GPC else 1
                    kcol = (i * 3 + part) * 2 * c.GPC + g2
                    base = ((hf * 3 + part) * c.GPC * c.D + r * c.D
                            + (j * c.HPC + i) * c.HD)
                    g_[:, kcol] = base + np.arange(128)
        out.append(g_)
    return out


def core_aidx(cfg):
    c = cfg
    out = []
    for core in range(c.NCORES):
        j = core % c.GPC
        cbase = [j * c.CH, (2 * c.GPC - 1 - j) * c.CH]
        a_ = np.zeros((128, c.NTT * 2 * c.DC), dtype=np.int32)
        for tt in range(c.NTT):
            lo_l, hi_l = tt * c.TT, (tt + 1) * c.TT
            for h in (0, 1):
                aa = max(lo_l, h * c.CH)
                bb = min(hi_l, (h + 1) * c.CH)
                if aa >= bb:
                    continue
                gcol = cbase[h] + (aa - h * c.CH)
                NSP = 4 if c.HPC >= 4 else (2 if c.HPC >= 2 else 1)
                HH = c.HPC // NSP
                for do in range(c.DC):
                    gh = do
                    r, i = gh // c.HPC, gh % c.HPC
                    hg, i2 = i // HH, i % HH
                    row0 = (hg * c.GPC * HH + r * HH + i2) * 128
                    col = (tt * 2 + h) * c.DC + do
                    a_[:, col] = (row0 + np.arange(128)) * c.S + gcol
        out.append(a_)
    return out


def make_in_maps(cfg, x_full, weights):
    shards = shard_x(cfg, x_full)
    gidxs = core_gidx(cfg)
    aidxs = core_aidx(cfg)
    sk3s = core_sk3(cfg)
    alcols = core_alcol(cfg)
    return [dict(weights, x=shards[i], gidx=gidxs[i],
                 aidx=aidxs[i], sk3=sk3s[i], alcol=alcols[i])
            for i in range(cfg.NCORES)]



def kernel(L0_attn_hidden_states, L0_attn_Wqkv, L0_attn_q_ln_w, L0_attn_q_ln_b,
           L0_attn_k_ln_w, L0_attn_k_ln_b, L0_attn_Wout, L0_ffn_W, L0_ffn_b,
           L0_ffn_A, L0_ffn_Bmat, L0_ffn_LiMEs):
    cfg = make_cfg()
    x = np.asarray(L0_attn_hidden_states, dtype=np.float32)
    weights = host_layouts(
        cfg, np.asarray(L0_attn_Wqkv, np.float32),
        np.asarray(L0_attn_Wout, np.float32),
        np.asarray(L0_ffn_W, np.float32),
        np.asarray(L0_ffn_A, np.float32),
        np.asarray(L0_ffn_Bmat, np.float32),
        np.asarray(L0_ffn_LiMEs, np.float32),
        np.asarray(L0_attn_q_ln_w, np.float32),
        np.asarray(L0_attn_q_ln_b, np.float32),
        np.asarray(L0_attn_k_ln_w, np.float32),
        np.asarray(L0_attn_k_ln_b, np.float32),
        np.asarray(L0_ffn_b, np.float32))
    in_maps = make_in_maps(cfg, x, weights)
    nc = build_program(cfg)
    res = run_bass_kernel_spmd(nc, in_maps, core_ids=list(range(cfg.NCORES)))
    outs = [np.asarray(res.results[i]["out"]) for i in range(cfg.NCORES)]
    return unshard_out(cfg, outs)


# version 21
# speedup vs baseline: 1.0804x; 1.0287x over previous
import math
from contextlib import ExitStack
from types import SimpleNamespace

import ml_dtypes
import numpy as np

import concourse.bass as bass
import concourse.mybir as mybir
from concourse.bass import ds
from concourse.expressions_rust import RuntimeValue
from concourse.bass_core_rust import IndirectOffsetOnAxis
from concourse.bass_utils import run_bass_kernel_spmd

F32 = mybir.dt.float32
BF16 = mybir.dt.bfloat16
AF = mybir.ActivationFunctionType
ALU = mybir.AluOpType
AX = mybir.AxisListType

CLIP = 8.0
LN_EPS = 1e-5
FFN_EPS = 1e-6
NEG_BIG = -1e30
ARS = 512


def make_cfg(S=4096, D=4096, NH=32, B=2, NCORES=8):
    c = SimpleNamespace()
    c.S, c.D, c.NH, c.B, c.NCORES = S, D, NH, B, NCORES
    c.HD = D // NH
    assert c.HD == 128
    c.GPC = NCORES // B
    c.CH = S // (2 * c.GPC)
    c.T = 2 * c.CH
    c.HPC = NH // c.GPC
    c.DC = D // 128
    c.QKVC = 3 * D // 128
    c.E, c.R = 8, 32
    c.TT = min(512, c.T)
    c.NTT = c.T // c.TT
    c.JT = c.TT // 128
    c.NTC = c.T // 128
    c.QB = S // 128
    c.NQG = c.QB // 4
    c.ANCH = c.T // 2
    assert c.DC % 4 == 0 and c.T % 128 == 0 and c.S % 512 == 0
    assert c.QB % 4 == 0
    c.BLK = 3 * c.HPC * c.HD
    c.ABLK = c.HPC * c.HD
    return c


def alibi_slopes_np(n_heads, bias_max=8.0):
    next_pow2 = 2 ** int(np.ceil(np.log2(n_heads)))
    m = np.arange(1, next_pow2 + 1, dtype=np.float32) * (bias_max / next_pow2)
    slopes = 1.0 / np.power(2.0, m)
    if next_pow2 != n_heads:
        slopes = np.concatenate([slopes[1::2], slopes[::2]])[:n_heads]
    return slopes.astype(np.float32)



class Prog:
    QUEUE = {
        'PE': 'tensor', 'DVE': 'vector', 'ACT': 'scalar', 'POOL': 'gpsimd',
        'SP': 'sync', 'SPD': 'sync', 'ACTD': 'scalar', 'PLD': 'gpsimd',
        'CC': 'gpsimd',
    }
    DMA_CH = frozenset(('SPD', 'ACTD', 'PLD'))
    NSEM = 16

    def __init__(self, nc):
        self.nc = nc
        self.sem = {}
        for ch in self.QUEUE:
            if ch in self.DMA_CH:
                self.sem[ch] = [nc.alloc_semaphore(f"sem_{ch}_{i}")
                                for i in range(self.NSEM)]
            else:
                self.sem[ch] = nc.alloc_semaphore(f"sem_{ch}")
        self.cnt = {ch: 0 for ch in self.QUEUE}
        self.lastw = {ch: {} for ch in self.QUEUE}
        self.ops = []
        self.st = {}

    def _wait_spec(self, pch, v):
        if pch in self.DMA_CH:
            idx = v - 1
            slot = idx % self.NSEM
            return (pch, slot), self.sem[pch][slot], 16 * (idx // self.NSEM + 1)
        return (pch, 0), self.sem[pch], v

    def emit(self, ch, fn, reads=(), writes=(), extra_waits=()):
        needs = {}

        def need(pch, v):
            if pch == ch and ch == 'PE':
                return
            key, semh, val = self._wait_spec(pch, v)
            cur = needs.get(key)
            if cur is None or val > cur[1]:
                needs[key] = (semh, val)

        for t in reads:
            s = self.st.get(t)
            if s and s[0]:
                need(*s[0])
        for t in writes:
            s = self.st.get(t)
            if s:
                if s[0]:
                    need(*s[0])
                for (rch, _slot), rv in s[1].items():
                    need(rch, rv)
        for pch, v in extra_waits:
            need(pch, v)
        waits = []
        for key in sorted(needs):
            semh, val = needs[key]
            if self.lastw[ch].get(key, 0) >= val:
                continue
            self.lastw[ch][key] = val
            waits.append((semh, val))
        self.cnt[ch] += 1
        mycnt = self.cnt[ch]
        if ch in self.DMA_CH:
            idx = mycnt - 1
            slot = idx % self.NSEM
            mysem = self.sem[ch][slot]
            amt = 16
            if idx >= self.NSEM:
                key = (ch, slot)
                val = 16 * (idx // self.NSEM)
                if self.lastw[ch].get(key, 0) < val:
                    self.lastw[ch][key] = val
                    waits.append((mysem, val))
        else:
            mysem = self.sem[ch]
            amt = 1
        self.ops.append((ch, waits, fn, mysem, amt))
        for t in writes:
            self.st[t] = [(ch, mycnt), {}]
        rkey = (ch, (mycnt - 1) % self.NSEM if ch in self.DMA_CH else 0)
        for t in reads:
            s = self.st.setdefault(t, [None, {}])
            if s[1].get(rkey, 0) < mycnt:
                s[1][rkey] = mycnt

    def snap(self):
        return [(pch, self.cnt[pch]) for pch in self.QUEUE if self.cnt[pch] > 0]

    def barrier(self, exclude=()):
        specs = []
        for pch in self.QUEUE:
            n = self.cnt[pch]
            if n == 0 or pch in exclude:
                continue
            if pch in self.DMA_CH:
                for s in range(min(n, self.NSEM)):
                    ns = (n + self.NSEM - 1 - s) // self.NSEM
                    specs.append((self.sem[pch][s], 16 * ns, (pch, s)))
            else:
                specs.append((self.sem[pch], n, (pch, 0)))
        for ch in ('PE', 'DVE', 'ACT', 'POOL', 'SP'):
            waits = []
            for semh, val, key in specs:
                if self.lastw[ch].get(key, 0) >= val:
                    continue
                self.lastw[ch][key] = val
                waits.append((semh, val))
            self.cnt[ch] += 1
            self.ops.append((ch, waits, lambda e: e.nop(),
                             self.sem[ch] if ch not in self.DMA_CH else None, 1))

    def build(self):
        by_queue = {}
        for op in self.ops:
            by_queue.setdefault(self.QUEUE[op[0]], []).append(op)

        def run_list(eng, lst):
            for ch, waits, fn, mysem, amt in lst:
                for semh, v in waits:
                    eng.wait_ge(semh, v)
                inst = fn(eng)
                if inst is not None:
                    inst.then_inc(mysem, amt)

        nc = self.nc
        with nc.Block() as block:
            for qname in ('tensor', 'vector', 'scalar', 'gpsimd', 'sync'):
                lst = by_queue.get(qname, [])
                if not lst:
                    continue
                getattr(block, qname)(
                    (lambda lst: (lambda eng: run_list(eng, lst)))(lst))



def build_program(cfg, dbg=()):
    c = cfg
    nc = bass.Bass("TRN2", target_bir_lowering=False)

    x_in = nc.dram_tensor("x", [c.T, c.D], F32, kind="ExternalInput")
    wqkv = nc.dram_tensor("wqkv", [c.QKVC * 128, c.D], F32, kind="ExternalInput")
    wout = nc.dram_tensor("wout", [c.D, c.D], F32, kind="ExternalInput")
    wffn = nc.dram_tensor("wffn", [c.D, c.D], F32, kind="ExternalInput")
    w8 = nc.dram_tensor("w8", [128, c.DC * c.E], F32, kind="ExternalInput")
    amat = nc.dram_tensor("amat", [128, c.DC * c.R], F32, kind="ExternalInput")
    bmat = nc.dram_tensor("bmat", [c.R, c.D], F32, kind="ExternalInput")
    limes = nc.dram_tensor("limes", [c.E, c.R], F32, kind="ExternalInput")
    limest = nc.dram_tensor("limest", [c.R, c.E], F32, kind="ExternalInput")
    qlnw = nc.dram_tensor("qlnw", [c.DC, 128], F32, kind="ExternalInput")
    qlnb = nc.dram_tensor("qlnb", [c.DC, 128], F32, kind="ExternalInput")
    klnw = nc.dram_tensor("klnw", [c.DC, 128], F32, kind="ExternalInput")
    klnb = nc.dram_tensor("klnb", [c.DC, 128], F32, kind="ExternalInput")
    fbias = nc.dram_tensor("fbias", [c.DC, 128], F32, kind="ExternalInput")
    sk3 = nc.dram_tensor("sk3", [3 * c.HPC, c.S], BF16, kind="ExternalInput")
    alcol = nc.dram_tensor("alcol", [128, (c.S // 128) * c.HPC], F32,
                           kind="ExternalInput")
    gidx = nc.dram_tensor("gidx", [128, 6 * c.GPC * c.HPC], mybir.dt.int32,
                          kind="ExternalInput")
    aidx = nc.dram_tensor("aidx", [128, c.NTT * 2 * c.DC], mybir.dt.int32,
                          kind="ExternalInput")
    out_t = nc.dram_tensor("out", [c.T, c.D], F32, kind="ExternalOutput")

    qkvc = nc.dram_tensor("qkvc", [2 * 3 * c.D, c.CH], BF16)
    qkva = nc.dram_tensor("qkva", [2 * c.GPC * 3 * c.D, c.CH], BF16)
    attnc = nc.dram_tensor("attnc", [c.ABLK, c.S], BF16)
    attnm = nc.dram_tensor("attnm", [c.GPC * c.ABLK, c.S], BF16)
    x2sp = nc.dram_tensor("x2sp", [c.T, c.D], F32)
    wbfd = nc.dram_tensor("wbfd", [2 * c.D, c.D], BF16)
    x2fm = nc.dram_tensor("x2fm", [c.D, c.T], BF16)
    scalc = nc.dram_tensor("scalc", [2, 1], F32)
    scala = nc.dram_tensor("scala", [2 * c.NCORES, 1], F32)

    p = Prog(nc)

    def S_(name, shape, dt):
        return nc.alloc_sbuf_tensor(name, shape, dt)

    ident_b = S_("ident_b", [128, 128], BF16)
    ident_f = S_("ident_f", [128, 128], F32)
    ones_f = S_("ones_f", [1, 128], F32)
    pvec = S_("pvec", [32, 128], F32)
    wq_s = S_("wq_s", [128, c.DC], F32)
    bq_s = S_("bq_s", [128, c.DC], F32)
    wk_s = S_("wk_s", [128, c.DC], F32)
    bk_s = S_("bk_s", [128, c.DC], F32)
    bcol = S_("bcol", [128, c.DC], F32)
    stat = S_("stat", [128, 16], F32)
    accs = S_("accs", [128, 8], F32)
    accq = S_("accq", [128, 8], F32)
    grec = S_("grec", [128, 16], F32)
    gsc = S_("gsc", [16, 20], F32)
    gbc = S_("gbc", [128, 2], F32)
    cst = S_("cst", [128, 4], F32)
    gix = S_("gix", [128, 6 * c.GPC * c.HPC], mybir.dt.int32)
    aix = S_("aix", [128, c.NTT * 2 * c.DC], mybir.dt.int32)

    es1 = ExitStack()

    def S1(name, shape, dt):
        return es1.enter_context(nc.sbuf_tensor(name, shape, dt))

    x_fm = S1("x_fm", [128, c.DC * c.T], BF16)
    qk_tm = S1("qk_tm", [128, c.NTC * c.D], BF16)
    sqd1 = S1("sqd", [128, c.D], F32)
    wsl = [S1(f"wsl{i}", [128, c.D], BF16) for i in range(3)]
    xst = [S1("xst0", [128, c.D], F32)]
    xbt = [S1("xbt0", [128, c.D], BF16)]
    est = [S1(f"est{i}", [128, 512], BF16) for i in range(2)]
    fmst = [S1(f"fmst{i}", [128, max(c.T, 512)], BF16) for i in range(2)]

    ps = {}
    for i in range(6):
        ps[f"f{i}"] = nc.alloc_psum_tensor(f"psf{i}", [128, 512], F32)
    for i in range(2):
        ps[f"b{i}"] = nc.alloc_psum_tensor(f"psb{i}", [128, 1024], BF16)

    def tk(*a):
        return a

    E = p.emit

    E('POOL', lambda e: e.memset(ident_b[:, :], 0.0), writes=[tk('ident_b')])
    E('POOL', lambda e: e.affine_select(
        out=ident_b[:, :], in_=ident_b[:, :], compare_op=ALU.not_equal,
        fill=1.0, base=0, pattern=[[-1, 128]], channel_multiplier=1),
      reads=[tk('ident_b')], writes=[tk('ident_b')])
    E('POOL', lambda e: e.memset(ident_f[:, :], 0.0), writes=[tk('ident_f')])
    E('POOL', lambda e: e.affine_select(
        out=ident_f[:, :], in_=ident_f[:, :], compare_op=ALU.not_equal,
        fill=1.0, base=0, pattern=[[-1, 128]], channel_multiplier=1),
      reads=[tk('ident_f')], writes=[tk('ident_f')])
    E('POOL', lambda e: e.memset(ones_f[:, :], 1.0), writes=[tk('ones_f')])
    E('POOL', lambda e: e.memset(cst[:, 0:1], 0.0), writes=[tk('cst')])
    E('POOL', lambda e: e.memset(cst[:, 1:2], LN_EPS), writes=[tk('cst')])
    E('POOL', lambda e: e.memset(cst[:, 2:3], 1e-9), writes=[tk('cst')])

    E('PLD', lambda e: e.dma_start(out=gix[:, :], in_=gidx[:, :]),
      writes=[tk('gix')])
    E('PLD', lambda e: e.dma_start(out=aix[:, :], in_=aidx[:, :]),
      writes=[tk('aix')])

    def load_param(name, dram, dest, scale=None):
        E('SPD', lambda e, d=dram: e.dma_start(out=pvec[0:c.DC, :], in_=d[:, :]),
          writes=[tk('pvec')])
        E('PE', lambda e: e.transpose(ps['f0'][:, 0:c.DC], pvec[0:c.DC, :],
                                      ident_f[0:c.DC, 0:c.DC]),
          reads=[tk('pvec'), tk('ident_f')], writes=[tk('pmm', 0)])
        if scale is None:
            E('DVE', lambda e, d2=dest: e.tensor_copy(d2[:, :],
                                                      ps['f0'][:, 0:c.DC]),
              reads=[tk('pmm', 0)], writes=[tk(name)])
        else:
            E('DVE', lambda e, d2=dest, sc=scale: e.tensor_scalar(
                out=d2[:, :], in0=ps['f0'][:, 0:c.DC], scalar1=sc,
                scalar2=None, op0=ALU.mult),
              reads=[tk('pmm', 0)], writes=[tk(name)])

    att_scale = 1.0 / math.sqrt(c.HD)
    load_param('wq_s', qlnw, wq_s, att_scale)
    load_param('bq_s', qlnb, bq_s, att_scale)
    load_param('wk_s', klnw, wk_s)
    load_param('bk_s', klnb, bk_s)
    load_param('bcol', fbias, bcol)


    xfm3 = x_fm.ap().rearrange("p (do t) -> p do t", do=c.DC)
    NXQ = max(1, c.DC // 8)
    XQW = c.D // NXQ
    for j in range(c.NTC):
        xs, xb = xst[0], xbt[0]
        for q in range(NXQ):
            E('SPD', lambda e, j=j, q=q, xs=xs: e.dma_start(
                out=xs[:, q * XQW:(q + 1) * XQW],
                in_=x_in[j * 128:(j + 1) * 128, q * XQW:(q + 1) * XQW]),
              writes=[tk('xst', q)])
            E('POOL', lambda e, q=q, xs=xs, xb=xb: e.tensor_copy(
                xb[:, q * XQW:(q + 1) * XQW], xs[:, q * XQW:(q + 1) * XQW]),
              reads=[tk('xst', q)], writes=[tk('xbt', q)])
        for dg in range(c.DC // 4):
            tb = ps[f"b{dg % 2}"]
            for dd in range(4):
                do = dg * 4 + dd
                E('PE', lambda e, xb=xb, do=do, dd=dd, tb=tb: e.transpose(
                    tb[:, dd * 128:(dd + 1) * 128],
                    xb[:, do * 128:(do + 1) * 128], ident_b[:, :]),
                  reads=[tk('xbt', do * 128 // XQW), tk('ident_b')],
                  writes=[tk('psb', dg % 2)])
            src3 = tb.ap()[:, 0:512].rearrange("p (g t) -> p g t", g=4)
            dst3 = xfm3[:, dg * 4:dg * 4 + 4, j * 128:(j + 1) * 128]
            E('DVE', lambda e, s=src3, d=dst3: e.tensor_copy(d, s),
              reads=[tk('psb', dg % 2)], writes=[tk('x_fm', dg, j)])

    def vrow0(e_in):
        return 2 * c.D + e_in * 128

    def qkrow0(part, e_in):
        return part * c.D + e_in * 128

    groups = [[b_ * c.GPC + r for r in range(c.GPC)] for b_ in range(c.B)]

    def emit_gather(part):
        for hf in (0, 1):
            toks = [tk('qkvc', part, e2, hf) for e2 in range(c.DC)]
            E('CC', lambda e, hf=hf, part=part: e.collective_compute(
                "AllGather", ALU.bypass, replica_groups=groups,
                ins=[qkvc[hf * 3 * c.D + part * c.D:
                          hf * 3 * c.D + (part + 1) * c.D, :].opt()],
                outs=[qkva[(hf * 3 + part) * c.GPC * c.D:
                           (hf * 3 + part + 1) * c.GPC * c.D, :].opt()]),
              reads=toks, writes=[tk('qkva', part, hf)])

    qktm3 = qk_tm.ap().rearrange("p (j d) -> p j d", j=c.NTC)
    pend1 = [None]
    for ec in range(c.QKVC):
        pidx, e_in = ec // c.DC, ec % c.DC
        part = (2, 0, 1)[pidx]
        wb = wsl[ec % 3]
        E('PLD', lambda e, ec=ec, wb=wb: e.dma_start(
            out=wb[:, :], in_=wqkv[ec * 128:(ec + 1) * 128, :]),
          writes=[tk('wsl', ec % 3)])
        for tt in range(c.NTT):
            pm = ps[f"f{tt % 2}"]
            for do in range(c.DC):
                E('PE', lambda e, wb=wb, do=do, tt=tt, pm=pm: e.matmul(
                    pm[:, 0:c.TT], wb[:, do * 128:(do + 1) * 128],
                    x_fm[:, do * c.T + tt * c.TT: do * c.T + (tt + 1) * c.TT],
                    start=(do == 0), stop=(do == c.DC - 1)),
                  reads=[tk('wsl', ec % 3)] + [tk('x_fm', do // 4, jj)
                         for jj in range(tt * c.JT, (tt + 1) * c.JT)],
                  writes=[tk('pmm', tt % 2)])
            est_b = est[tt % 2]
            if pend1[0] is not None:
                pend1[0]()
                pend1[0] = None
            E('DVE', lambda e, pm=pm, est_b=est_b: e.tensor_scalar(
                out=est_b[:, 0:c.TT], in0=pm[:, 0:c.TT], scalar1=CLIP,
                scalar2=-CLIP, op0=ALU.min, op1=ALU.max),
              reads=[tk('pmm', tt % 2)], writes=[tk('est', tt % 2)])
            if part < 2:
                def _p1t(est_b=est_b, tt=tt, e_in=e_in):
                    tb = ps[f"b{tt % 2}"]
                    for jj in range(c.JT):
                        E('PE', lambda e, est_b=est_b, jj=jj, tb=tb:
                          e.transpose(
                            tb[:, jj * 128:(jj + 1) * 128],
                            est_b[:, jj * 128:(jj + 1) * 128],
                            ident_b[:, :]),
                          reads=[tk('est', tt % 2), tk('ident_b')],
                          writes=[tk('psb', tt % 2)])
                    src3 = tb.ap()[:, 0:c.JT * 128].rearrange(
                        "p (g t) -> p g t", g=c.JT)
                    dst3 = qktm3[:, tt * c.JT:(tt + 1) * c.JT,
                                 e_in * 128:e_in * 128 + 128]
                    E('DVE', lambda e, s=src3, d=dst3: e.tensor_copy(d, s),
                      reads=[tk('psb', tt % 2)],
                      writes=[tk('qktm', e_in, tt)])
                pend1[0] = _p1t
            else:
                r0 = vrow0(e_in)
                lo_l, hi_l = tt * c.TT, (tt + 1) * c.TT
                for hf in (0, 1):
                    aa = max(lo_l, hf * c.CH)
                    bb = min(hi_l, (hf + 1) * c.CH)
                    if aa >= bb:
                        continue
                    E('SPD', lambda e, est_b=est_b, r0=r0, hf=hf, aa=aa,
                      bb=bb, lo_l=lo_l: e.dma_start(
                        out=qkvc[hf * 3 * c.D + r0:hf * 3 * c.D + r0 + 128,
                                 aa - hf * c.CH:bb - hf * c.CH],
                        in_=est_b[:, aa - lo_l:bb - lo_l]),
                      reads=[tk('est', tt % 2)],
                      writes=[tk('qkvc', 2, e_in, hf)])
        if part == 2 and e_in == c.DC - 1:
            emit_gather(2)
        if part < 2 and e_in == c.DC - 1:
            if pend1[0] is not None:
                pend1[0]()
                pend1[0] = None
            for j in range(c.NTC):
                tile = qk_tm[:, j * c.D:(j + 1) * c.D]
                rtoks = [tk('qktm', e2, j // c.JT) for e2 in range(c.DC)]
                E('DVE', lambda e, tile=tile: e.reduce_sum(
                    stat[:, 0:1], tile, axis=AX.X, negate=True),
                  reads=rtoks, writes=[tk('stat', 0)])
                E('ACT', lambda e: e.activation(
                    out=stat[:, 1:2], in_=stat[:, 0:1], func=AF.Copy,
                    scale=1.0 / c.D),
                  reads=[tk('stat', 0)], writes=[tk('stat', 1)])
                E('ACT', lambda e, tile=tile: e.activation(
                    out=sqd1[:, 0:c.D], in_=tile, func=AF.Square,
                    bias=stat[:, 1:2], accum_out=stat[:, 2:3]),
                  reads=rtoks + [tk('stat', 1)],
                  writes=[tk('sqd'), tk('stat', 2)])
                E('ACT', lambda e: e.activation(
                    out=stat[:, 3:4], in_=stat[:, 2:3], func=AF.Ln,
                    scale=1.0 / c.D, bias=cst[:, 1:2]),
                  reads=[tk('stat', 2), tk('cst')], writes=[tk('stat', 3)])
                E('ACT', lambda e: e.activation(
                    out=stat[:, 4:5], in_=stat[:, 3:4], func=AF.Exp,
                    scale=-0.5, bias=cst[:, 0:1]),
                  reads=[tk('stat', 3), tk('cst')], writes=[tk('stat', 4)])
                E('DVE', lambda e, tile=tile: e.tensor_scalar(
                    out=tile, in0=tile, scalar1=stat[:, 1:2],
                    scalar2=stat[:, 4:5], op0=ALU.add, op1=ALU.mult),
                  reads=rtoks + [tk('stat', 1), tk('stat', 4)], writes=rtoks)
            wsv, bsv = (wq_s, bq_s) if part == 0 else (wk_s, bk_s)
            G4 = min(4, c.NTC)
            for ein2 in range(c.DC):
                fst = fmst[ein2 % 2]
                for jg in range(c.NTC // G4):
                    tb = ps[f"b{jg % 2}"]
                    for jj in range(G4):
                        j = jg * G4 + jj
                        E('PE', lambda e, j=j, jj=jj, ein2=ein2, tb=tb:
                          e.transpose(
                              tb[:, jj * 128:(jj + 1) * 128],
                              qk_tm[:, j * c.D + ein2 * 128:
                                    j * c.D + ein2 * 128 + 128],
                              ident_b[:, :]),
                          reads=[tk('qktm', ein2, j // c.JT), tk('ident_b')],
                          writes=[tk('psb', jg % 2)])
                    E('DVE', lambda e, tb=tb, jg=jg, G4=G4, fst=fst, wsv=wsv,
                      bsv=bsv, ein2=ein2: e.tensor_scalar(
                          out=fst[:, jg * G4 * 128:(jg + 1) * G4 * 128],
                          in0=tb[:, 0:G4 * 128],
                          scalar1=wsv[:, ein2:ein2 + 1],
                          scalar2=bsv[:, ein2:ein2 + 1],
                          op0=ALU.mult, op1=ALU.add),
                      reads=[tk('psb', jg % 2), tk(wsv.name), tk(bsv.name)],
                      writes=[tk('fmst', ein2 % 2, jg)])
                r0 = qkrow0(part, ein2)
                for hf in (0, 1):
                    E('SPD', lambda e, fst=fst, r0=r0, hf=hf: e.dma_start(
                        out=qkvc[hf * 3 * c.D + r0:hf * 3 * c.D + r0 + 128,
                                 0:c.CH],
                        in_=fst[:, hf * c.CH:(hf + 1) * c.CH]),
                      reads=[tk('fmst', ein2 % 2, jg2)
                             for jg2 in range(c.NTC // G4)],
                      writes=[tk('qkvc', part, ein2, hf)])
            if part == 0:
                emit_gather(0)

    es1.close()
    p.barrier()
    emit_gather(1)

    es3 = ExitStack()

    def S3(name, shape, dt):
        return es3.enter_context(nc.sbuf_tensor(name, shape, dt))

    qsbs = [S3(f"qsb{k}", [128, c.S], BF16) for k in range(2)]
    ksbs = [S3(f"ksb{k}", [128, c.S], BF16) for k in range(2)]
    vfsbs = [S3(f"vfsb{k}", [128, c.S], BF16) for k in range(2)]
    vtsbs = [S3(f"vtsb{k}", [128, c.S], BF16) for k in range(2)]
    atsb = [S3(f"atsb{i}", [128, c.S], BF16) for i in range(2)]
    skh = [S3(f"skh{i}", [128, c.S], BF16) for i in range(2)]
    prbT = [S3(f"prbt{t}", [128, 512], BF16) for t in range(3)]
    zrec = [S3(f"zrec{i}", [128, 512], F32) for i in range(2)]
    alc = S3("alc", [128, (c.S // 128) * c.HPC], F32)
    ones3 = S3("ones3", [128, 128], BF16)
    msk = S3("msk", [128, 1024], BF16)
    NKB = c.S // 128
    E('POOL', lambda e: e.memset(ones3[:, :], 1.0), writes=[tk('ones3')])
    for _i in range(2):
        E('POOL', lambda e, _i=_i: e.memset(skh[_i][:, :], 0.0),
          writes=[tk('skh', _i)])
    E('POOL', lambda e: e.memset(msk[:, :], 0.0), writes=[tk('msk')])
    E('POOL', lambda e: e.affine_select(
        out=msk[:, :], in_=msk[:, :], compare_op=ALU.is_ge,
        fill=NEG_BIG, base=-384, pattern=[[1, 1024]], channel_multiplier=-1),
      reads=[tk('msk')], writes=[tk('msk')])
    E('PLD', lambda e: e.dma_start(out=alc[:, :], in_=alcol[:, :]),
      writes=[tk('alc')])

    def chunk_owner(g):
        r = g if g < c.GPC else 2 * c.GPC - 1 - g
        lo = 0 if g < c.GPC else c.CH
        slot = 0 if g < c.GPC else 1
        return r, lo, slot

    KVS = 2 if c.NQG % 2 == 0 else 1
    GH = 2 * c.GPC // KVS

    def emit_loads(i, hf):
        hp = i % 2
        qsb, ksb, vfsb = qsbs[hp], ksbs[hp], vfsbs[hp]
        for g in range(0 if hf else 0, (hf + 1) * GH):
            parts = ((0, qsb, ('qsb', hp)), (2, vfsb, ('vfsb', hp)),
                     (1, ksb, ('ksb', hp)))
            if hf == 1 and g < GH:
                parts = ((2, vfsb, ('vfsb', hp)), (1, ksb, ('ksb', hp)))
            for part, dest, dtok in parts:
                kcol = (i * 3 + part) * 2 * c.GPC + g
                E('PLD', lambda e, dest=dest, g=g, kcol=kcol:
                  e.indirect_dma_start(
                      out=dest[:, g * c.CH:(g + 1) * c.CH], out_offset=None,
                      in_=qkva.ap(),
                      in_offset=IndirectOffsetOnAxis(gix[:, kcol:kcol + 1], 0),
                      element_offset=0),
                  reads=[tk('qkva', part, g // c.GPC), tk('gix')],
                  writes=[dtok + (g,)])

    def emit_vt(i, hf):
        hp = i % 2
        vfsb, vtsb = vfsbs[hp], vtsbs[hp]
        vread = [tk('vfsb', hp, g) for g in range(0, (hf + 1) * GH)]
        w0 = 0
        for cg in range((hf + 1) * (c.S // KVS) // 512):
            tb = ps[f"b{cg % 2}"]
            for cc in range(4):
                ch_i = (w0 // 128) + cg * 4 + cc
                E('PE', lambda e, ch_i=ch_i, cc=cc, tb=tb, vfsb=vfsb:
                  e.transpose(
                    tb[:, cc * 128:(cc + 1) * 128],
                    vfsb[:, ch_i * 128:(ch_i + 1) * 128], ident_b[:, :]),
                  reads=vread + [tk('ident_b')], writes=[tk('psb', cg % 2)])
            E('DVE', lambda e, cg=cg, tb=tb, vtsb=vtsb, w0=w0: e.tensor_copy(
                vtsb[:, w0 + cg * 512:w0 + (cg + 1) * 512], tb[:, 0:512]),
              reads=[tk('psb', cg % 2)],
              writes=[tk('vtsb', hp, (w0 // 512) + cg)])

    def emit_dh(i):
        hq = i % 2
        E('PLD', lambda e, i=i, hq=hq: e.dma_start(
            out=skh[hq][0:3, :], in_=sk3[3 * i:3 * i + 3, :]),
          writes=[tk('skh', hq)])

    def emit_attn(i, gi):
        hp = i % 2
        hq = i % 2
        qsb, ksb, vtsb = qsbs[hp], ksbs[hp], vtsbs[hp]
        B0 = gi * 4
        ntl = B0 + 4
        q0 = gi * 512
        qread = [tk('qsb', hp, g)
                 for g in range(q0 // c.CH,
                                min(2 * c.GPC, (q0 + 512 + c.CH - 1) // c.CH))]
        scb = [ps['f0'], ps['f1'], ps['f3']]
        pav = ps['f2'] if gi % 2 == 0 else ps['f5']
        zr = zrec[gi % 2]
        pend = [None]
        for t in range(ntl):
            pm = scb[t % 3]
            d = t - B0
            kread = [tk('ksb', hp, g)
                     for g in range(t * 128 // c.CH,
                                    min(2 * c.GPC,
                                        ((t + 1) * 128 + c.CH - 1) // c.CH))]
            ptok = tk('psc', t % 3)
            if d >= 0:
                off = 384 - 128 * d
                E('PE', lambda e, pm=pm, off=off: e.matmul(
                    pm[:, 0:512], ident_b[:, :], msk[:, off:off + 512],
                    start=True, stop=False),
                  reads=[tk('ident_b'), tk('msk')], writes=[ptok])
            E('PE', lambda e, pm=pm, hq=hq, q0=q0, d=d: e.matmul(
                pm[:, 0:512], ones3[:, :], skh[hq][:, q0:q0 + 512],
                start=(d < 0), stop=False),
              reads=[tk('ones3'), tk('skh', hq)], writes=[ptok])
            E('PE', lambda e, pm=pm, t=t, q0=q0, qsb=qsb, ksb=ksb: e.matmul(
                pm[:, 0:512], ksb[:, t * 128:(t + 1) * 128],
                qsb[:, q0:q0 + 512], start=False, stop=True),
              reads=kread + qread, writes=[ptok])
            if pend[0] is not None:
                pend[0]()
                pend[0] = None
            pb = prbT[t % 3]
            E('ACT', lambda e, pm=pm, pb=pb, i=i, t=t: e.activation(
                out=pb[:, :], in_=pm[:, 0:512], func=AF.Exp,
                bias=alc[:, i * NKB + t:i * NKB + t + 1]),
              reads=[ptok, tk('alc')], writes=[tk('prbt', t % 3)])

            def _zav(t=t, pb=pb):
                fs = (t == 0)
                fe = (t == ntl - 1)
                E('PE', lambda e, pb=pb, fs=fs, fe=fe: e.matmul(
                    ps['f4'][:, 0:512], ones3[:, :], pb[:, :],
                    start=fs, stop=fe),
                  reads=[tk('ones3'), tk('prbt', t % 3)], writes=[tk('pz')])
                E('PE', lambda e, pb=pb, t=t, fs=fs, fe=fe, pav=pav,
                  vtsb=vtsb: e.matmul(
                    pav[:, 0:512], vtsb[:, t * 128:(t + 1) * 128], pb[:, :],
                    start=fs, stop=fe),
                  reads=[tk('vtsb', hp, t // 4), tk('prbt', t % 3)],
                  writes=[tk('pav', gi % 2)])
            pend[0] = _zav
        pend[0]()
        E('DVE', lambda e, zr=zr: e.reciprocal(zr[:, :], ps['f4'][:, 0:512]),
          reads=[tk('pz')], writes=[tk('zrec', gi % 2)])
        E('DVE', lambda e, i=i, q0=q0, pav=pav, zr=zr: e.tensor_tensor(
            atsb[i % 2][:, q0:q0 + 512], pav[:, 0:512], zr[:, :],
            op=ALU.mult),
          reads=[tk('pav', gi % 2), tk('zrec', gi % 2)],
          writes=[tk('atsb', i % 2, gi)])

    def emit_attn_dma(i, hf):
        w0 = hf * (c.S // KVS)
        wid = c.S // KVS
        E('SPD', lambda e, i=i, w0=w0, wid=wid: e.dma_start(
            out=attnc[i * c.HD:(i + 1) * c.HD, w0:w0 + wid],
            in_=atsb[i % 2][:, w0:w0 + wid]),
          reads=[tk('atsb', i % 2, gi2)
                 for gi2 in range(hf * c.NQG // KVS,
                                  (hf + 1) * c.NQG // KVS)],
          writes=[tk('attnc', i, hf)])

    wcast_jobs = ([(wout, 0, ec) for ec in range(c.DC)]
                  + [(wffn, 1, ec) for ec in range(c.DC)])

    def emit_wcast(batch_idx, nbatches):
        n = len(wcast_jobs)
        lo = batch_idx * n // nbatches
        hi = (batch_idx + 1) * n // nbatches
        for (wd, half, ec) in wcast_jobs[lo:hi]:
            E('PLD', lambda e, wd=wd, half=half, ec=ec: e.dma_start(
                out=wbfd[half * c.D + ec * 128:half * c.D + (ec + 1) * 128, :],
                in_=wd[ec * 128:(ec + 1) * 128, :]),
              writes=[tk('wbfd', half, ec)])

    NSPg = 4 if c.HPC >= 4 else (2 if c.HPC >= 2 else 1)
    HHg = c.HPC // NSPg

    def emit_attn_gather(hg):
        E('CC', lambda e, hg=hg: e.collective_compute(
            "AllGather", ALU.bypass, replica_groups=groups,
            ins=[attnc[hg * HHg * c.HD:(hg + 1) * HHg * c.HD, :].opt()],
            outs=[attnm[hg * c.GPC * HHg * c.HD:
                        (hg + 1) * c.GPC * HHg * c.HD, :].opt()]),
          reads=[tk('attnc', i, hf2) for i in range(hg * HHg, (hg + 1) * HHg)
                 for hf2 in range(KVS)],
          writes=[tk('attnm', hg)])

    NQH = c.NQG // KVS
    seq = [(hf, i, hf * NQH + gl) for hf in range(KVS)
           for i in range(c.HPC) for gl in range(NQH)]
    emit_loads(0, 0)
    emit_vt(0, 0)
    emit_dh(0)
    NWB = KVS * c.HPC
    wb_done = [0]
    for k in range(len(seq)):
        hf, i, gi = seq[k]
        first_g = (gi == hf * NQH)
        last_g = (gi == (hf + 1) * NQH - 1)
        if first_g:
            nhf, ni = (hf, i + 1) if i + 1 < c.HPC else (hf + 1, 0)
            if nhf < KVS:
                emit_loads(ni, nhf)
            emit_wcast(wb_done[0], NWB)
            wb_done[0] += 1
        emit_attn(i, gi)
        if last_g:
            nhf, ni = (hf, i + 1) if i + 1 < c.HPC else (hf + 1, 0)
            if nhf < KVS:
                emit_vt(ni, nhf)
                emit_dh(ni)
            emit_attn_dma(i, hf)
            if hf == KVS - 1 and (i + 1) % HHg == 0:
                hg2 = (i + 1) // HHg - 1
                emit_attn_gather(hg2)

    es3.close()
    p.barrier(exclude=('CC',))

    es5 = ExitStack()

    def S5(name, shape, dt):
        return es5.enter_context(nc.sbuf_tensor(name, shape, dt))

    arhss = [S5(f"arhs{i}", [128, c.DC * ARS], BF16) for i in range(2)]
    abuf = S5("abuf", [128, c.DC * c.R], BF16)
    bbuf = S5("bbuf", [c.R, c.D], BF16)
    w8buf = S5("w8buf", [128, c.DC * c.E], BF16)
    limb = S5("limb", [c.E, c.R], BF16)
    limtb = S5("limtb", [c.R, c.E], BF16)
    wsl = [S5(f"wsl5{i}", [128, c.D], BF16) for i in range(3)]
    fmst5 = [S5(f"fmst5{i}", [128, 512], BF16) for i in range(2)]
    wgrps = [S5(f"wgrp{i}", [128, 512], F32) for i in range(2)]
    E('PLD', lambda e: e.dma_start(out=abuf[:, :], in_=amat[:, :]),
      writes=[tk('abuf')])
    E('PLD', lambda e: e.dma_start(out=bbuf[:, :], in_=bmat[:, :]),
      writes=[tk('bbuf')])
    E('PLD', lambda e: e.dma_start(out=w8buf[:, :], in_=w8[:, :]),
      writes=[tk('w8buf')])
    E('PLD', lambda e: e.dma_start(out=limb[:, :], in_=limes[:, :]),
      writes=[tk('limb')])
    E('PLD', lambda e: e.dma_start(out=limtb[:, :], in_=limest[:, :]),
      writes=[tk('limtb')])
    xres = [S5(f"xres{i}", [128, 512], F32) for i in range(4)]
    s1sb = [S5(f"s1sb{i}", [128, c.D], F32) for i in range(c.JT)]
    x2bt = [S5(f"x2bt{i}", [128, 512], BF16) for i in range(2)]
    rdt = [S5(f"rdt{i}", [128, 512], F32) for i in range(2)]
    zfm = S5("zfm", [c.R, c.T], F32)
    zab = S5("zab", [c.R, c.ANCH], BF16)
    h8fm = S5("h8fm", [c.E, c.T], F32)
    h8ab = S5("h8ab", [c.E, c.ANCH], F32)
    dsfm = S5("dsfm", [c.E, c.ANCH], F32)
    gtm = S5("gtm", [128, 32], F32)
    wffull = S5("wffull", [c.E, c.T], BF16)
    pmfm = S5("pmfm", [c.R, c.T], F32)
    zpbf = S5("zpbf", [c.R, c.T], BF16)

    attnm3 = attnm.ap().rearrange("(do p) t -> p do t", p=128)

    attnm_flat = attnm.ap().rearrange("r (t one) -> (r t) one", one=1)

    def hg_of(do):
        return (do % c.HPC) // HHg if NSPg > 1 else 0

    def arhs_load(tt, src3, dyncols, rtoks):
        ab = arhss[tt % 2]
        lo_l, hi_l = tt * c.TT, (tt + 1) * c.TT
        for h in (0, 1):
            a = max(lo_l, h * c.CH)
            b = min(hi_l, (h + 1) * c.CH)
            if a >= b:
                continue
            if dyncols:
                for do in sorted(range(c.DC), key=lambda d2: (hg_of(d2), d2)):
                    col = (tt * 2 + h) * c.DC + do
                    E('PLD', lambda e, do=do, col=col, a=a, b=b, lo_l=lo_l,
                      ab=ab: e.indirect_dma_start(
                          out=ab[:, do * ARS + (a - lo_l):
                                 do * ARS + (b - lo_l)],
                          out_offset=None, in_=attnm_flat,
                          in_offset=IndirectOffsetOnAxis(
                              aix[:, col:col + 1], 0),
                          element_offset=0),
                      reads=[tk('attnm', hg_of(do)), tk('aix')],
                      writes=[tk('arhs', tt % 2, do)])
            else:
                sp = src3[:, :, a:b]
                dp = arhs3s[tt % 2][:, :, a - lo_l:b - lo_l]
                E('ACTD', lambda e, sp=sp, dp=dp: e.dma_start(out=dp, in_=sp),
                  reads=rtoks,
                  writes=[tk('arhs', tt % 2, do) for do in range(c.DC)])
    arhs3s = [ab.ap().rearrange("p (do t) -> p do t", do=c.DC)
              for ab in arhss]
    x2fm3 = x2fm.ap().rearrange("(do p) t -> p do t", p=128)

    def stream_proj(tt, whalf, src_resid, resid_tok, phase,
                    extra_mm=None, bias_col=None):
        JT = c.JT
        E('DVE', lambda e: e.memset(accs[:, 0:JT], 0.0), writes=[tk('accs')])
        E('DVE', lambda e: e.memset(accq[:, 0:JT], 0.0), writes=[tk('accq')])
        pend = [None]
        for ec in range(c.DC):
            wb = wsl[ec % 3]
            E('PLD', lambda e, ec=ec, wb=wb, whalf=whalf: e.dma_start(
                out=wb[:, :],
                in_=wbfd[whalf * c.D + ec * 128:
                         whalf * c.D + (ec + 1) * 128, :]),
              reads=[tk('wbfd', whalf, ec)],
              writes=[tk('wsl', ec % 3)])
            if ec % 4 == 0:
                for jj in range(JT):
                    rt = ([resid_tok(tt, jj, ec // 4)]
                          if resid_tok is not None else [])
                    E('SPD', lambda e, jj=jj, ec=ec, tt=tt,
                      src_resid=src_resid: e.dma_start(
                          out=xres[jj][:, :],
                          in_=src_resid[tt * c.TT + jj * 128:
                                        tt * c.TT + (jj + 1) * 128,
                                        (ec // 4) * 512:(ec // 4 + 1) * 512]),
                      reads=rt, writes=[tk('xres', jj)])
            pm = ps[f"f{ec % 2}"]
            ab = arhss[tt % 2]
            dorder = (sorted(range(c.DC), key=lambda do: (hg_of(do), do))
                      if phase == 5 else list(range(c.DC)))
            for di, do in enumerate(dorder):
                E('PE', lambda e, wb=wb, do=do, pm=pm, di=di, ab=ab: e.matmul(
                    pm[:, 0:c.TT], wb[:, do * 128:(do + 1) * 128],
                    ab[:, do * ARS:do * ARS + c.TT],
                    start=(di == 0),
                    stop=(di == c.DC - 1 and extra_mm is None)),
                  reads=[tk('wsl', ec % 3), tk('arhs', tt % 2, do)],
                  writes=[tk('pmm', ec % 2)])
            if extra_mm is not None:
                extra_mm(tt, pm, ec)
            wgrp = wgrps[ec % 2]
            if pend[0] is not None:
                pend[0]()
                pend[0] = None
            if bias_col is None:
                E('DVE', lambda e, pm=pm, wgrp=wgrp: e.tensor_copy(
                    wgrp[:, 0:c.TT], pm[:, 0:c.TT]),
                  reads=[tk('pmm', ec % 2)], writes=[tk('wgrp', ec % 2)])
            else:
                E('DVE', lambda e, pm=pm, ec=ec, wgrp=wgrp: e.tensor_scalar(
                    out=wgrp[:, 0:c.TT], in0=pm[:, 0:c.TT],
                    scalar1=bias_col[:, ec:ec + 1], scalar2=None,
                    op0=ALU.add),
                  reads=[tk('pmm', ec % 2), tk('bcol')],
                  writes=[tk('wgrp', ec % 2)])

            def _transposes(ec=ec, wgrp=wgrp):
                for jj in range(JT):
                    pt = ps[f"f{2 + jj}"]
                    E('PE', lambda e, jj=jj, ec=ec, pt=pt, wgrp=wgrp:
                      e.transpose(
                        pt[:, (ec % 4) * 128:(ec % 4 + 1) * 128],
                        wgrp[:, jj * 128:(jj + 1) * 128], ident_f[:, :]),
                      reads=[tk('wgrp', ec % 2), tk('ident_f')],
                      writes=[tk('ptr', jj)])
            pend[0] = _transposes
            if ec % 4 == 3:
                pend[0]()
                pend[0] = None
            if ec % 4 == 3:
                for jj in range(JT):
                    pt = ps[f"f{2 + jj}"]
                    eg0 = ec // 4
                    E('DVE', lambda e, pt=pt, jj=jj, eg0=eg0:
                      e.scalar_tensor_tensor(
                          out=s1sb[jj][:, eg0 * 512:(eg0 + 1) * 512],
                          in0=pt[:, 0:512], scalar=1.0,
                          in1=xres[jj][:, :], op0=ALU.mult, op1=ALU.add,
                          accum_out=stat[:, 0:1]),
                      reads=[tk('ptr', jj), tk('xres', jj)],
                      writes=[tk('s1sb', jj, eg0), tk('stat', 0)])
                    E('DVE', lambda e, jj=jj: e.tensor_tensor(
                        accs[:, jj:jj + 1], accs[:, jj:jj + 1],
                        stat[:, 0:1], ALU.add),
                      reads=[tk('accs'), tk('stat', 0)], writes=[tk('accs')])
                    E('ACT', lambda e, jj=jj, eg0=eg0: e.activation(
                        out=rdt[0][:, 0:512],
                        in_=s1sb[jj][:, eg0 * 512:(eg0 + 1) * 512],
                        func=AF.Square, bias=cst[:, 0:1],
                        accum_out=stat[:, 1:2]),
                      reads=[tk('s1sb', jj, eg0), tk('cst')],
                      writes=[tk('rdt', 0), tk('stat', 1)])
                    E('DVE', lambda e, jj=jj: e.tensor_tensor(
                        accq[:, jj:jj + 1], accq[:, jj:jj + 1],
                        stat[:, 1:2], ALU.add),
                      reads=[tk('accq'), tk('stat', 1)], writes=[tk('accq')])
        for jj in range(JT):
            E('ACT', lambda e, jj=jj: e.activation(
                out=stat[:, 2:3], in_=accs[:, jj:jj + 1], func=AF.Copy,
                scale=-1.0 / c.D),
              reads=[tk('accs')], writes=[tk('stat', 2)])
            E('ACT', lambda e: e.activation(
                out=stat[:, 3:4], in_=stat[:, 2:3], func=AF.Square,
                bias=cst[:, 0:1]),
              reads=[tk('stat', 2), tk('cst')], writes=[tk('stat', 3)])
            E('ACT', lambda e, jj=jj: e.activation(
                out=stat[:, 4:5], in_=accq[:, jj:jj + 1], func=AF.Copy,
                scale=1.0 / c.D),
              reads=[tk('accq')], writes=[tk('stat', 4)])
            E('DVE', lambda e: e.tensor_sub(
                stat[:, 5:6], stat[:, 4:5], stat[:, 3:4]),
              reads=[tk('stat', 3), tk('stat', 4)], writes=[tk('stat', 5)])
            E('ACT', lambda e: e.activation(
                out=stat[:, 6:7], in_=stat[:, 5:6], func=AF.Ln,
                bias=cst[:, 1:2]),
              reads=[tk('stat', 5), tk('cst')], writes=[tk('stat', 6)])
            E('ACT', lambda e: e.activation(
                out=stat[:, 7:8], in_=stat[:, 6:7], func=AF.Exp, scale=-0.5,
                bias=cst[:, 0:1]),
              reads=[tk('stat', 6), tk('cst')], writes=[tk('stat', 7)])
            for eg in range(c.DC // 4):
                rb = rdt[eg % 2]
                E('DVE', lambda e, rb=rb, jj=jj, eg=eg: e.tensor_scalar(
                    out=rb[:, :], in0=s1sb[jj][:, eg * 512:(eg + 1) * 512],
                    scalar1=stat[:, 2:3], scalar2=stat[:, 7:8],
                    op0=ALU.add, op1=ALU.mult),
                  reads=[tk('s1sb', jj, eg), tk('stat', 2), tk('stat', 7)],
                  writes=[tk('rdt', eg % 2)])
                if phase == 5:
                    E('SPD', lambda e, rb=rb, jj=jj, eg=eg, tt=tt:
                      e.dma_start(
                          out=x2sp[tt * c.TT + jj * 128:
                                   tt * c.TT + (jj + 1) * 128,
                                   eg * 512:(eg + 1) * 512],
                          in_=rb[:, :]),
                      reads=[tk('rdt', eg % 2)],
                      writes=[tk('x2sp', tt, jj, eg)])
                    xb2 = x2bt[eg % 2]
                    E('POOL', lambda e, rb=rb, xb2=xb2: e.tensor_copy(
                        xb2[:, :], rb[:, :]),
                      reads=[tk('rdt', eg % 2)], writes=[tk('x2bt', eg % 2)])
                    tb = ps[f"b{eg % 2}"]
                    for dd in range(4):
                        E('PE', lambda e, xb2=xb2, dd=dd, tb=tb: e.transpose(
                            tb[:, dd * 128:(dd + 1) * 128],
                            xb2[:, dd * 128:(dd + 1) * 128], ident_b[:, :]),
                          reads=[tk('x2bt', eg % 2), tk('ident_b')],
                          writes=[tk('psb', eg % 2)])
                    fst = fmst5[eg % 2]
                    E('DVE', lambda e, tb=tb, fst=fst: e.tensor_copy(
                        fst[:, 0:512], tb[:, 0:512]),
                      reads=[tk('psb', eg % 2)], writes=[tk('fmst2', eg % 2)])
                    src3 = fst.ap()[:, 0:512].rearrange(
                        "p (g t) -> p g t", g=4)
                    dst3 = x2fm3[:, eg * 4:eg * 4 + 4,
                                 tt * c.TT + jj * 128:
                                 tt * c.TT + (jj + 1) * 128]
                    E('SPD', lambda e, s=src3, d=dst3: e.dma_start(
                        out=d, in_=s),
                      reads=[tk('fmst2', eg % 2)],
                      writes=[tk('x2fm', tt, jj, eg)])
                else:
                    E('SPD', lambda e, rb=rb, jj=jj, eg=eg, tt=tt:
                      e.dma_start(
                          out=out_t[tt * c.TT + jj * 128:
                                    tt * c.TT + (jj + 1) * 128,
                                    eg * 512:(eg + 1) * 512],
                          in_=rb[:, :]),
                      reads=[tk('rdt', eg % 2)],
                      writes=[tk('out', tt, jj, eg)])

    def emit_gating_proj(tt):
        ab = arhss[tt % 2]
        src3 = x2fm3[:, :, tt * c.TT:(tt + 1) * c.TT]
        dst3 = arhs3s[tt % 2][:, :, 0:c.TT]
        E('ACTD', lambda e, s=src3, d=dst3: e.dma_start(out=d, in_=s),
          reads=[tk('x2fm', tt, jj, eg) for jj in range(c.JT)
                 for eg in range(c.DC // 4)],
          writes=[tk('arhs', tt % 2, do) for do in range(c.DC)])
        for do in range(c.DC):
            E('PE', lambda e, do=do, ab=ab: e.matmul(
                ps['f0'][0:c.R, 0:c.TT], abuf[:, do * c.R:(do + 1) * c.R],
                ab[:, do * ARS:do * ARS + c.TT],
                start=(do == 0), stop=(do == c.DC - 1)),
              reads=[tk('abuf'), tk('arhs', tt % 2, do)],
              writes=[tk('pmm', 0)])
        E('DVE', lambda e, tt=tt: e.tensor_copy(
            zfm[:, tt * c.TT:(tt + 1) * c.TT], ps['f0'][0:c.R, 0:c.TT]),
          reads=[tk('pmm', 0)], writes=[tk('zfm', tt)])
        for do in range(c.DC):
            E('PE', lambda e, do=do, ab=ab: e.matmul(
                ps['f1'][0:c.E, 0:c.TT], w8buf[:, do * c.E:(do + 1) * c.E],
                ab[:, do * ARS:do * ARS + c.TT],
                start=(do == 0), stop=(do == c.DC - 1)),
              reads=[tk('w8buf'), tk('arhs', tt % 2, do)],
              writes=[tk('pmm', 1)])
        E('DVE', lambda e, tt=tt: e.tensor_scalar(
            out=h8fm[:, tt * c.TT:(tt + 1) * c.TT],
            in0=ps['f1'][0:c.E, 0:c.TT],
            scalar1=bcol[0:c.E, 0:1], scalar2=None, op0=ALU.add),
          reads=[tk('pmm', 1), tk('bcol')], writes=[tk('h8fm', tt)])

    for tt in range(c.NTT):
        arhs_load(tt, attnm3, True, None)
        stream_proj(tt, 0, x_in, None, 5)
        emit_gating_proj(tt)


    zread = [tk('zfm', tt) for tt in range(c.NTT)]
    hread = [tk('h8fm', tt) for tt in range(c.NTT)]
    z3 = zfm.ap().rearrange("p (t two) -> p t two", two=2)
    h3 = h8fm.ap().rearrange("p (t two) -> p t two", two=2)
    E('DVE', lambda e: e.tensor_copy(zab[:, :], z3[:, :, 1:2]),
      reads=zread, writes=[tk('zab')])
    E('DVE', lambda e: e.tensor_copy(h8ab[:, :], h3[0:c.E, :, 1:2]),
      reads=hread, writes=[tk('h8ab')])
    npt = (c.ANCH + 511) // 512
    for q in range(npt):
        w = min(512, c.ANCH - q * 512)
        E('PE', lambda e, q=q, w=w: e.matmul(
            ps['f0'][0:c.E, 0:w], limtb[:, :],
            zab[:, q * 512:q * 512 + w], start=True, stop=True),
          reads=[tk('zab'), tk('limtb')], writes=[tk('pmm', 0)])
        E('DVE', lambda e, q=q, w=w: e.tensor_copy(
            dsfm[:, q * 512:q * 512 + w], ps['f0'][0:c.E, 0:w]),
          reads=[tk('pmm', 0)], writes=[tk('dsfm', q)])
    dread = [tk('dsfm', q) for q in range(npt)]
    E('DVE', lambda e: e.reduce_max(
        gsc[0:c.E, 0:1], h8ab[:, :], axis=AX.X, apply_absolute_value=True),
      reads=[tk('h8ab')], writes=[tk('gsc', 0)])
    E('DVE', lambda e: e.reduce_max(
        gsc[0:c.E, 1:2], dsfm[:, :], axis=AX.X, apply_absolute_value=True),
      reads=dread, writes=[tk('gsc', 1)])
    E('PE', lambda e: e.transpose(ps['f0'][0:2, 0:c.E], gsc[0:c.E, 0:2],
                                  ident_f[0:c.E, 0:c.E]),
      reads=[tk('gsc', 0), tk('gsc', 1), tk('ident_f')],
      writes=[tk('pmm', 0)])
    E('DVE', lambda e: e.tensor_copy(gsc[0:2, 2:2 + c.E],
                                     ps['f0'][0:2, 0:c.E]),
      reads=[tk('pmm', 0)], writes=[tk('gsc', 2)])
    E('DVE', lambda e: e.reduce_max(gsc[0:2, 10:11], gsc[0:2, 2:2 + c.E],
                                    axis=AX.X),
      reads=[tk('gsc', 2)], writes=[tk('gsc', 10)])
    E('SPD', lambda e: e.dma_start(out=scalc[:, :], in_=gsc[0:2, 10:11]),
      reads=[tk('gsc', 10)], writes=[tk('scalc')])
    E('CC', lambda e: e.collective_compute(
        "AllGather", ALU.bypass, replica_groups=[list(range(c.NCORES))],
        ins=[scalc.ap().opt()], outs=[scala.ap().opt()]),
      reads=[tk('scalc')], writes=[tk('scala')])
    E('SPD', lambda e: e.dma_start(out=gsc[0:2 * c.NCORES, 3:4],
                                   in_=scala[:, :]),
      reads=[tk('scala')], writes=[tk('gsc', 3)])
    E('PE', lambda e: e.transpose(
        ps['f0'][0:1, 0:2 * c.NCORES], gsc[0:2 * c.NCORES, 3:4],
        ident_f[0:2 * c.NCORES, 0:2 * c.NCORES]),
      reads=[tk('gsc', 3), tk('ident_f')], writes=[tk('pmm', 0)])
    E('DVE', lambda e: e.tensor_copy(gsc[0:1, 4:4 + 2 * c.NCORES],
                                     ps['f0'][0:1, 0:2 * c.NCORES]),
      reads=[tk('pmm', 0)], writes=[tk('gsc', 4)])
    g4 = gsc.ap()[0:1, 4:4 + 2 * c.NCORES].rearrange(
        "p (r two) -> p r two", two=2)
    E('DVE', lambda e: e.reduce_max(gsc[0:1, 12:13], g4[:, :, 0:1],
                                    axis=AX.XY),
      reads=[tk('gsc', 4)], writes=[tk('gsc', 12)])
    E('DVE', lambda e: e.reduce_max(gsc[0:1, 13:14], g4[:, :, 1:2],
                                    axis=AX.XY),
      reads=[tk('gsc', 4)], writes=[tk('gsc', 13)])
    for k in (12, 13):
        E('DVE', lambda e, k=k: e.tensor_scalar(
            out=gsc[0:1, k:k + 1], in0=gsc[0:1, k:k + 1], scalar1=FFN_EPS,
            scalar2=None, op0=ALU.max),
          reads=[tk('gsc', k)], writes=[tk('gsc', k)])
        E('DVE', lambda e, k=k: e.reciprocal(gsc[0:1, k + 2:k + 3],
                                             gsc[0:1, k:k + 1]),
          reads=[tk('gsc', k)], writes=[tk('gsc', k + 2)])
        E('ACT', lambda e, k=k: e.activation(
            out=gsc[0:1, k + 4:k + 5], in_=gsc[0:1, k + 2:k + 3],
            func=AF.Copy, scale=0.5),
          reads=[tk('gsc', k + 2)], writes=[tk('gsc', k + 4)])
    for k, col in ((16, 0), (17, 1)):
        E('PE', lambda e, k=k, col=col: e.matmul(
            ps['f0'][:, col:col + 1], ones_f[0:1, :], gsc[0:1, k:k + 1],
            start=True, stop=True),
          reads=[tk('gsc', k), tk('ones_f')], writes=[tk('pmm', 0)])
        E('DVE', lambda e, col=col: e.tensor_copy(gbc[:, col:col + 1],
                                                  ps['f0'][:, col:col + 1]),
          reads=[tk('pmm', 0)], writes=[tk('gbc', col)])

    ACB = min(128, c.ANCH)
    wf3 = wffull.ap().rearrange("p (t two) -> p t two", two=2)
    for ac in range(c.ANCH // ACB):
        a0 = ac * ACB
        E('PE', lambda e, a0=a0: e.transpose(
            ps['f0'][0:ACB, 0:c.E], h8ab[:, a0:a0 + ACB],
            ident_f[0:c.E, 0:c.E]),
          reads=[tk('h8ab'), tk('ident_f')], writes=[tk('pmm', 0)])
        E('DVE', lambda e: e.tensor_copy(gtm[0:ACB, 0:c.E],
                                         ps['f0'][0:ACB, 0:c.E]),
          reads=[tk('pmm', 0)], writes=[tk('gtm', 0)])
        E('PE', lambda e, a0=a0: e.transpose(
            ps['f1'][0:ACB, 0:c.E], dsfm[0:c.E, a0:a0 + ACB],
            ident_f[0:c.E, 0:c.E]),
          reads=dread + [tk('ident_f')], writes=[tk('pmm', 1)])
        E('DVE', lambda e: e.tensor_copy(gtm[0:ACB, 8:8 + c.E],
                                         ps['f1'][0:ACB, 0:c.E]),
          reads=[tk('pmm', 1)], writes=[tk('gtm', 1)])
        E('DVE', lambda e: e.tensor_scalar(
            out=gtm[0:ACB, 0:c.E], in0=gtm[0:ACB, 0:c.E],
            scalar1=gbc[0:ACB, 0:1], scalar2=None, op0=ALU.mult),
          reads=[tk('gtm', 0), tk('gbc', 0)], writes=[tk('gtm', 0)])
        E('DVE', lambda e: e.scalar_tensor_tensor(
            out=gtm[0:ACB, 0:c.E], in0=gtm[0:ACB, 8:8 + c.E],
            scalar=gbc[0:ACB, 1:2], in1=gtm[0:ACB, 0:c.E],
            op0=ALU.mult, op1=ALU.add),
          reads=[tk('gtm', 0), tk('gtm', 1), tk('gbc', 1)],
          writes=[tk('gtm', 0)])
        E('DVE', lambda e: e.reduce_max(gtm[0:ACB, 16:17], gtm[0:ACB, 0:c.E],
                                        axis=AX.X, negate=True),
          reads=[tk('gtm', 0)], writes=[tk('gtm', 16)])
        E('ACT', lambda e: e.activation(
            out=gtm[0:ACB, 0:c.E], in_=gtm[0:ACB, 0:c.E], func=AF.Exp,
            bias=gtm[0:ACB, 16:17], accum_out=gtm[0:ACB, 17:18]),
          reads=[tk('gtm', 0), tk('gtm', 16)],
          writes=[tk('gtm', 0), tk('gtm', 17)])
        E('DVE', lambda e: e.reciprocal(gtm[0:ACB, 18:19],
                                        gtm[0:ACB, 17:18]),
          reads=[tk('gtm', 17)], writes=[tk('gtm', 18)])
        E('DVE', lambda e: e.tensor_scalar(
            out=gtm[0:ACB, 0:c.E], in0=gtm[0:ACB, 0:c.E],
            scalar1=gtm[0:ACB, 18:19], scalar2=None, op0=ALU.mult),
          reads=[tk('gtm', 0), tk('gtm', 18)], writes=[tk('gtm', 0)])
        E('DVE', lambda e: e.reduce_max(gtm[0:ACB, 19:20], gtm[0:ACB, 0:c.E],
                                        axis=AX.X),
          reads=[tk('gtm', 0)], writes=[tk('gtm', 19)])
        E('DVE', lambda e: e.tensor_scalar(
            out=gtm[0:ACB, 8:8 + c.E], in0=gtm[0:ACB, 0:c.E],
            scalar1=gtm[0:ACB, 19:20], scalar2=None, op0=ALU.subtract),
          reads=[tk('gtm', 0), tk('gtm', 19)], writes=[tk('gtm', 1)])
        E('ACT', lambda e: e.activation(
            out=gtm[0:ACB, 8:8 + c.E], in_=gtm[0:ACB, 8:8 + c.E],
            func=AF.Sign, bias=cst[0:ACB, 0:1]),
          reads=[tk('gtm', 1), tk('cst')], writes=[tk('gtm', 1)])
        E('DVE', lambda e: e.tensor_scalar(
            out=gtm[0:ACB, 8:8 + c.E], in0=gtm[0:ACB, 8:8 + c.E],
            scalar1=1.0, scalar2=None, op0=ALU.add),
          reads=[tk('gtm', 1)], writes=[tk('gtm', 1)])
        E('DVE', lambda e: e.scalar_tensor_tensor(
            out=gtm[0:ACB, 8:8 + c.E], in0=gtm[0:ACB, 8:8 + c.E],
            scalar=NEG_BIG, in1=gtm[0:ACB, 0:c.E], op0=ALU.mult,
            op1=ALU.add),
          reads=[tk('gtm', 0), tk('gtm', 1)], writes=[tk('gtm', 1)])
        E('DVE', lambda e: e.reduce_max(gtm[0:ACB, 20:21],
                                        gtm[0:ACB, 8:8 + c.E], axis=AX.X),
          reads=[tk('gtm', 1)], writes=[tk('gtm', 20)])
        E('ACT', lambda e: e.activation(
            out=gtm[0:ACB, 21:22], in_=gtm[0:ACB, 20:21], func=AF.Copy,
            scale=-2.0),
          reads=[tk('gtm', 20)], writes=[tk('gtm', 21)])
        E('ACT', lambda e: e.activation(
            out=gtm[0:ACB, 8:8 + c.E], in_=gtm[0:ACB, 0:c.E],
            func=AF.Sigmoid, scale=2.0, bias=gtm[0:ACB, 21:22]),
          reads=[tk('gtm', 0), tk('gtm', 21)], writes=[tk('gtm', 1)])
        E('DVE', lambda e: e.tensor_mul(gtm[0:ACB, 0:c.E],
                                        gtm[0:ACB, 0:c.E],
                                        gtm[0:ACB, 8:8 + c.E]),
          reads=[tk('gtm', 0), tk('gtm', 1)], writes=[tk('gtm', 0)])
        E('DVE', lambda e: e.reduce_sum(gtm[0:ACB, 22:23],
                                        gtm[0:ACB, 0:c.E], axis=AX.X),
          reads=[tk('gtm', 0)], writes=[tk('gtm', 22)])
        E('ACT', lambda e: e.activation(
            out=gtm[0:ACB, 23:24], in_=gtm[0:ACB, 22:23], func=AF.Copy,
            bias=1e-9),
          reads=[tk('gtm', 22)], writes=[tk('gtm', 23)])
        E('DVE', lambda e: e.reciprocal(gtm[0:ACB, 24:25],
                                        gtm[0:ACB, 23:24]),
          reads=[tk('gtm', 23)], writes=[tk('gtm', 24)])
        E('DVE', lambda e: e.tensor_scalar(
            out=gtm[0:ACB, 0:c.E], in0=gtm[0:ACB, 0:c.E],
            scalar1=gtm[0:ACB, 24:25], scalar2=None, op0=ALU.mult),
          reads=[tk('gtm', 0), tk('gtm', 24)], writes=[tk('gtm', 0)])
        E('PE', lambda e: e.transpose(
            ps['f0'][0:c.E, 0:ACB], gtm[0:ACB, 0:c.E],
            ident_f[0:ACB, 0:ACB]),
          reads=[tk('gtm', 0), tk('ident_f')], writes=[tk('pmm', 0)])
        E('DVE', lambda e, a0=a0: e.tensor_copy(
            wf3[0:c.E, a0:a0 + ACB, 0:1], ps['f0'][0:c.E, 0:ACB]),
          reads=[tk('pmm', 0)], writes=[tk('wffull', ac, 0)])
        E('DVE', lambda e, a0=a0: e.tensor_copy(
            wf3[0:c.E, a0:a0 + ACB, 1:2], ps['f0'][0:c.E, 0:ACB]),
          reads=[tk('pmm', 0)], writes=[tk('wffull', ac, 1)])
    wfread = [tk('wffull', ac, s) for ac in range(c.ANCH // ACB)
              for s in range(2)]
    for tt in range(c.NTT):
        E('PE', lambda e, tt=tt: e.matmul(
            ps['f0'][0:c.R, 0:c.TT], limb[:, :],
            wffull[:, tt * c.TT:(tt + 1) * c.TT], start=True, stop=True),
          reads=wfread + [tk('limb')], writes=[tk('pmm', 0)])
        E('DVE', lambda e, tt=tt: e.tensor_copy(
            pmfm[:, tt * c.TT:(tt + 1) * c.TT], ps['f0'][0:c.R, 0:c.TT]),
          reads=[tk('pmm', 0)], writes=[tk('pmfm', tt)])
    E('DVE', lambda e: e.tensor_mul(pmfm[:, :], pmfm[:, :], zfm[:, :]),
      reads=zread + [tk('pmfm', tt2) for tt2 in range(c.NTT)],
      writes=[tk('pmfm2')])
    E('DVE', lambda e: e.tensor_scalar(
        out=zpbf[:, :], in0=pmfm[:, :], scalar1=0.5, scalar2=None,
        op0=ALU.mult),
      reads=[tk('pmfm2')], writes=[tk('zpbf')])

    def ffn_extra(tt, pm, ec):
        E('PE', lambda e, pm=pm, ec=ec, tt=tt: e.matmul(
            pm[:, 0:c.TT], bbuf[:, ec * 128:(ec + 1) * 128],
            zpbf[:, tt * c.TT:(tt + 1) * c.TT], start=False, stop=True),
          reads=[tk('bbuf'), tk('zpbf')], writes=[tk('pmm', ec % 2)])

    for tt in range(c.NTT):
        src3 = x2fm3[:, :, tt * c.TT:(tt + 1) * c.TT]
        dst3 = arhs3s[tt % 2][:, :, 0:c.TT]
        E('ACTD', lambda e, s=src3, d=dst3: e.dma_start(out=d, in_=s),
          reads=[tk('x2fm', tt, jj, eg) for jj in range(c.JT)
                 for eg in range(c.DC // 4)],
          writes=[tk('arhs', tt % 2, do) for do in range(c.DC)])
        stream_proj(tt, 1, x2sp,
                    (lambda tt2, jj2, eg2: tk('x2sp', tt2, jj2, eg2)),
                    7, extra_mm=ffn_extra, bias_col=bcol)

    dbg_map = {'qkvc': qkvc, 'qkva': qkva, 'attnc': attnc, 'attnm': attnm,
               'x2sp': x2sp, 'x2fm': x2fm}
    if dbg:
        snap = p.snap()
        for name in dbg:
            if name not in dbg_map:
                continue
            src = dbg_map[name]
            dt = nc.dram_tensor(f"dbg_{name}", list(src.shape), src.dtype,
                                kind="ExternalOutput")
            E('SPD', lambda e, dt=dt, src=src: e.dma_start(
                out=dt[:, :], in_=src[:, :]), extra_waits=snap)

    p.barrier()
    es5.close()
    p.build()
    return nc



def host_layouts(cfg, Wqkv, Wout, Wffn, A, Bmat, LiMEs, qw, qb, kw, kb, fb):
    c = cfg

    def wt(W, nec):
        return np.ascontiguousarray(
            W.reshape(nec, 128, c.DC, 128).transpose(0, 3, 2, 1)
            .reshape(nec * 128, c.DC * 128))

    d = {}
    Wqkv_r = np.concatenate([Wqkv[2 * c.D:3 * c.D], Wqkv[0:c.D],
                             Wqkv[c.D:2 * c.D]], axis=0)
    d['wqkv'] = wt(Wqkv_r, c.QKVC)
    d['wout'] = wt(Wout, c.DC)
    d['wffn'] = wt(Wffn, c.DC)
    d['w8'] = np.ascontiguousarray(
        Wffn[0:c.E].reshape(c.E, c.DC, 128).transpose(2, 1, 0)
        .reshape(128, c.DC * c.E))
    d['amat'] = np.ascontiguousarray(
        A.reshape(c.DC, 128, c.R).transpose(1, 0, 2).reshape(128, c.DC * c.R))
    d['bmat'] = np.ascontiguousarray(Bmat)
    d['limes'] = np.ascontiguousarray(LiMEs)
    d['limest'] = np.ascontiguousarray(LiMEs.T)
    d['qlnw'] = np.ascontiguousarray(qw.reshape(c.DC, 128))
    d['qlnb'] = np.ascontiguousarray(qb.reshape(c.DC, 128))
    d['klnw'] = np.ascontiguousarray(kw.reshape(c.DC, 128))
    d['klnb'] = np.ascontiguousarray(kb.reshape(c.DC, 128))
    d['fbias'] = np.ascontiguousarray(fb.reshape(c.DC, 128))
    return d


def shard_x(cfg, x):
    c = cfg
    shards = []
    for core in range(c.NCORES):
        b, j = core // c.GPC, core % c.GPC
        g2 = 2 * c.GPC - 1 - j
        shards.append(np.ascontiguousarray(np.concatenate([
            x[b, j * c.CH:(j + 1) * c.CH],
            x[b, g2 * c.CH:(g2 + 1) * c.CH]], axis=0)))
    return shards


def unshard_out(cfg, outs):
    c = cfg
    D = outs[0].shape[1]
    full = np.empty((c.B, c.S, D), dtype=outs[0].dtype)
    for core in range(c.NCORES):
        b, j = core // c.GPC, core % c.GPC
        g2 = 2 * c.GPC - 1 - j
        full[b, j * c.CH:(j + 1) * c.CH] = outs[core][0:c.CH]
        full[b, g2 * c.CH:(g2 + 1) * c.CH] = outs[core][c.CH:2 * c.CH]
    return full


def core_sk3(cfg):
    c = cfg
    sl = alibi_slopes_np(c.NH).astype(np.float64)
    gk = np.arange(c.S, dtype=np.float64)
    out = []
    for core in range(c.NCORES):
        j = core % c.GPC
        rows = np.zeros((3 * c.HPC, c.S), dtype=ml_dtypes.bfloat16)
        for i in range(c.HPC):
            v = -sl[j * c.HPC + i] * gk
            hi = v.astype(ml_dtypes.bfloat16)
            r1 = v - hi.astype(np.float64)
            lo = r1.astype(ml_dtypes.bfloat16)
            r2 = r1 - lo.astype(np.float64)
            lo2 = r2.astype(ml_dtypes.bfloat16)
            rows[3 * i + 0] = hi
            rows[3 * i + 1] = lo
            rows[3 * i + 2] = lo2
        out.append(rows)
    return out


def core_alcol(cfg):
    c = cfg
    sl = alibi_slopes_np(c.NH).astype(np.float64)
    nkb = c.S // 128
    out = []
    for core in range(c.NCORES):
        j = core % c.GPC
        a_ = np.zeros((128, c.HPC * nkb), dtype=np.float32)
        p = np.arange(128, dtype=np.float64)
        for i in range(c.HPC):
            for t in range(nkb):
                a_[:, i * nkb + t] = sl[j * c.HPC + i] * (128 * t + p)
        out.append(a_)
    return out


def core_gidx(cfg):
    c = cfg
    out = []
    for core in range(c.NCORES):
        j = core % c.GPC
        g_ = np.zeros((128, 6 * c.GPC * c.HPC), dtype=np.int32)
        for i in range(c.HPC):
            for part in range(3):
                for g2 in range(2 * c.GPC):
                    r = g2 if g2 < c.GPC else 2 * c.GPC - 1 - g2
                    hf = 0 if g2 < c.GPC else 1
                    kcol = (i * 3 + part) * 2 * c.GPC + g2
                    base = ((hf * 3 + part) * c.GPC * c.D + r * c.D
                            + (j * c.HPC + i) * c.HD)
                    g_[:, kcol] = base + np.arange(128)
        out.append(g_)
    return out


def core_aidx(cfg):
    c = cfg
    out = []
    for core in range(c.NCORES):
        j = core % c.GPC
        cbase = [j * c.CH, (2 * c.GPC - 1 - j) * c.CH]
        a_ = np.zeros((128, c.NTT * 2 * c.DC), dtype=np.int32)
        for tt in range(c.NTT):
            lo_l, hi_l = tt * c.TT, (tt + 1) * c.TT
            for h in (0, 1):
                aa = max(lo_l, h * c.CH)
                bb = min(hi_l, (h + 1) * c.CH)
                if aa >= bb:
                    continue
                gcol = cbase[h] + (aa - h * c.CH)
                NSP = 4 if c.HPC >= 4 else (2 if c.HPC >= 2 else 1)
                HH = c.HPC // NSP
                for do in range(c.DC):
                    gh = do
                    r, i = gh // c.HPC, gh % c.HPC
                    hg, i2 = i // HH, i % HH
                    row0 = (hg * c.GPC * HH + r * HH + i2) * 128
                    col = (tt * 2 + h) * c.DC + do
                    a_[:, col] = (row0 + np.arange(128)) * c.S + gcol
        out.append(a_)
    return out


def make_in_maps(cfg, x_full, weights):
    shards = shard_x(cfg, x_full)
    gidxs = core_gidx(cfg)
    aidxs = core_aidx(cfg)
    sk3s = core_sk3(cfg)
    alcols = core_alcol(cfg)
    return [dict(weights, x=shards[i], gidx=gidxs[i],
                 aidx=aidxs[i], sk3=sk3s[i], alcol=alcols[i])
            for i in range(cfg.NCORES)]



def kernel(L0_attn_hidden_states, L0_attn_Wqkv, L0_attn_q_ln_w, L0_attn_q_ln_b,
           L0_attn_k_ln_w, L0_attn_k_ln_b, L0_attn_Wout, L0_ffn_W, L0_ffn_b,
           L0_ffn_A, L0_ffn_Bmat, L0_ffn_LiMEs):
    cfg = make_cfg()
    x = np.asarray(L0_attn_hidden_states, dtype=np.float32)
    weights = host_layouts(
        cfg, np.asarray(L0_attn_Wqkv, np.float32),
        np.asarray(L0_attn_Wout, np.float32),
        np.asarray(L0_ffn_W, np.float32),
        np.asarray(L0_ffn_A, np.float32),
        np.asarray(L0_ffn_Bmat, np.float32),
        np.asarray(L0_ffn_LiMEs, np.float32),
        np.asarray(L0_attn_q_ln_w, np.float32),
        np.asarray(L0_attn_q_ln_b, np.float32),
        np.asarray(L0_attn_k_ln_w, np.float32),
        np.asarray(L0_attn_k_ln_b, np.float32),
        np.asarray(L0_ffn_b, np.float32))
    in_maps = make_in_maps(cfg, x, weights)
    nc = build_program(cfg)
    res = run_bass_kernel_spmd(nc, in_maps, core_ids=list(range(cfg.NCORES)))
    outs = [np.asarray(res.results[i]["out"]) for i in range(cfg.NCORES)]
    return unshard_out(cfg, outs)
